# version 16
# baseline (speedup 1.0000x reference)
import numpy as np
from contextlib import ExitStack

import concourse.bass as bass
import concourse.mybir as mybir
from concourse.bass_utils import run_bass_kernel_spmd

F32 = mybir.dt.float32
BF16 = mybir.dt.bfloat16
AF = mybir.ActivationFunctionType
ALU = mybir.AluOpType

D = 1024
NKC = D // 128
NH = 16
HD = 64
DL = 256
NE = 8
FF = 4096
FCB = 1024
NFC = FF // FCB
NFT = FCB // 128
QB = 512
EXPB = 4

LN_EPS = 1e-5
GATE_EPS = 1e-6
DECAY_SCALE = -(0.01 * 0.01)
ATT_SCALE_LNBIAS = float(np.log(1.0 / (np.sqrt(np.float64(1024.0)) + 1e-8)))
ATT_SCALE = float(1.0 / (np.sqrt(np.float64(1024.0)) + 1e-8))


class Chan:

    def __init__(self, stack, nc, name):
        self.sem = stack.enter_context(nc.semaphore(name))
        self.n = 0

    def inc(self, inst, k=1):
        inst.then_inc(self.sem, k)
        self.n += k
        return self.n


def nsplit(total, maxn=512):
    return [(o, min(maxn, total - o)) for o in range(0, total, maxn)]


def dma_ser(eng, chan, out, in_):
    v = chan.inc(eng.dma_start(out, in_), 16)
    eng.wait_ge(chan.sem, v)
    return v


def build(S=4096, debug=False):
    NT = S // 128
    NQC = S // QB
    KT = NT
    NKG = KT // EXPB
    TH = S // 2
    NTTH = TH // 128
    TC = 1024 if TH % 1024 == 0 else TH
    NTC = TH // TC
    NTI = TC // 128

    nc = bass.Bass("TRN2", target_bir_lowering=False, debug=False)

    x_h = nc.dram_tensor("x", [S, D], F32, kind="ExternalInput").ap()
    wq_h = nc.dram_tensor("wq", [D, D], F32, kind="ExternalInput").ap()
    wk_h = nc.dram_tensor("wk", [D, D], F32, kind="ExternalInput").ap()
    wv_h = nc.dram_tensor("wv", [D, D], F32, kind="ExternalInput").ap()
    wo_h = nc.dram_tensor("wo", [D, D], F32, kind="ExternalInput").ap()
    wdn_h = nc.dram_tensor("wdn", [D, DL], F32, kind="ExternalInput").ap()
    wupk_h = nc.dram_tensor("wupk", [DL, D], F32, kind="ExternalInput").ap()
    wupv_h = nc.dram_tensor("wupv", [DL, D], F32, kind="ExternalInput").ap()
    gwt_h = nc.dram_tensor("gwt", [D, NE], F32, kind="ExternalInput").ap()
    gbias_h = nc.dram_tensor("gbias", [1, NE], F32, kind="ExternalInput").ap()
    wut_h = nc.dram_tensor("wut", [NE, D, FF], F32, kind="ExternalInput").ap()
    bu2_h = nc.dram_tensor("bu2", [NE, 128, FF // 128], F32, kind="ExternalInput").ap()
    wdt_h = nc.dram_tensor("wdt", [NE, FF, D], F32, kind="ExternalInput").ap()
    bd_h = nc.dram_tensor("bd", [NE, D], F32, kind="ExternalInput").ap()
    swut_h = nc.dram_tensor("swut", [D, FF], F32, kind="ExternalInput").ap()
    sbu2_h = nc.dram_tensor("sbu2", [128, FF // 128], F32, kind="ExternalInput").ap()
    swdt_h = nc.dram_tensor("swdt", [FF, D], F32, kind="ExternalInput").ap()
    sbd_h = nc.dram_tensor("sbd", [1, D], F32, kind="ExternalInput").ap()
    out_h = nc.dram_tensor("out", [S, D], F32, kind="ExternalOutput").ap()

    scratch_kind = "ExternalOutput" if debug else "Internal"
    xnt_sp = nc.dram_tensor("xnt_sp", [128, NKC * S], BF16, kind=scratch_kind).ap()
    pv_sp = nc.dram_tensor("pv_sp", [128, NKC * S], BF16, kind=scratch_kind).ap()
    rr_dram = nc.dram_tensor("rr_dram", [2, 512], F32, kind="Internal").ap()
    attn_sp = nc.dram_tensor("attn_sp", [S, D], F32, kind=scratch_kind).ap()
    if debug:
        dbg_lat = nc.dram_tensor("dbg_lat", [128, 2 * S], BF16, kind="ExternalOutput").ap()
        dbg_q = nc.dram_tensor("dbg_q", [128, NKC * S], BF16, kind="ExternalOutput").ap()
        dbg_gsig = nc.dram_tensor("dbg_gsig", [128, NE * NT], F32, kind="ExternalOutput").ap()
        dbg_w = nc.dram_tensor("dbg_w", [128, NE * NT], F32, kind="ExternalOutput").ap()

    st = ExitStack()
    with st:
        def sbuf(name, shape, dtype, **kw):
            return st.enter_context(nc.sbuf_tensor(name, shape, dtype, **kw))

        ident = sbuf("ident", [128, 128], BF16)
        onesf = sbuf("onesf", [128, 128], BF16)
        consts = sbuf("consts", [128, 6], F32)
        mrv = sbuf("mrv", [128, 8], F32)
        gsig = sbuf("gsig", [128, NE * NT], F32)
        wgt = sbuf("wgt", [128, NE * NT], F32)
        wupk_b = sbuf("wupk_b", [128, 2 * D], BF16)
        wupv_b = sbuf("wupv_b", [128, 2 * D], BF16)
        gwt_b = sbuf("gwt_b", [128, NKC * NE], BF16)
        gbias_b = sbuf("gbias_b", [1, NE], BF16)
        bd_aug = sbuf("bd_aug", [32, D], BF16)
        wt_sb = sbuf("wt_sb", [32, 128], BF16)
        sbd_st = sbuf("sbd_st", [1, D], F32)
        sbd_bf = sbuf("sbd_bf", [1, D], BF16)

        s_gw = Chan(st, nc, "s_gw")
        s_qx = Chan(st, nc, "s_qx")
        s_bc = Chan(st, nc, "s_bc")
        s_gsp = Chan(st, nc, "s_gsp")
        s_gpc = Chan(st, nc, "s_gpc")
        s_v0 = Chan(st, nc, "s_v0")
        s_t1 = Chan(st, nc, "s_t1")
        s_s1 = Chan(st, nc, "s_s1")
        s_v1 = Chan(st, nc, "s_v1")
        s_t1b = Chan(st, nc, "s_t1b")
        s_v1b = Chan(st, nc, "s_v1b")
        s_t1c = Chan(st, nc, "s_t1c")
        s_s1c = Chan(st, nc, "s_s1c")
        s_v2s = Chan(st, nc, "s_v2s")
        s_s2s = Chan(st, nc, "s_s2s")
        s_t2p = Chan(st, nc, "s_t2p")
        s_v2p = Chan(st, nc, "s_v2p")
        s_t2m = Chan(st, nc, "s_t2m")
        s_s2m = Chan(st, nc, "s_s2m")
        s_v2m = Chan(st, nc, "s_v2m")
        s_t2mB = Chan(st, nc, "s_t2mB")
        s_s2mB = Chan(st, nc, "s_s2mB")
        s_v2mB = Chan(st, nc, "s_v2mB")
        s_t3 = Chan(st, nc, "s_t3")
        s_v3 = Chan(st, nc, "s_v3")
        s_t4w = Chan(st, nc, "s_t4w")
        s_v4w = Chan(st, nc, "s_v4w")
        s_t4 = Chan(st, nc, "s_t4")
        s_s4 = Chan(st, nc, "s_s4")
        s_v4 = Chan(st, nc, "s_v4")
        s_v5 = Chan(st, nc, "s_v5")

        dmarec = {}

        with nc.Block() as blk:
            @blk.gpsimd
            def _(gp):
                s_gpc.inc(gp.iota(ident[:, :], pattern=[[-1, 128]], base=0,
                                  channel_multiplier=1,
                                  allow_small_or_imprecise_dtypes=True))
                dmarec["gwt"] = dma_ser(gp, s_gw, gwt_b.ap().rearrange("p (kc e) -> p kc e", kc=NKC),
                    gwt_h.rearrange("(kc p) e -> p kc e", p=128))
                dmarec["gbias"] = dma_ser(gp, s_gw, gbias_b[:, :], gbias_h)
                dmarec["wupk"] = dma_ser(gp, s_gw, wupk_b.ap().rearrange("p (lc d) -> p lc d", lc=2),
                    wupk_h.rearrange("(lc p) d -> p lc d", p=128))
                dmarec["wupv"] = dma_ser(gp, s_gw, wupv_b.ap().rearrange("p (lc d) -> p lc d", lc=2),
                    wupv_h.rearrange("(lc p) d -> p lc d", p=128))
                dmarec["bd"] = dma_ser(gp, s_gw, bd_aug[0:8, :], bd_h)

            @blk.sync
            def _(sy):
                dmarec["sbd"] = dma_ser(sy, s_qx, sbd_st[:, :], sbd_h)
                sy.wait_ge(s_v0.sem, 5)
                dmarec["sbdrow"] = dma_ser(sy, s_qx, bd_aug[8:9, :], sbd_bf[:, :])

            @blk.vector
            def _(ve):
                ve.memset(consts[:, 0:1], LN_EPS)
                ve.memset(consts[:, 1:2], DECAY_SCALE)
                ve.memset(consts[:, 2:3], ATT_SCALE_LNBIAS)
                ve.memset(consts[:, 3:4], -1.0)
                ve.memset(consts[:, 4:5], -DECAY_SCALE)
                ve.memset(consts[:, 5:6], ATT_SCALE)
                ve.memset(onesf[:, :], 1.0)
                ve.memset(mrv[:, 2:8], -7.25e30)
                s_v0.inc(ve.memset(wt_sb[0:32, :], 1.0))
                ve.wait_ge(s_gpc.sem, 1)
                s_v0.inc(ve.tensor_single_scalar(ident[:, :], ident[:, :], 0.0,
                                                 op=ALU.is_equal))
                ve.wait_ge(s_qx.sem, dmarec["sbd"])
                s_v0.inc(ve.tensor_single_scalar(sbd_st[:, :], sbd_st[:, :], 1.0,
                                                 op=ALU.mult))
                ve.wait_ge(s_v0.sem, 3)
                s_v0.inc(ve.tensor_single_scalar(sbd_st[:, :], sbd_st[:, :], 0.1,
                                                 op=ALU.mult))
                ve.wait_ge(s_v0.sem, 4)
                s_v0.inc(ve.tensor_copy(sbd_bf[:, :], sbd_st[:, :]))

        with ExitStack() as stp1:
            latT = stp1.enter_context(nc.sbuf_tensor("latT", [128, 2 * S], BF16, side="left"))
            with ExitStack() as stp1q:
                xnT = stp1q.enter_context(nc.sbuf_tensor("xnT", [128, NKC * S], BF16, side="right"))
                wq_b = stp1q.enter_context(nc.sbuf_tensor("wq_b", [128, NKC * D], BF16, side="right"))
                with ExitStack() as stp1b:
                    wk_b = stp1b.enter_context(nc.sbuf_tensor("wk_b", [128, NKC * D], BF16, side="right"))
                    wv_b = stp1b.enter_context(nc.sbuf_tensor("wv_b", [128, NKC * D], BF16, side="right"))
                    wdn_b = stp1b.enter_context(nc.sbuf_tensor("wdn_b", [128, NKC * DL], BF16, side="right"))
                    kv_b = stp1b.enter_context(nc.sbuf_tensor("kv_b", [128, NKC * 512], BF16, side="right"))
                    with ExitStack() as stp1a:
                        sb1 = lambda n, sh, dt: stp1a.enter_context(nc.sbuf_tensor(n, sh, dt, side="right"))
                        xst = sb1("xst", [128, 2 * D], F32)
                        stats = sb1("stats", [128, 2 * 12], F32)
                        mv = sb1("mv", [128, 2 * 2], F32)
                        rstd = sb1("rstd", [128, 2], F32)
                        nmr = sb1("nmr", [128, 2], F32)
                        xnbf = sb1("xnbf", [128, 2 * D], BF16)
                        m8 = sb1("m8", [128, 8], F32)
                        killed = sb1("killed", [128, 8], F32)
                        gmask = sb1("gmask", [128, 8], F32)
                        gden = sb1("gden", [128, 1], F32)

                        VQ1, SQ1, TQ1 = 13, 11, 9
                        with ExitStack() as stps, nc.Block() as blk:
                            trp = [stps.enter_context(nc.psum_tensor(f"trp{i}", [128, 128], BF16)) for i in range(2)]
                            gps = [stps.enter_context(nc.psum_tensor(f"gps{i}", [128, NE], F32)) for i in range(2)]

                            @blk.gpsimd
                            def _(gp):
                                for nm, dst, src in (("wk", wk_b, wk_h), ("wv", wv_b, wv_h),
                                                     ("wq", wq_b, wq_h)):
                                    dmarec[nm] = dma_ser(gp, s_gw, dst.ap().rearrange("p (kc d) -> p kc d", kc=NKC),
                                        src.rearrange("(kc p) d -> p kc d", p=128))
                                dmarec["wdn"] = dma_ser(gp, s_gw, wdn_b.ap().rearrange("p (kc l) -> p kc l", kc=NKC),
                                    wdn_h.rearrange("(kc p) l -> p kc l", p=128))

                            @blk.sync
                            def _(sy):
                                for tt in range(NT):
                                    if tt >= 2:
                                        sy.wait_ge(s_v1.sem, VQ1 * (tt - 2) + 1)
                                        sy.wait_ge(s_s1.sem, SQ1 * (tt - 2) + 2)
                                    dmarec[("xt", tt)] = dma_ser(sy, s_qx, xst[:, (tt % 2) * D:(tt % 2 + 1) * D],
                                        x_h[tt * 128:(tt + 1) * 128, :])

                            @blk.tensor
                            def _(te):
                                te.wait_ge(s_v0.sem, 2)
                                te.wait_ge(s_gw.sem, dmarec["gbias"])
                                for tt in range(NT):
                                    te.wait_ge(s_s1.sem, SQ1 * tt + 2)
                                    for i in range(8):
                                        gi = 8 * tt + i
                                        if gi >= 2:
                                            pj, pi = divmod(gi - 2, 8)
                                            te.wait_ge(s_s1.sem, SQ1 * pj + 3 + pi)
                                        s_t1.inc(te.transpose(
                                            trp[gi % 2][:, :],
                                            xnbf[:, (tt % 2) * D + i * 128:(tt % 2) * D + (i + 1) * 128],
                                            ident[:, :]))
                                    te.wait_ge(s_s1.sem, SQ1 * tt + 10)
                                    if tt >= 2:
                                        te.wait_ge(s_s1.sem, SQ1 * (tt - 2) + 11)
                                    for kc in range(NKC):
                                        te.matmul(gps[tt % 2][:, :],
                                                  xnT[:, kc * S + tt * 128:kc * S + (tt + 1) * 128],
                                                  gwt_b[:, kc * NE:(kc + 1) * NE],
                                                  start=(kc == 0), stop=False)
                                    s_t1.inc(te.matmul(gps[tt % 2][:, :], onesf[0:1, 0:128],
                                                       gbias_b[0:1, :], start=False, stop=True))
                                assert s_t1.n == TQ1 * NT

                            @blk.scalar
                            def _(sc):
                                for tt in range(NT):
                                    sc.wait_ge(s_v1.sem, VQ1 * tt + 2)
                                    s_s1.inc(sc.activation(rstd[:, tt % 2:tt % 2 + 1],
                                                           mv[:, (tt % 2) * 2 + 1:(tt % 2) * 2 + 2],
                                                           AF.Sqrt, bias=consts[:, 0:1]))
                                    sc.wait_ge(s_v1.sem, VQ1 * tt + 5)
                                    sc.wait_ge(s_qx.sem, dmarec[("xt", tt)])
                                    if tt >= 2:
                                        sc.wait_ge(s_t1.sem, TQ1 * (tt - 2) + 8)
                                    s_s1.inc(sc.activation(
                                        xnbf[:, (tt % 2) * D:(tt % 2 + 1) * D],
                                        xst[:, (tt % 2) * D:(tt % 2 + 1) * D],
                                        AF.Identity,
                                        bias=nmr[:, tt % 2:tt % 2 + 1],
                                        scale=rstd[:, tt % 2:tt % 2 + 1]))
                                    for i in range(8):
                                        sc.wait_ge(s_t1.sem, TQ1 * tt + i + 1)
                                        s_s1.inc(sc.activation(
                                            xnT[:, i * S + tt * 128:i * S + (tt + 1) * 128],
                                            trp[(8 * tt + i) % 2][:, :], AF.Identity))
                                    sc.wait_ge(s_t1.sem, TQ1 * tt + 9)
                                    s_s1.inc(sc.activation(gsig[:, tt * NE:(tt + 1) * NE],
                                                           gps[tt % 2][:, :], AF.Sigmoid))
                                assert s_s1.n == SQ1 * NT

                            @blk.vector
                            def _(ve):
                                ve.wait_ge(s_v0.sem, 5)
                                for tt in range(NT):
                                    b = tt % 2
                                    ve.wait_ge(s_qx.sem, dmarec[("xt", tt)])
                                    if tt >= 2:
                                        ve.wait_ge(s_v1.sem, VQ1 * (tt - 2) + 2)
                                    ve.bn_stats(stats[:, b * 12:b * 12 + 6],
                                                xst[:, b * D:b * D + 512])
                                    s_v1.inc(ve.bn_stats(stats[:, b * 12 + 6:b * 12 + 12],
                                                         xst[:, b * D + 512:b * D + 1024]))
                                    ve.wait_ge(s_v1.sem, VQ1 * tt + 1)
                                    if tt >= 2:
                                        ve.wait_ge(s_s1.sem, SQ1 * (tt - 2) + 1)
                                    s_v1.inc(ve.bn_aggr(mv[:, b * 2:b * 2 + 2],
                                                        stats[:, b * 12:b * 12 + 12]))
                                    ve.wait_ge(s_s1.sem, SQ1 * tt + 1)
                                    s_v1.inc(ve.reciprocal(rstd[:, b:b + 1], rstd[:, b:b + 1]))
                                    ve.wait_ge(s_v1.sem, VQ1 * tt + 3)
                                    s_v1.inc(ve.tensor_mul(nmr[:, b:b + 1], mv[:, b * 2:b * 2 + 1],
                                                           rstd[:, b:b + 1]))
                                    ve.wait_ge(s_v1.sem, VQ1 * tt + 4)
                                    s_v1.inc(ve.tensor_single_scalar(nmr[:, b:b + 1], nmr[:, b:b + 1],
                                                                     -1.0, op=ALU.mult))
                                    gs = gsig[:, tt * NE:(tt + 1) * NE]
                                    ve.wait_ge(s_s1.sem, SQ1 * tt + 11)
                                    s_v1.inc(ve.max(m8[:, :], gs))
                                    ve.wait_ge(s_v1.sem, VQ1 * tt + 6)
                                    if tt >= 1:
                                        ve.wait_ge(s_v1.sem, VQ1 * (tt - 1) + 8)
                                    s_v1.inc(ve.tensor_copy(mrv[:, 0:2], m8[:, 0:2]))
                                    ve.wait_ge(s_v1.sem, VQ1 * tt + 7)
                                    s_v1.inc(ve.match_replace(killed[:, :], mrv[:, :], gs, -1e30))
                                    ve.wait_ge(s_v1.sem, VQ1 * tt + 8)
                                    s_v1.inc(ve.tensor_tensor(gmask[:, :], gs, killed[:, :],
                                                              op=ALU.is_gt))
                                    s_v1.inc(ve.tensor_add(gden[:, :], m8[:, 0:1], m8[:, 1:2]))
                                    ve.wait_ge(s_v1.sem, VQ1 * tt + 10)
                                    s_v1.inc(ve.tensor_single_scalar(gden[:, :], gden[:, :],
                                                                     GATE_EPS, op=ALU.add))
                                    ve.wait_ge(s_v1.sem, VQ1 * tt + 11)
                                    s_v1.inc(ve.reciprocal(gden[:, :], gden[:, :]))
                                    ve.wait_ge(s_v1.sem, VQ1 * tt + 12)
                                    s_v1.inc(ve.scalar_tensor_tensor(
                                        wgt[:, tt * NE:(tt + 1) * NE], gs, gden[:, :],
                                        gmask[:, :], op0=ALU.mult, op1=ALU.mult))
                                assert s_v1.n == VQ1 * NT

                    with ExitStack() as stps, nc.Block() as blk:
                        kvp = [stps.enter_context(nc.psum_tensor(f"kvp{i}", [128, 512], F32)) for i in range(2)]
                        latp = [stps.enter_context(nc.psum_tensor(f"latp{i}", [128, 512], F32)) for i in range(2)]

                        @blk.sync
                        def _(sy):
                            dmarec["xnt_spill"] = dma_ser(sy, s_qx, xnt_sp, xnT[:, :])

                        @blk.tensor
                        def _(te):
                            te.wait_ge(s_gw.sem, dmarec["wdn"])
                            te.wait_ge(s_s1.sem, SQ1 * NT)
                            for nn in range(NQC):
                                for mt in range(8):
                                    gi = 10 * nn + mt
                                    if gi >= 2:
                                        te.wait_ge(s_v1b.sem, gi - 1)
                                    first = True
                                    last = None
                                    for wsrc in (wk_b, wv_b):
                                        for kc in range(NKC):
                                            last = te.matmul(
                                                kvp[mt % 2][:, :],
                                                wsrc[:, kc * D + mt * 128:kc * D + (mt + 1) * 128],
                                                xnT[:, kc * S + nn * 512:kc * S + (nn + 1) * 512],
                                                start=first,
                                                stop=(wsrc is wv_b and kc == NKC - 1))
                                            first = False
                                    s_t1b.inc(last)
                                te.wait_ge(s_v1b.sem, 10 * nn + 8)
                                for lt in range(2):
                                    gi = 10 * nn + 8 + lt
                                    if gi >= 2:
                                        te.wait_ge(s_v1b.sem, gi - 1)
                                    last = None
                                    for kc in range(NKC):
                                        last = te.matmul(
                                            latp[lt][:, :],
                                            wdn_b[:, kc * DL + lt * 128:kc * DL + (lt + 1) * 128],
                                            kv_b[:, kc * 512:(kc + 1) * 512],
                                            start=(kc == 0), stop=(kc == NKC - 1))
                                    s_t1b.inc(last)
                            assert s_t1b.n == 10 * NQC

                        @blk.vector
                        def _(ve):
                            for nn in range(NQC):
                                for mt in range(8):
                                    ve.wait_ge(s_t1b.sem, 10 * nn + mt + 1)
                                    if mt == 0 and nn >= 1:
                                        ve.wait_ge(s_t1b.sem, 10 * (nn - 1) + 10)
                                    s_v1b.inc(ve.tensor_copy(kv_b[:, mt * 512:(mt + 1) * 512],
                                                             kvp[mt % 2][:, :]))
                                for lt in range(2):
                                    ve.wait_ge(s_t1b.sem, 10 * nn + 9 + lt)
                                    s_v1b.inc(ve.tensor_copy(
                                        latT[:, lt * S + nn * 512:lt * S + (nn + 1) * 512],
                                        latp[lt][:, :]))
                            assert s_v1b.n == 10 * NQC

                qT = stp1.enter_context(nc.sbuf_tensor("qT", [128, NKC * S], BF16, side="left"))
                with ExitStack() as stps, nc.Block() as blk:
                    qp = [stps.enter_context(nc.psum_tensor(f"qp{i}", [128, 512], F32)) for i in range(2)]

                    @blk.tensor
                    def _(te):
                        te.wait_ge(s_gw.sem, dmarec["wq"])
                        for mt in range(8):
                            for nn in range(NQC):
                                gi = mt * NQC + nn
                                if gi >= 2:
                                    te.wait_ge(s_s1c.sem, gi - 1)
                                last = None
                                for kc in range(NKC):
                                    last = te.matmul(
                                        qp[gi % 2][:, :],
                                        wq_b[:, kc * D + mt * 128:kc * D + (mt + 1) * 128],
                                        xnT[:, kc * S + nn * 512:kc * S + (nn + 1) * 512],
                                        start=(kc == 0), stop=(kc == NKC - 1))
                                s_t1c.inc(last)
                        assert s_t1c.n == 8 * NQC

                    @blk.scalar
                    def _(sc):
                        for mt in range(8):
                            for nn in range(NQC):
                                gi = mt * NQC + nn
                                sc.wait_ge(s_t1c.sem, gi + 1)
                                s_s1c.inc(sc.activation(
                                    qT[:, mt * S + nn * 512:mt * S + (nn + 1) * 512],
                                    qp[gi % 2][:, :], AF.Identity))
                        assert s_s1c.n == 8 * NQC

                    @blk.sync
                    def _(sy):
                        sy.wait_ge(s_qx.sem, dmarec["xnt_spill"])

                if debug:
                    with nc.Block() as blk:
                        @blk.sync
                        def _(sy):
                            dma_ser(sy, s_qx, dbg_lat, latT[:, :])
                            dma_ser(sy, s_qx, dbg_q, qT[:, :])
                            dma_ser(sy, s_qx, dbg_gsig, gsig[:, :])
                            dma_ser(sy, s_qx, dbg_w, wgt[:, :])
                            sy.wait_ge(s_qx.sem, s_qx.n)

            with ExitStack() as stp2:
                sb2 = lambda n, sh, dt: stp2.enter_context(nc.sbuf_tensor(n, sh, dt, side="right"))
                rowP = sb2("rowP", [128, S], BF16)
                rowM = sb2("rowM", [128, S], BF16)
                decayR = sb2("decayR", [128, 1024], BF16)
                s2s_base = s_s2s.n
                v2s_base = s_v2s.n
                with ExitStack() as stp2s:
                    sb2t = lambda n, sh, dt: stp2s.enter_context(nc.sbuf_tensor(n, sh, dt, side="right"))
                    dmraw = sb2t("dmraw", [128, 1024], F32)
                    dmrawB = sb2t("dmrawB", [128, 1024], F32)
                    rowraw = sb2t("rowraw", [128, S], F32)
                    with nc.Block() as blk:
                        @blk.gpsimd
                        def _(gp):
                            s_gpc.inc(gp.iota(dmraw[:, :], pattern=[[-1, 1024]], base=384,
                                              channel_multiplier=1,
                                              allow_small_or_imprecise_dtypes=True))
                            s_gpc.inc(gp.iota(rowraw[:, :], pattern=[[-1, S]], base=0,
                                              channel_multiplier=0,
                                              allow_small_or_imprecise_dtypes=True))

                        @blk.scalar
                        def _(sc):
                            sc.wait_ge(s_gpc.sem, s_gpc.n)
                            s_s2s.inc(sc.activation(dmrawB[:, :], dmraw[:, :], AF.Abs))
                            s_s2s.inc(sc.activation(rowP[:, :], rowraw[:, :], AF.Exp,
                                                    scale=consts[:, 1:2]))
                            s_s2s.inc(sc.activation(rowM[:, :], rowraw[:, :], AF.Exp,
                                                    scale=consts[:, 4:5]))
                            sc.wait_ge(s_v2s.sem, v2s_base + 1)
                            s_s2s.inc(sc.activation(decayR[:, :], dmrawB[:, :], AF.Exp,
                                                    scale=consts[:, 1:2]))

                        @blk.vector
                        def _(ve):
                            ve.wait_ge(s_gpc.sem, s_gpc.n)
                            ve.wait_ge(s_s2s.sem, s2s_base + 1)
                            s_v2s.inc(ve.tensor_add(dmrawB[:, :], dmrawB[:, :], dmraw[:, :]))

                krecTP = sb2("krecTP", [128, 2 * S], BF16)
                krecTM = sb2("krecTM", [128, 2 * S], BF16)
                vrec = sb2("vrec", [128, NT * 260], BF16)
                expT = sb2("expT", [128, 6 * 1024], BF16)
                qstg = [sb2(f"qstg{i}", [128, 1024], BF16) for i in range(2)]
                pvT = sb2("pvT", [128, 2 * S], BF16)
                pvst = [sb2(f"pvst{i}", [64, 512], BF16) for i in range(2)]
                bcsb = [sb2(f"bcsb{i}", [64, 512], F32) for i in range(2)]
                rr_sb = sb2("rr_sb", [128, 512], F32)

                NHK = KT // 2
                JPG = 4 * NQC
                NJOBS = 4 * JPG
                SQ2 = NHK + 2
                VB = 7 * JPG + 1
                trec = {}
                oddrec = {}
                ojn_s = [0]
                ojn_v = [0]

                def vseg(J):
                    gj, pj = divmod(J, JPG)
                    return gj * VB + 7 * pj

                def evac_done(J):
                    gj, pj = divmod(J, JPG)
                    if pj == JPG - 1:
                        return (gj + 1) * VB
                    return gj * VB + 7 * (pj + 2)

                LAG = min(4, NHK)
                LSL = min(4, NHK)
                SB = SQ2 * JPG + 2

                def exp_done(J, hk):
                    gj, pj = divmod(J, JPG)
                    return gj * SB + SQ2 * pj + hk + 1 + (2 if hk >= LSL else 0)

                def rr_done(J):
                    gj, pj = divmod(J, JPG)
                    if pj == JPG - 1:
                        return (gj + 1) * SB
                    return gj * SB + SQ2 * (pj + 1) + LSL + 2

                stps2 = stp2.enter_context(ExitStack())
                scp = [stps2.enter_context(nc.psum_tensor(f"scp{i}", [128, 1024], F32)) for i in range(3)]
                pvp = [stps2.enter_context(nc.psum_tensor(f"pvp{i}", [128, 512], F32)) for i in range(2)]

                for g in range(4):
                    with nc.Block() as blk:
                        krp = [scp[0][:, i * 512:(i + 1) * 512] for i in range(2)]
                        vrp = [scp[1][:, i * 512:i * 512 + 256] for i in range(2)]
                        T0p, V0p = s_t2p.n, s_v2p.n
                        NKR = 2 * NQC

                        @blk.tensor
                        def _(te):
                            te.wait_ge(s_gw.sem, dmarec["wupv"])
                            if g == 0:
                                te.wait_ge(s_v1b.sem, 10 * NQC)
                            else:
                                te.wait_ge(s_s2m.sem, SB * g)
                                te.wait_ge(s_v2m.sem, VB * g)
                            for mt in range(2):
                                for nn in range(NQC):
                                    gi = mt * NQC + nn
                                    if gi >= 2:
                                        te.wait_ge(s_v2p.sem, V0p + 1 + 2 * (gi - 1))
                                    last = None
                                    for lc in range(2):
                                        last = te.matmul(
                                            krp[gi % 2],
                                            wupk_b[:, lc * D + g * 256 + mt * 128:lc * D + g * 256 + (mt + 1) * 128],
                                            latT[:, lc * S + nn * 512:lc * S + (nn + 1) * 512],
                                            start=(lc == 0), stop=(lc == 1))
                                    s_t2p.inc(last)
                            for tt in range(NT):
                                if tt >= 2:
                                    te.wait_ge(s_v2p.sem, V0p + 1 + 2 * NKR + tt - 1)
                                last = None
                                for lc in range(2):
                                    last = te.matmul(
                                        vrp[tt % 2],
                                        latT[:, lc * S + tt * 128:lc * S + (tt + 1) * 128],
                                        wupv_b[:, lc * D + g * 256:lc * D + (g + 1) * 256],
                                        start=(lc == 0), stop=(lc == 1))
                                s_t2p.inc(last)
                            assert s_t2p.n == T0p + NKR + NT

                        @blk.vector
                        def _(ve):
                            if g == 0:
                                ve.wait_ge(s_s2s.sem, s2s_base + 3)
                            else:
                                ve.wait_ge(s_t2m.sem, trec[(JPG * g - 1, "pv", NHK - 1)])
                            s_v2p.inc(ve.memset(vrec[:, :], 1.0))
                            for mt in range(2):
                                for nn in range(NQC):
                                    gi = mt * NQC + nn
                                    ve.wait_ge(s_t2p.sem, T0p + gi + 1)
                                    s_v2p.inc(ve.tensor_mul(
                                        krecTP[:, mt * S + nn * 512:mt * S + (nn + 1) * 512],
                                        krp[gi % 2],
                                        rowP[:, nn * 512:(nn + 1) * 512]))
                                    s_v2p.inc(ve.tensor_mul(
                                        krecTM[:, mt * S + nn * 512:mt * S + (nn + 1) * 512],
                                        krp[gi % 2],
                                        rowM[:, nn * 512:(nn + 1) * 512]))
                            ve.wait_ge(s_v2p.sem, V0p + 1)
                            for tt in range(NT):
                                ve.wait_ge(s_t2p.sem, T0p + NKR + tt + 1)
                                last = None
                                for lam in range(4):
                                    last = ve.tensor_copy(
                                        vrec[:, tt * 260 + lam * 65:tt * 260 + lam * 65 + 64],
                                        vrp[tt % 2][:, lam * 64:(lam + 1) * 64])
                                s_v2p.inc(last)
                            assert s_v2p.n == V0p + 1 + 2 * NKR + NT

                    with nc.Block() as blk:
                        v2p_done = s_v2p.n
                        s1c_done = s_s1c.n

                        def jinfo(pj):
                            J = JPG * g + pj
                            lam, qc = divmod(pj, NQC)
                            return J, lam, qc

                        @blk.tensor
                        def _(te):
                            te.wait_ge(s_v2p.sem, v2p_done)
                            te.wait_ge(s_s1c.sem, s1c_done)

                            def emit_sc(pj, hk):
                                J, lam, qc = jinfo(pj)
                                pbase = 64 * (lam % 2)
                                ktile = lam // 2
                                lh = NHK * J + hk
                                u = lh % 3
                                for half in range(2):
                                    kt = 2 * hk + half
                                    if kt < 4 * qc + 4:
                                        ksrc, qcol = krecTP, 0
                                    else:
                                        ksrc, qcol = krecTM, 512
                                    trec[(J, "sc", kt)] = s_t2m.inc(te.matmul(
                                        scp[u][:, half * 512:(half + 1) * 512],
                                        ksrc[pbase:pbase + 64, ktile * S + kt * 128:ktile * S + (kt + 1) * 128],
                                        qstg[J % 2][pbase:pbase + 64, qcol:qcol + 512],
                                        start=True, stop=True))

                            def emit_pv(pvJ, hk):
                                pj2 = pvJ - JPG * g
                                lam, qc = divmod(pj2, NQC)
                                lh = NHK * pvJ + hk
                                u6 = lh % 6
                                c = None
                                for half in range(2):
                                    kt = 2 * hk + half
                                    mv = expT[:, u6 * 1024 + half * 512:u6 * 1024 + (half + 1) * 512]
                                    c = s_t2m.inc(te.matmul(
                                        pvp[pvJ % 2][0:65, :],
                                        vrec[:, kt * 260 + lam * 65:kt * 260 + lam * 65 + 65],
                                        mv,
                                        start=(kt == 0), stop=(kt == KT - 1)))
                                trec[(pvJ, "pv", hk)] = c

                            pending = []
                            for pj in range(JPG):
                                J, lam, qc = jinfo(pj)
                                for b in range(NHK // 2):
                                    hk0 = 2 * b
                                    pops = []
                                    pending.append((J, hk0))
                                    pending.append((J, hk0 + 1))
                                    while len(pending) > LAG:
                                        pops.append(pending.pop(0))
                                    scw = 0
                                    lh1 = NHK * J + hk0 + 1
                                    if lh1 >= 3:
                                        Jp, hkp = divmod(lh1 - 3, NHK)
                                        scw = exp_done(Jp, hkp)
                                    for (Jv, hkv) in pops:
                                        scw = max(scw, exp_done(Jv, hkv))
                                    if scw:
                                        te.wait_ge(s_s2m.sem, scw)
                                    if hk0 == 0:
                                        te.wait_ge(s_v2m.sem, vseg(J) + 2)
                                    for (Jv, hkv) in pops:
                                        if hkv == 0 and Jv >= 2:
                                            te.wait_ge(s_v2m.sem, evac_done(Jv - 2))
                                    emit_sc(pj, hk0)
                                    emit_sc(pj, hk0 + 1)
                                    for (Jv, hkv) in pops:
                                        emit_pv(Jv, hkv)
                            if g == 3 or True:
                                scw = 0
                                for (Jv, hkv) in pending:
                                    scw = max(scw, exp_done(Jv, hkv))
                                te.wait_ge(s_s2m.sem, scw)
                                for (Jv, hkv) in pending:
                                    if hkv == 0 and Jv >= 2:
                                        te.wait_ge(s_v2m.sem, evac_done(Jv - 2))
                                    emit_pv(Jv, hkv)
                                pending = []

                        @blk.sync
                        def _(sy):
                            for pj in range(JPG):
                                J, lam, qc = jinfo(pj)
                                sy.wait_ge(s_s2m.sem, rr_done(J))
                                dmarec[("rrst", J)] = dma_ser(sy, s_bc,
                                    rr_dram[J % 2:J % 2 + 1, :], rr_sb[64:65, :])
                                if J >= 2:
                                    sy.wait_ge(s_v2m.sem, evac_done(J - 2))
                                dmarec[("bc", J)] = dma_ser(sy, s_bc,
                                    bcsb[J % 2][0:64, :],
                                    bass.AP(rr_dram.tensor, (J % 2) * 512, [[0, 64], [1, 512]]))
                                if lam % 2 == 1:
                                    if g > 0 and pj == NQC:
                                        sy.wait_ge(s_gsp.sem, dmarec[("pvspill", g - 1)])
                                    sy.wait_ge(s_v2m.sem, evac_done(J))
                                    oj = ojn_s[0]
                                    oddrec[oj] = dma_ser(sy, s_qx,
                                        pvT[64:128, (lam // 2) * S + qc * 512:(lam // 2) * S + (qc + 1) * 512],
                                        pvst[oj % 2][0:64, :])
                                    ojn_s[0] = oj + 1

                        @blk.gpsimd
                        def _(gp):
                            gp.wait_ge(s_v2m.sem, (g + 1) * VB)
                            gp.wait_ge(s_qx.sem, s_qx.n)
                            dmarec[("pvspill", g)] = dma_ser(gp, s_gsp,
                                pv_sp[:, (2 * g) * S:(2 * g + 2) * S], pvT[:, :])

                        @blk.scalar
                        def _(sc):
                            def emit_exp(pj, hk):
                                J, lam, qc = jinfo(pj)
                                lh = NHK * J + hk
                                u = lh % 3
                                u6 = lh % 6
                                if hk in (2 * qc, 2 * qc + 1):
                                    b = hk - 2 * qc
                                    sc.wait_ge(s_v2m.sem, vseg(J) + 4 + 2 * b)
                                    s_s2m.inc(sc.activation(expT[:, u6 * 1024:(u6 + 1) * 1024],
                                                            expT[:, u6 * 1024:(u6 + 1) * 1024],
                                                            AF.Exp, scale=consts[:, 5:6]))
                                else:
                                    sc.wait_ge(s_t2m.sem, trec[(J, "sc", 2 * hk + 1)])
                                    if lh >= 6:
                                        Jp, hkp = divmod(lh - 6, NHK)
                                        sc.wait_ge(s_t2m.sem, trec[(Jp, "pv", hkp)])
                                    s_s2m.inc(sc.activation(expT[:, u6 * 1024:(u6 + 1) * 1024],
                                                            scp[u][:, :], AF.Exp,
                                                            scale=consts[:, 5:6]))

                            def emit_lnrr(Jm):
                                sc.wait_ge(s_t2m.sem, trec[(Jm, "pv", NHK - 1)])
                                sc.wait_ge(s_s2m.sem, s_s2m.n)
                                if Jm >= 1:
                                    sc.wait_ge(s_bc.sem, dmarec[("rrst", Jm - 1)])
                                s_s2m.inc(sc.activation(rr_sb[64:65, :], pvp[Jm % 2][64:65, :],
                                                        AF.Ln))
                                sc.wait_ge(s_s2m.sem, s_s2m.n)
                                s_s2m.inc(sc.activation(rr_sb[64:65, :], rr_sb[64:65, :],
                                                        AF.Exp, scale=consts[64:65, 3:4]))

                            for pj in range(JPG):
                                J, lam, qc = jinfo(pj)
                                for hk in range(min(LSL, NHK)):
                                    emit_exp(pj, hk)
                                if pj >= 1:
                                    emit_lnrr(J - 1)
                                else:
                                    s_s2m.inc(sc.activation(rr_sb[0:1, 1:2], consts[0:1, 0:1],
                                                            AF.Identity))
                                    s_s2m.inc(sc.activation(rr_sb[0:1, 2:3], consts[0:1, 0:1],
                                                            AF.Identity))
                                for hk in range(min(LSL, NHK), NHK):
                                    emit_exp(pj, hk)
                            emit_lnrr(JPG * (g + 1) - 1)
                            assert s_s2m.n == SB * (g + 1)

                        @blk.vector
                        def _(ve):
                            if g == 0:
                                ve.wait_ge(s_s2s.sem, s2s_base + 4)

                            def emit_evac(Jm):
                                lamm, qcm = divmod(Jm % JPG, NQC)
                                ve.wait_ge(s_t2m.sem, trec[(Jm, "pv", NHK - 1)])
                                ve.wait_ge(s_bc.sem, dmarec[("bc", Jm)])
                                if lamm % 2 == 1:
                                    oj = ojn_v[0]
                                    if oj >= 2:
                                        ve.wait_ge(s_qx.sem, oddrec[oj - 2])
                                    dst = pvst[oj % 2][0:64, :]
                                    ojn_v[0] = oj + 1
                                else:
                                    if Jm % JPG == 0 and Jm >= JPG:
                                        ve.wait_ge(s_gsp.sem, dmarec[("pvspill", Jm // JPG - 1)])
                                    dst = pvT[0:64, (lamm // 2) * S + qcm * 512:(lamm // 2) * S + (qcm + 1) * 512]
                                s_v2m.inc(ve.tensor_mul(dst, pvp[Jm % 2][0:64, :],
                                                        bcsb[Jm % 2][:, :]))

                            for pj in range(JPG):
                                J, lam, qc = jinfo(pj)
                                pbase = 64 * (lam % 2)
                                qtile = (g * 4 + lam) // 2
                                if J >= 2:
                                    ve.wait_ge(s_t2m.sem, trec[(J - 2, "sc", KT - 1)])
                                qsl = qT[pbase:pbase + 64, qtile * S + qc * 512:qtile * S + (qc + 1) * 512]
                                s_v2m.inc(ve.tensor_mul(qstg[J % 2][pbase:pbase + 64, 0:512],
                                                        qsl, rowM[pbase:pbase + 64, qc * 512:(qc + 1) * 512]))
                                s_v2m.inc(ve.tensor_mul(qstg[J % 2][pbase:pbase + 64, 512:1024],
                                                        qsl, rowP[pbase:pbase + 64, qc * 512:(qc + 1) * 512]))
                                for i0 in range(4):
                                    kt = 4 * qc + i0
                                    hk = 2 * qc + i0 // 2
                                    lh = NHK * J + hk
                                    u = lh % 3
                                    u6 = lh % 6
                                    ve.wait_ge(s_t2m.sem, trec[(J, "sc", kt)])
                                    if lh >= 6:
                                        Jp, hkp = divmod(lh - 6, NHK)
                                        ve.wait_ge(s_t2m.sem, trec[(Jp, "pv", hkp)])
                                    s_v2m.inc(ve.tensor_mul(
                                        expT[:, u6 * 1024 + (kt % 2) * 512:u6 * 1024 + (kt % 2) * 512 + 512],
                                        scp[u][:, (kt % 2) * 512:(kt % 2 + 1) * 512],
                                        decayR[:, 384 - 128 * i0:384 - 128 * i0 + 512]))
                                if pj >= 1:
                                    emit_evac(J - 1)
                                else:
                                    s_v2m.inc(ve.memset(rr_sb[0:1, 0:1], 0.0))
                            emit_evac(JPG * (g + 1) - 1)
                            assert s_v2m.n == VB * (g + 1)

        with ExitStack() as stp3:
            sb3 = lambda n, sh, dt: stp3.enter_context(nc.sbuf_tensor(n, sh, dt, side="right"))
            pvF = sb3("pvF", [128, NKC * S], BF16)
            wo_b = sb3("wo_b", [128, NKC * D], BF16)
            ast = sb3("ast", [128, 2 * D], F32)
            with ExitStack() as stps, nc.Block() as blk:
                wop = [stps.enter_context(nc.psum_tensor(f"wop{i}", [128, 512], F32)) for i in range(2)]

                @blk.gpsimd
                def _(gp):
                    dmarec["wo"] = dma_ser(gp, s_gw, wo_b.ap().rearrange("p (kc d) -> p kc d", kc=NKC),
                        wo_h.rearrange("(kc p) d -> p kc d", p=128))

                @blk.sync
                def _(sy):
                    sy.wait_ge(s_gsp.sem, dmarec[("pvspill", 3)])
                    dmarec["pvload"] = dma_ser(sy, s_qx, pvF[:, :], pv_sp)
                    for tt in range(NT):
                        sy.wait_ge(s_v3.sem, 2 * tt + 2)
                        dmarec[("attn", tt)] = dma_ser(sy, s_qx, attn_sp[tt * 128:(tt + 1) * 128, :],
                            ast[:, (tt % 2) * D:(tt % 2 + 1) * D])

                @blk.tensor
                def _(te):
                    te.wait_ge(s_gw.sem, dmarec["wo"])
                    te.wait_ge(s_qx.sem, dmarec["pvload"])
                    for tt in range(NT):
                        for nn in range(2):
                            gi = 2 * tt + nn
                            if gi >= 2:
                                te.wait_ge(s_v3.sem, gi - 1)
                            last = None
                            for kc in range(NKC):
                                last = te.matmul(
                                    wop[gi % 2][:, :],
                                    pvF[:, kc * S + tt * 128:kc * S + (tt + 1) * 128],
                                    wo_b[:, kc * D + nn * 512:kc * D + (nn + 1) * 512],
                                    start=(kc == 0), stop=(kc == NKC - 1))
                            s_t3.inc(last)
                    assert s_t3.n == 2 * NT

                @blk.vector
                def _(ve):
                    for tt in range(NT):
                        for nn in range(2):
                            gi = 2 * tt + nn
                            ve.wait_ge(s_t3.sem, gi + 1)
                            if tt >= 2 and nn == 0:
                                ve.wait_ge(s_qx.sem, dmarec[("attn", tt - 2)])
                            s_v3.inc(ve.tensor_copy(
                                ast[:, (tt % 2) * D + nn * 512:(tt % 2) * D + (nn + 1) * 512],
                                wop[gi % 2][:, :]))
                    assert s_v3.n == 2 * NT

        with ExitStack() as stp4:
            sb4 = lambda n, sh, dt: stp4.enter_context(nc.sbuf_tensor(n, sh, dt, side="right"))
            xnTh = sb4("xnTh", [128, NKC * TH], BF16)
            comb = sb4("comb", [128, NTTH * D], F32)
            wu_sb = sb4("wu_sb", [128, NKC * FCB], BF16)
            wd_sb = sb4("wd_sb", [128, NFT * D], BF16)
            a1s = sb4("a1s", [128, NFT * TC], BF16)
            bu_sb = sb4("bu_sb", [128, 2 * (FF // 128)], F32)
            wbf = sb4("wbf", [128, 8], BF16)
            xst5 = sb4("xst5", [128, 2 * D], F32)
            att5 = sb4("att5", [128, 2 * D], F32)

            TQ4 = NTC * (NFT + NTI)
            SQ4 = NTC * NFT
            VQ4 = NTTH
            H1LAST = (NTC - 1) * (NFT + NTI) + NFT

            def t4_h1_val(bl, tc, ft):
                return bl * TQ4 + tc * (NFT + NTI) + ft + 1

            def t4_y_val(bl, tc, tti):
                return bl * TQ4 + tc * (NFT + NTI) + NFT + tti + 1

            for half in range(2):
                with ExitStack() as stps, nc.Block() as blk:
                    trwp = stps.enter_context(nc.psum_tensor(f"trw{half}", [8, 128], BF16))
                    wbdp = [stps.enter_context(nc.psum_tensor(f"wbd{half}{i}", [128, D], F32)) for i in range(2)]
                    T0w, V0w = s_t4w.n, s_v4w.n

                    @blk.sync
                    def _(sy):
                        dmarec[("xnth", half)] = dma_ser(sy, s_qx, xnTh.ap().rearrange("p (kc t) -> p kc t", kc=NKC),
                            xnt_sp.rearrange("p (kc t) -> p kc t", kc=NKC)[:, :, half * TH:(half + 1) * TH])
                        for tt in range(NTTH):
                            r0 = (half * NTTH + tt) * 128
                            if tt >= 2:
                                sy.wait_ge(s_v4w.sem, V0w + 5 * (tt - 2) + 5)
                            dmarec[("x5", half, tt)] = dma_ser(sy, s_qx, xst5[:, (tt % 2) * D:(tt % 2 + 1) * D],
                                x_h[r0:r0 + 128, :])
                            dmarec[("a5", half, tt)] = dma_ser(sy, s_qx, att5[:, (tt % 2) * D:(tt % 2 + 1) * D],
                                attn_sp[r0:r0 + 128, :])

                    @blk.tensor
                    def _(te):
                        te.wait_ge(s_gw.sem, dmarec["bd"])
                        te.wait_ge(s_qx.sem, dmarec["sbdrow"])
                        for tt in range(NTTH):
                            te.wait_ge(s_v4w.sem, V0w + 5 * tt + 1)
                            if tt >= 1:
                                te.wait_ge(s_v4w.sem, V0w + 5 * (tt - 1) + 2)
                            s_t4w.inc(te.transpose(trwp[:, :], wbf[:, :], ident[:, :]))
                            te.wait_ge(s_v4w.sem, V0w + 5 * tt + 2)
                            if tt >= 2:
                                te.wait_ge(s_v4w.sem, V0w + 5 * (tt - 2) + 3)
                            last = None
                            for nn in range(2):
                                last = te.matmul(wbdp[tt % 2][:, nn * 512:(nn + 1) * 512],
                                                 wt_sb[0:9, :],
                                                 bd_aug[0:9, nn * 512:(nn + 1) * 512],
                                                 start=True, stop=True)
                            s_t4w.inc(last)
                        assert s_t4w.n == T0w + 2 * NTTH

                    @blk.vector
                    def _(ve):
                        if half == 1:
                            ve.wait_ge(s_qx.sem, dmarec[("out", 0, NTTH - 1)])
                        ve.wait_ge(s_v0.sem, 1)
                        for tt in range(NTTH):
                            gt = half * NTTH + tt
                            if tt >= 1:
                                ve.wait_ge(s_t4w.sem, T0w + 2 * (tt - 1) + 1)
                            s_v4w.inc(ve.tensor_copy(wbf[:, :],
                                                     wgt[:, gt * NE:(gt + 1) * NE]))
                            ve.wait_ge(s_t4w.sem, T0w + 2 * tt + 1)
                            s_v4w.inc(ve.tensor_copy(wt_sb[0:8, :], trwp[:, :]))
                            ve.wait_ge(s_t4w.sem, T0w + 2 * tt + 2)
                            s_v4w.inc(ve.tensor_copy(comb[:, tt * D:(tt + 1) * D],
                                                     wbdp[tt % 2][:, :]))
                            ve.wait_ge(s_qx.sem, dmarec[("a5", half, tt)])
                            ve.wait_ge(s_v4w.sem, V0w + 5 * tt + 3)
                            s_v4w.inc(ve.tensor_add(comb[:, tt * D:(tt + 1) * D],
                                                    comb[:, tt * D:(tt + 1) * D],
                                                    xst5[:, (tt % 2) * D:(tt % 2 + 1) * D]))
                            ve.wait_ge(s_v4w.sem, V0w + 5 * tt + 4)
                            s_v4w.inc(ve.tensor_add(comb[:, tt * D:(tt + 1) * D],
                                                    comb[:, tt * D:(tt + 1) * D],
                                                    att5[:, (tt % 2) * D:(tt % 2 + 1) * D]))
                        assert s_v4w.n == V0w + 5 * NTTH

                with ExitStack() as stps, nc.Block() as blk:
                    h1p = [stps.enter_context(nc.psum_tensor(f"h1p{half}{i}", [128, TC], F32)) for i in range(2)]
                    yp = [stps.enter_context(nc.psum_tensor(f"yp{half}{i}", [128, D], F32)) for i in range(2)]
                    T0, S0, V0 = s_t4.n, s_s4.n, s_v4.n
                    B0 = T0 // TQ4
                    v4w_done = s_v4w.n

                    @blk.gpsimd
                    def _(gp):
                        for e in range(9):
                            for fc in range(NFC):
                                b = B0 + e * NFC + fc
                                wsrc = wut_h[e, :, :] if e < NE else swut_h
                                dsrc = wdt_h[e, :, :] if e < NE else swdt_h
                                if b >= 1:
                                    gp.wait_ge(s_t4.sem, TQ4 * (b - 1) + H1LAST)
                                dmarec[("wu", b)] = dma_ser(gp, s_gw, wu_sb.ap().rearrange("p (kc f) -> p kc f", kc=NKC),
                                    wsrc[:, fc * FCB:(fc + 1) * FCB].rearrange(
                                        "(kc p) f -> p kc f", p=128))
                                if b >= 1:
                                    gp.wait_ge(s_t4.sem, TQ4 * b)
                                dmarec[("wd", b)] = dma_ser(gp, s_gw, wd_sb.ap().rearrange("p (fi d) -> p fi d", fi=NFT),
                                    dsrc[fc * FCB:(fc + 1) * FCB, :].rearrange(
                                        "(fi p) d -> p fi d", p=128))

                    @blk.sync
                    def _(sy):
                        for e in range(9):
                            b = B0 + e * NFC
                            if e >= 2:
                                sy.wait_ge(s_s4.sem, SQ4 * (b - NFC))
                            src = bu2_h[e, :, :] if e < NE else sbu2_h
                            dmarec[("bu", half, e)] = dma_ser(sy, s_qx, bu_sb[:, (e % 2) * 32:(e % 2) * 32 + 32], src)

                    @blk.tensor
                    def _(te):
                        te.wait_ge(s_qx.sem, dmarec[("xnth", half)])
                        for e in range(9):
                            for fc in range(NFC):
                                bl = e * NFC + fc
                                b = B0 + bl
                                te.wait_ge(s_gw.sem, dmarec[("wu", b)])
                                for tc in range(NTC):
                                    for ft in range(NFT):
                                        hh = SQ4 * b + tc * NFT + ft
                                        if hh >= 2:
                                            te.wait_ge(s_s4.sem, hh - 1)
                                        last = None
                                        for kc in range(NKC):
                                            for no, nw in nsplit(TC):
                                                last = te.matmul(
                                                    h1p[(tc * NFT + ft) % 2][:, no:no + nw],
                                                    wu_sb[:, kc * FCB + ft * 128:kc * FCB + (ft + 1) * 128],
                                                    xnTh[:, kc * TH + tc * TC + no:kc * TH + tc * TC + no + nw],
                                                    start=(kc == 0), stop=(kc == NKC - 1))
                                        s_t4.inc(last)
                                    te.wait_ge(s_gw.sem, dmarec[("wd", b)])
                                    for tti in range(NTI):
                                        tt = tc * NTI + tti
                                        yy = VQ4 * b + tt
                                        if yy >= 2:
                                            te.wait_ge(s_v4.sem, yy - 1)
                                        te.wait_ge(s_s4.sem, SQ4 * b + NFT * (tc + 1))
                                        last = None
                                        for kc in range(NFT):
                                            for no, nw in nsplit(D):
                                                last = te.matmul(
                                                    yp[tt % 2][:, no:no + nw],
                                                    a1s[:, kc * TC + tti * 128:kc * TC + (tti + 1) * 128],
                                                    wd_sb[:, kc * D + no:kc * D + no + nw],
                                                    start=(kc == 0), stop=(kc == NFT - 1))
                                        s_t4.inc(last)
                        assert s_t4.n == T0 + TQ4 * 9 * NFC

                    @blk.scalar
                    def _(sc):
                        for e in range(9):
                            for fc in range(NFC):
                                bl = e * NFC + fc
                                b = B0 + bl
                                if fc == 0:
                                    sc.wait_ge(s_qx.sem, dmarec[("bu", half, e)])
                                for tc in range(NTC):
                                    for ft in range(NFT):
                                        sc.wait_ge(s_t4.sem, T0 + t4_h1_val(bl, tc, ft))
                                        s_s4.inc(sc.activation(
                                            a1s[:, ft * TC:(ft + 1) * TC],
                                            h1p[(tc * NFT + ft) % 2][:, :], AF.Gelu,
                                            bias=bu_sb[:, (e % 2) * 32 + fc * NFT + ft:
                                                       (e % 2) * 32 + fc * NFT + ft + 1]))
                        assert s_s4.n == S0 + SQ4 * 9 * NFC

                    @blk.vector
                    def _(ve):
                        ve.wait_ge(s_v4w.sem, v4w_done)
                        for e in range(9):
                            for fc in range(NFC):
                                bl = e * NFC + fc
                                b = B0 + bl
                                for tc in range(NTC):
                                    for tti in range(NTI):
                                        tt = tc * NTI + tti
                                        ve.wait_ge(s_t4.sem, T0 + t4_y_val(bl, tc, tti))
                                        gt = half * NTTH + tt
                                        scalar = (wgt[:, gt * NE + e:gt * NE + e + 1]
                                                  if e < NE else 0.1)
                                        yy = VQ4 * bl + tt
                                        if yy >= NTTH:
                                            ve.wait_ge(s_v4.sem, V0 + yy - NTTH + 1)
                                        s_v4.inc(ve.scalar_tensor_tensor(
                                            comb[:, tt * D:(tt + 1) * D],
                                            yp[tt % 2][:, :], scalar,
                                            comb[:, tt * D:(tt + 1) * D],
                                            op0=ALU.mult, op1=ALU.add))
                        assert s_v4.n == V0 + VQ4 * 9 * NFC

                with nc.Block() as blk:
                    v4done = s_v4.n

                    @blk.sync
                    def _(sy):
                        for tt in range(NTTH):
                            r0 = (half * NTTH + tt) * 128
                            sy.wait_ge(s_v4.sem, v4done - VQ4 + tt + 1)
                            dmarec[("out", half, tt)] = dma_ser(sy, s_qx, out_h[r0:r0 + 128, :],
                                comb[:, tt * D:(tt + 1) * D])
                        sy.wait_ge(s_qx.sem, dmarec[("out", half, NTTH - 1)])

    return nc


def _prep_inputs(L0_attn_x, L0_attn_Wq, L0_attn_Wk, L0_attn_Wv, L0_attn_Wo,
                 L0_attn_W_down_kv, L0_attn_W_up_k, L0_attn_W_up_v,
                 L0_ffn_gate_w, L0_ffn_gate_bias, L0_ffn_Wu, L0_ffn_bu,
                 L0_ffn_Wd, L0_ffn_bd, L0_ffn_sWu, L0_ffn_sbu, L0_ffn_sWd,
                 L0_ffn_sbd, S):
    c = np.ascontiguousarray
    f = lambda a: np.asarray(a, dtype=np.float32)
    shared = {
        "wq": c(f(L0_attn_Wq)),
        "wk": c(f(L0_attn_Wk)),
        "wv": c(f(L0_attn_Wv)),
        "wo": c(f(L0_attn_Wo)),
        "wdn": c(f(L0_attn_W_down_kv)),
        "wupk": c(f(L0_attn_W_up_k)),
        "wupv": c(f(L0_attn_W_up_v)),
        "gwt": c(f(L0_ffn_gate_w).T),
        "gbias": c(f(L0_ffn_gate_bias).reshape(1, NE)),
        "wut": c(f(L0_ffn_Wu).transpose(0, 2, 1)),
        "bu2": c(f(L0_ffn_bu).reshape(NE, FF // 128, 128).transpose(0, 2, 1)),
        "wdt": c(f(L0_ffn_Wd).transpose(0, 2, 1)),
        "bd": c(f(L0_ffn_bd)),
        "swut": c(f(L0_ffn_sWu).T),
        "sbu2": c(f(L0_ffn_sbu).reshape(FF // 128, 128).T),
        "swdt": c(f(L0_ffn_sWd).T),
        "sbd": c(f(L0_ffn_sbd).reshape(1, D)),
    }
    x = f(L0_attn_x)
    in_maps = []
    for b in range(x.shape[0]):
        m = dict(shared)
        m["x"] = c(x[b, :S, :])
        in_maps.append(m)
    return in_maps


def kernel(**inputs):
    S = 4096
    in_maps = _prep_inputs(S=S, **inputs)
    nc = build(S=S)
    res = run_bass_kernel_spmd(nc, in_maps, core_ids=list(range(len(in_maps))))
    return np.stack([r["out"] for r in res.results], axis=0)


# version 18
# speedup vs baseline: 1.1189x; 1.1189x over previous
import numpy as np
from contextlib import ExitStack

import concourse.bass as bass
import concourse.mybir as mybir
from concourse.bass_utils import run_bass_kernel_spmd

F32 = mybir.dt.float32
BF16 = mybir.dt.bfloat16
AF = mybir.ActivationFunctionType
ALU = mybir.AluOpType

D = 1024
NKC = D // 128
NH = 16
HD = 64
DL = 256
NE = 8
FF = 4096
FCB = 1024
NFC = FF // FCB
NFT = FCB // 128
QB = 512
EXPB = 4

LN_EPS = 1e-5
GATE_EPS = 1e-6
DECAY_SCALE = -(0.01 * 0.01)
ATT_SCALE_LNBIAS = float(np.log(1.0 / (np.sqrt(np.float64(1024.0)) + 1e-8)))
ATT_SCALE = float(1.0 / (np.sqrt(np.float64(1024.0)) + 1e-8))


class Chan:

    def __init__(self, stack, nc, name):
        self.sem = stack.enter_context(nc.semaphore(name))
        self.n = 0

    def inc(self, inst, k=1):
        inst.then_inc(self.sem, k)
        self.n += k
        return self.n


def nsplit(total, maxn=512):
    return [(o, min(maxn, total - o)) for o in range(0, total, maxn)]


def dma_ser(eng, chan, out, in_):
    v = chan.inc(eng.dma_start(out, in_), 16)
    eng.wait_ge(chan.sem, v)
    return v


def build(S=4096, debug=False):
    NT = S // 128
    NQC = S // QB
    KT = NT
    NKG = KT // EXPB
    TH = S // 2
    NTTH = TH // 128
    TC = 1024 if TH % 1024 == 0 else TH
    NTC = TH // TC
    NTI = TC // 128

    nc = bass.Bass("TRN2", target_bir_lowering=False, debug=False)

    x_h = nc.dram_tensor("x", [S, D], F32, kind="ExternalInput").ap()
    wq_h = nc.dram_tensor("wq", [D, D], F32, kind="ExternalInput").ap()
    wk_h = nc.dram_tensor("wk", [D, D], F32, kind="ExternalInput").ap()
    wv_h = nc.dram_tensor("wv", [D, D], F32, kind="ExternalInput").ap()
    wo_h = nc.dram_tensor("wo", [D, D], F32, kind="ExternalInput").ap()
    wdn_h = nc.dram_tensor("wdn", [D, DL], F32, kind="ExternalInput").ap()
    wupk_h = nc.dram_tensor("wupk", [DL, D], F32, kind="ExternalInput").ap()
    wupv_h = nc.dram_tensor("wupv", [DL, D], F32, kind="ExternalInput").ap()
    gwt_h = nc.dram_tensor("gwt", [D, NE], F32, kind="ExternalInput").ap()
    gbias_h = nc.dram_tensor("gbias", [1, NE], F32, kind="ExternalInput").ap()
    wut_h = nc.dram_tensor("wut", [NE, D, FF], F32, kind="ExternalInput").ap()
    bu2_h = nc.dram_tensor("bu2", [NE, 128, FF // 128], F32, kind="ExternalInput").ap()
    wdt_h = nc.dram_tensor("wdt", [NE, FF, D], F32, kind="ExternalInput").ap()
    bd_h = nc.dram_tensor("bd", [NE, D], F32, kind="ExternalInput").ap()
    swut_h = nc.dram_tensor("swut", [D, FF], F32, kind="ExternalInput").ap()
    sbu2_h = nc.dram_tensor("sbu2", [128, FF // 128], F32, kind="ExternalInput").ap()
    swdt_h = nc.dram_tensor("swdt", [FF, D], F32, kind="ExternalInput").ap()
    sbd_h = nc.dram_tensor("sbd", [1, D], F32, kind="ExternalInput").ap()
    out_h = nc.dram_tensor("out", [S, D], F32, kind="ExternalOutput").ap()

    scratch_kind = "ExternalOutput" if debug else "Internal"
    xnt_sp = nc.dram_tensor("xnt_sp", [128, NKC * S], BF16, kind=scratch_kind).ap()
    pv_sp = nc.dram_tensor("pv_sp", [128, NKC * S], BF16, kind=scratch_kind).ap()
    rr_dram = nc.dram_tensor("rr_dram", [2, 512], F32, kind="Internal").ap()
    attn_sp = nc.dram_tensor("attn_sp", [S, D], F32, kind=scratch_kind).ap()
    if debug:
        dbg_lat = nc.dram_tensor("dbg_lat", [128, 2 * S], BF16, kind="ExternalOutput").ap()
        dbg_q = nc.dram_tensor("dbg_q", [128, NKC * S], BF16, kind="ExternalOutput").ap()
        dbg_gsig = nc.dram_tensor("dbg_gsig", [128, NE * NT], F32, kind="ExternalOutput").ap()
        dbg_w = nc.dram_tensor("dbg_w", [128, NE * NT], F32, kind="ExternalOutput").ap()

    st = ExitStack()
    with st:
        def sbuf(name, shape, dtype, **kw):
            return st.enter_context(nc.sbuf_tensor(name, shape, dtype, **kw))

        ident = sbuf("ident", [128, 128], BF16)
        onesf = sbuf("onesf", [128, 128], BF16)
        consts = sbuf("consts", [128, 6], F32)
        mrv = sbuf("mrv", [128, 8], F32)
        gsig = sbuf("gsig", [128, NE * NT], F32)
        wgt = sbuf("wgt", [128, NE * NT], F32)
        wupk_b = sbuf("wupk_b", [128, 2 * D], BF16)
        wupv_b = sbuf("wupv_b", [128, 2 * D], BF16)
        gwt_b = sbuf("gwt_b", [128, NKC * NE], BF16)
        gbias_b = sbuf("gbias_b", [1, NE], BF16)
        bd_aug = sbuf("bd_aug", [32, D], BF16)
        wt_sb = sbuf("wt_sb", [32, 128], BF16)
        sbd_st = sbuf("sbd_st", [1, D], F32)
        sbd_bf = sbuf("sbd_bf", [1, D], BF16)

        s_gw = Chan(st, nc, "s_gw")
        s_qx = Chan(st, nc, "s_qx")
        s_sx = Chan(st, nc, "s_sx")
        s_bc = Chan(st, nc, "s_bc")
        s_gsp = Chan(st, nc, "s_gsp")
        s_gpc = Chan(st, nc, "s_gpc")
        s_v0 = Chan(st, nc, "s_v0")
        s_t1 = Chan(st, nc, "s_t1")
        s_s1 = Chan(st, nc, "s_s1")
        s_v1 = Chan(st, nc, "s_v1")
        s_t1b = Chan(st, nc, "s_t1b")
        s_v1b = Chan(st, nc, "s_v1b")
        s_t1c = Chan(st, nc, "s_t1c")
        s_s1c = Chan(st, nc, "s_s1c")
        s_v2s = Chan(st, nc, "s_v2s")
        s_s2s = Chan(st, nc, "s_s2s")
        s_t2p = Chan(st, nc, "s_t2p")
        s_v2p = Chan(st, nc, "s_v2p")
        s_t2m = Chan(st, nc, "s_t2m")
        s_s2m = Chan(st, nc, "s_s2m")
        s_v2m = Chan(st, nc, "s_v2m")
        s_t2mB = Chan(st, nc, "s_t2mB")
        s_s2mB = Chan(st, nc, "s_s2mB")
        s_v2mB = Chan(st, nc, "s_v2mB")
        s_t3 = Chan(st, nc, "s_t3")
        s_v3 = Chan(st, nc, "s_v3")
        s_t4w = Chan(st, nc, "s_t4w")
        s_v4w = Chan(st, nc, "s_v4w")
        s_t4 = Chan(st, nc, "s_t4")
        s_s4 = Chan(st, nc, "s_s4")
        s_v4 = Chan(st, nc, "s_v4")
        s_v5 = Chan(st, nc, "s_v5")

        dmarec = {}

        with nc.Block() as blk:
            @blk.gpsimd
            def _(gp):
                s_gpc.inc(gp.iota(ident[:, :], pattern=[[-1, 128]], base=0,
                                  channel_multiplier=1,
                                  allow_small_or_imprecise_dtypes=True))
                dmarec["gwt"] = dma_ser(gp, s_gw, gwt_b.ap().rearrange("p (kc e) -> p kc e", kc=NKC),
                    gwt_h.rearrange("(kc p) e -> p kc e", p=128))
                dmarec["gbias"] = dma_ser(gp, s_gw, gbias_b[:, :], gbias_h)
                dmarec["wupk"] = dma_ser(gp, s_gw, wupk_b.ap().rearrange("p (lc d) -> p lc d", lc=2),
                    wupk_h.rearrange("(lc p) d -> p lc d", p=128))
                dmarec["wupv"] = dma_ser(gp, s_gw, wupv_b.ap().rearrange("p (lc d) -> p lc d", lc=2),
                    wupv_h.rearrange("(lc p) d -> p lc d", p=128))
                dmarec["bd"] = dma_ser(gp, s_gw, bd_aug[0:8, :], bd_h)

            @blk.sync
            def _(sy):
                dmarec["sbd"] = dma_ser(sy, s_qx, sbd_st[:, :], sbd_h)
                sy.wait_ge(s_v0.sem, 5)
                dmarec["sbdrow"] = dma_ser(sy, s_qx, bd_aug[8:9, :], sbd_bf[:, :])

            @blk.vector
            def _(ve):
                ve.memset(consts[:, 0:1], LN_EPS)
                ve.memset(consts[:, 1:2], DECAY_SCALE)
                ve.memset(consts[:, 2:3], ATT_SCALE_LNBIAS)
                ve.memset(consts[:, 3:4], -1.0)
                ve.memset(consts[:, 4:5], -DECAY_SCALE)
                ve.memset(consts[:, 5:6], ATT_SCALE)
                ve.memset(onesf[:, :], 1.0)
                ve.memset(mrv[:, 2:8], -7.25e30)
                s_v0.inc(ve.memset(wt_sb[0:32, :], 1.0))
                ve.wait_ge(s_gpc.sem, 1)
                s_v0.inc(ve.tensor_single_scalar(ident[:, :], ident[:, :], 0.0,
                                                 op=ALU.is_equal))
                ve.wait_ge(s_qx.sem, dmarec["sbd"])
                s_v0.inc(ve.tensor_single_scalar(sbd_st[:, :], sbd_st[:, :], 1.0,
                                                 op=ALU.mult))
                ve.wait_ge(s_v0.sem, 3)
                s_v0.inc(ve.tensor_single_scalar(sbd_st[:, :], sbd_st[:, :], 0.1,
                                                 op=ALU.mult))
                ve.wait_ge(s_v0.sem, 4)
                s_v0.inc(ve.tensor_copy(sbd_bf[:, :], sbd_st[:, :]))

        with ExitStack() as stp1:
            latT = stp1.enter_context(nc.sbuf_tensor("latT", [128, 2 * S], BF16, side="left"))
            with ExitStack() as stp1q:
                xnT = stp1q.enter_context(nc.sbuf_tensor("xnT", [128, NKC * S], BF16, side="right"))
                wq_b = stp1q.enter_context(nc.sbuf_tensor("wq_b", [128, NKC * D], BF16, side="right"))
                with ExitStack() as stp1b:
                    wk_b = stp1b.enter_context(nc.sbuf_tensor("wk_b", [128, NKC * D], BF16, side="right"))
                    wv_b = stp1b.enter_context(nc.sbuf_tensor("wv_b", [128, NKC * D], BF16, side="right"))
                    wdn_b = stp1b.enter_context(nc.sbuf_tensor("wdn_b", [128, NKC * DL], BF16, side="right"))
                    kv_b = stp1b.enter_context(nc.sbuf_tensor("kv_b", [128, NKC * 512], BF16, side="right"))
                    with ExitStack() as stp1a:
                        sb1 = lambda n, sh, dt: stp1a.enter_context(nc.sbuf_tensor(n, sh, dt, side="right"))
                        xst = sb1("xst", [128, 2 * D], F32)
                        stats = sb1("stats", [128, 2 * 12], F32)
                        mv = sb1("mv", [128, 2 * 2], F32)
                        rstd = sb1("rstd", [128, 2], F32)
                        nmr = sb1("nmr", [128, 2], F32)
                        xnbf = sb1("xnbf", [128, 2 * D], BF16)
                        m8 = sb1("m8", [128, 8], F32)
                        killed = sb1("killed", [128, 8], F32)
                        gmask = sb1("gmask", [128, 8], F32)
                        gden = sb1("gden", [128, 1], F32)

                        VQ1, SQ1, TQ1 = 13, 11, 9
                        with ExitStack() as stps, nc.Block() as blk:
                            trp = [stps.enter_context(nc.psum_tensor(f"trp{i}", [128, 128], BF16)) for i in range(2)]
                            gps = [stps.enter_context(nc.psum_tensor(f"gps{i}", [128, NE], F32)) for i in range(2)]

                            @blk.gpsimd
                            def _(gp):
                                for nm, dst, src in (("wk", wk_b, wk_h), ("wv", wv_b, wv_h),
                                                     ("wq", wq_b, wq_h)):
                                    dmarec[nm] = dma_ser(gp, s_gw, dst.ap().rearrange("p (kc d) -> p kc d", kc=NKC),
                                        src.rearrange("(kc p) d -> p kc d", p=128))
                                dmarec["wdn"] = dma_ser(gp, s_gw, wdn_b.ap().rearrange("p (kc l) -> p kc l", kc=NKC),
                                    wdn_h.rearrange("(kc p) l -> p kc l", p=128))

                            @blk.sync
                            def _(sy):
                                for tt in range(0, NT, 2):
                                    if tt >= 2:
                                        sy.wait_ge(s_v1.sem, VQ1 * (tt - 2) + 1)
                                        sy.wait_ge(s_s1.sem, SQ1 * (tt - 2) + 2)
                                    dmarec[("xt", tt)] = dma_ser(sy, s_qx, xst[:, 0:D],
                                        x_h[tt * 128:(tt + 1) * 128, :])

                            @blk.tensor
                            def _(te):
                                te.wait_ge(s_v0.sem, 2)
                                te.wait_ge(s_gw.sem, dmarec["gbias"])
                                for tt in range(NT):
                                    te.wait_ge(s_s1.sem, SQ1 * tt + 2)
                                    for i in range(8):
                                        gi = 8 * tt + i
                                        if gi >= 2:
                                            pj, pi = divmod(gi - 2, 8)
                                            te.wait_ge(s_s1.sem, SQ1 * pj + 3 + pi)
                                        s_t1.inc(te.transpose(
                                            trp[gi % 2][:, :],
                                            xnbf[:, (tt % 2) * D + i * 128:(tt % 2) * D + (i + 1) * 128],
                                            ident[:, :]))
                                    te.wait_ge(s_s1.sem, SQ1 * tt + 10)
                                    if tt >= 2:
                                        te.wait_ge(s_s1.sem, SQ1 * (tt - 2) + 11)
                                    for kc in range(NKC):
                                        te.matmul(gps[tt % 2][:, :],
                                                  xnT[:, kc * S + tt * 128:kc * S + (tt + 1) * 128],
                                                  gwt_b[:, kc * NE:(kc + 1) * NE],
                                                  start=(kc == 0), stop=False)
                                    s_t1.inc(te.matmul(gps[tt % 2][:, :], onesf[0:1, 0:128],
                                                       gbias_b[0:1, :], start=False, stop=True))
                                assert s_t1.n == TQ1 * NT

                            @blk.scalar
                            def _(sc):
                                dmarec[("xt", 1)] = s_sx.inc(sc.dma_start(
                                    xst[:, D:2 * D], x_h[128:256, :]), 16)
                                for tt in range(NT):
                                    sc.wait_ge(s_v1.sem, VQ1 * tt + 2)
                                    s_s1.inc(sc.activation(rstd[:, tt % 2:tt % 2 + 1],
                                                           mv[:, (tt % 2) * 2 + 1:(tt % 2) * 2 + 2],
                                                           AF.Sqrt, bias=consts[:, 0:1]))
                                    sc.wait_ge(s_v1.sem, VQ1 * tt + 5)
                                    sc.wait_ge((s_qx if tt % 2 == 0 else s_sx).sem,
                                               dmarec[("xt", tt)])
                                    if tt >= 2:
                                        sc.wait_ge(s_t1.sem, TQ1 * (tt - 2) + 8)
                                    s_s1.inc(sc.activation(
                                        xnbf[:, (tt % 2) * D:(tt % 2 + 1) * D],
                                        xst[:, (tt % 2) * D:(tt % 2 + 1) * D],
                                        AF.Identity,
                                        bias=nmr[:, tt % 2:tt % 2 + 1],
                                        scale=rstd[:, tt % 2:tt % 2 + 1]))
                                    if tt % 2 == 1 and tt + 2 < NT:
                                        sc.wait_ge(s_s1.sem, SQ1 * tt + 2)
                                        dmarec[("xt", tt + 2)] = s_sx.inc(sc.dma_start(
                                            xst[:, D:2 * D],
                                            x_h[(tt + 2) * 128:(tt + 3) * 128, :]), 16)
                                    for i in range(8):
                                        sc.wait_ge(s_t1.sem, TQ1 * tt + i + 1)
                                        s_s1.inc(sc.activation(
                                            xnT[:, i * S + tt * 128:i * S + (tt + 1) * 128],
                                            trp[(8 * tt + i) % 2][:, :], AF.Identity))
                                    sc.wait_ge(s_t1.sem, TQ1 * tt + 9)
                                    s_s1.inc(sc.activation(gsig[:, tt * NE:(tt + 1) * NE],
                                                           gps[tt % 2][:, :], AF.Sigmoid))
                                assert s_s1.n == SQ1 * NT

                            @blk.vector
                            def _(ve):
                                ve.wait_ge(s_v0.sem, 5)
                                for tt in range(NT):
                                    b = tt % 2
                                    ve.wait_ge((s_qx if tt % 2 == 0 else s_sx).sem,
                                               dmarec[("xt", tt)])
                                    if tt >= 2:
                                        ve.wait_ge(s_v1.sem, VQ1 * (tt - 2) + 2)
                                    ve.bn_stats(stats[:, b * 12:b * 12 + 6],
                                                xst[:, b * D:b * D + 512])
                                    s_v1.inc(ve.bn_stats(stats[:, b * 12 + 6:b * 12 + 12],
                                                         xst[:, b * D + 512:b * D + 1024]))
                                    ve.wait_ge(s_v1.sem, VQ1 * tt + 1)
                                    if tt >= 2:
                                        ve.wait_ge(s_s1.sem, SQ1 * (tt - 2) + 1)
                                    s_v1.inc(ve.bn_aggr(mv[:, b * 2:b * 2 + 2],
                                                        stats[:, b * 12:b * 12 + 12]))
                                    ve.wait_ge(s_s1.sem, SQ1 * tt + 1)
                                    s_v1.inc(ve.reciprocal(rstd[:, b:b + 1], rstd[:, b:b + 1]))
                                    ve.wait_ge(s_v1.sem, VQ1 * tt + 3)
                                    s_v1.inc(ve.tensor_mul(nmr[:, b:b + 1], mv[:, b * 2:b * 2 + 1],
                                                           rstd[:, b:b + 1]))
                                    ve.wait_ge(s_v1.sem, VQ1 * tt + 4)
                                    s_v1.inc(ve.tensor_single_scalar(nmr[:, b:b + 1], nmr[:, b:b + 1],
                                                                     -1.0, op=ALU.mult))
                                    gs = gsig[:, tt * NE:(tt + 1) * NE]
                                    ve.wait_ge(s_s1.sem, SQ1 * tt + 11)
                                    s_v1.inc(ve.max(m8[:, :], gs))
                                    ve.wait_ge(s_v1.sem, VQ1 * tt + 6)
                                    if tt >= 1:
                                        ve.wait_ge(s_v1.sem, VQ1 * (tt - 1) + 8)
                                    s_v1.inc(ve.tensor_copy(mrv[:, 0:2], m8[:, 0:2]))
                                    ve.wait_ge(s_v1.sem, VQ1 * tt + 7)
                                    s_v1.inc(ve.match_replace(killed[:, :], mrv[:, :], gs, -1e30))
                                    ve.wait_ge(s_v1.sem, VQ1 * tt + 8)
                                    s_v1.inc(ve.tensor_tensor(gmask[:, :], gs, killed[:, :],
                                                              op=ALU.is_gt))
                                    s_v1.inc(ve.tensor_add(gden[:, :], m8[:, 0:1], m8[:, 1:2]))
                                    ve.wait_ge(s_v1.sem, VQ1 * tt + 10)
                                    s_v1.inc(ve.tensor_single_scalar(gden[:, :], gden[:, :],
                                                                     GATE_EPS, op=ALU.add))
                                    ve.wait_ge(s_v1.sem, VQ1 * tt + 11)
                                    s_v1.inc(ve.reciprocal(gden[:, :], gden[:, :]))
                                    ve.wait_ge(s_v1.sem, VQ1 * tt + 12)
                                    s_v1.inc(ve.scalar_tensor_tensor(
                                        wgt[:, tt * NE:(tt + 1) * NE], gs, gden[:, :],
                                        gmask[:, :], op0=ALU.mult, op1=ALU.mult))
                                assert s_v1.n == VQ1 * NT

                    with ExitStack() as stps, nc.Block() as blk:
                        kvp = [stps.enter_context(nc.psum_tensor(f"kvp{i}", [128, 512], F32)) for i in range(2)]
                        latp = [stps.enter_context(nc.psum_tensor(f"latp{i}", [128, 512], F32)) for i in range(2)]

                        @blk.sync
                        def _(sy):
                            dmarec["xnt_spill"] = dma_ser(sy, s_qx, xnt_sp, xnT[:, :])

                        @blk.tensor
                        def _(te):
                            te.wait_ge(s_gw.sem, dmarec["wdn"])
                            te.wait_ge(s_s1.sem, SQ1 * NT)
                            for nn in range(NQC):
                                for mt in range(8):
                                    gi = 10 * nn + mt
                                    if gi >= 2:
                                        te.wait_ge(s_v1b.sem, gi - 1)
                                    first = True
                                    last = None
                                    for wsrc in (wk_b, wv_b):
                                        for kc in range(NKC):
                                            last = te.matmul(
                                                kvp[mt % 2][:, :],
                                                wsrc[:, kc * D + mt * 128:kc * D + (mt + 1) * 128],
                                                xnT[:, kc * S + nn * 512:kc * S + (nn + 1) * 512],
                                                start=first,
                                                stop=(wsrc is wv_b and kc == NKC - 1))
                                            first = False
                                    s_t1b.inc(last)
                                te.wait_ge(s_v1b.sem, 10 * nn + 8)
                                for lt in range(2):
                                    gi = 10 * nn + 8 + lt
                                    if gi >= 2:
                                        te.wait_ge(s_v1b.sem, gi - 1)
                                    last = None
                                    for kc in range(NKC):
                                        last = te.matmul(
                                            latp[lt][:, :],
                                            wdn_b[:, kc * DL + lt * 128:kc * DL + (lt + 1) * 128],
                                            kv_b[:, kc * 512:(kc + 1) * 512],
                                            start=(kc == 0), stop=(kc == NKC - 1))
                                    s_t1b.inc(last)
                            assert s_t1b.n == 10 * NQC

                        @blk.vector
                        def _(ve):
                            for nn in range(NQC):
                                for mt in range(8):
                                    ve.wait_ge(s_t1b.sem, 10 * nn + mt + 1)
                                    if mt == 0 and nn >= 1:
                                        ve.wait_ge(s_t1b.sem, 10 * (nn - 1) + 10)
                                    s_v1b.inc(ve.tensor_copy(kv_b[:, mt * 512:(mt + 1) * 512],
                                                             kvp[mt % 2][:, :]))
                                for lt in range(2):
                                    ve.wait_ge(s_t1b.sem, 10 * nn + 9 + lt)
                                    s_v1b.inc(ve.tensor_copy(
                                        latT[:, lt * S + nn * 512:lt * S + (nn + 1) * 512],
                                        latp[lt][:, :]))
                            assert s_v1b.n == 10 * NQC

                qT = stp1.enter_context(nc.sbuf_tensor("qT", [128, NKC * S], BF16, side="left"))
                with ExitStack() as stps, nc.Block() as blk:
                    qp = [stps.enter_context(nc.psum_tensor(f"qp{i}", [128, 512], F32)) for i in range(2)]

                    @blk.tensor
                    def _(te):
                        te.wait_ge(s_gw.sem, dmarec["wq"])
                        for mt in range(8):
                            for nn in range(NQC):
                                gi = mt * NQC + nn
                                if gi >= 2:
                                    te.wait_ge(s_s1c.sem, gi - 1)
                                last = None
                                for kc in range(NKC):
                                    last = te.matmul(
                                        qp[gi % 2][:, :],
                                        wq_b[:, kc * D + mt * 128:kc * D + (mt + 1) * 128],
                                        xnT[:, kc * S + nn * 512:kc * S + (nn + 1) * 512],
                                        start=(kc == 0), stop=(kc == NKC - 1))
                                s_t1c.inc(last)
                        assert s_t1c.n == 8 * NQC

                    @blk.scalar
                    def _(sc):
                        for mt in range(8):
                            for nn in range(NQC):
                                gi = mt * NQC + nn
                                sc.wait_ge(s_t1c.sem, gi + 1)
                                s_s1c.inc(sc.activation(
                                    qT[:, mt * S + nn * 512:mt * S + (nn + 1) * 512],
                                    qp[gi % 2][:, :], AF.Identity))
                        assert s_s1c.n == 8 * NQC

                    @blk.sync
                    def _(sy):
                        sy.wait_ge(s_qx.sem, dmarec["xnt_spill"])

                if debug:
                    with nc.Block() as blk:
                        @blk.sync
                        def _(sy):
                            dma_ser(sy, s_qx, dbg_lat, latT[:, :])
                            dma_ser(sy, s_qx, dbg_q, qT[:, :])
                            dma_ser(sy, s_qx, dbg_gsig, gsig[:, :])
                            dma_ser(sy, s_qx, dbg_w, wgt[:, :])
                            sy.wait_ge(s_qx.sem, s_qx.n)

            with ExitStack() as stp2:
                sb2 = lambda n, sh, dt: stp2.enter_context(nc.sbuf_tensor(n, sh, dt, side="right"))
                rowP = sb2("rowP", [128, S], BF16)
                rowM = sb2("rowM", [128, S], BF16)
                decayR = sb2("decayR", [128, 1024], BF16)
                s2s_base = s_s2s.n
                v2s_base = s_v2s.n
                with ExitStack() as stp2s:
                    sb2t = lambda n, sh, dt: stp2s.enter_context(nc.sbuf_tensor(n, sh, dt, side="right"))
                    dmraw = sb2t("dmraw", [128, 1024], F32)
                    dmrawB = sb2t("dmrawB", [128, 1024], F32)
                    rowraw = sb2t("rowraw", [128, S], F32)
                    with nc.Block() as blk:
                        @blk.gpsimd
                        def _(gp):
                            s_gpc.inc(gp.iota(dmraw[:, :], pattern=[[-1, 1024]], base=384,
                                              channel_multiplier=1,
                                              allow_small_or_imprecise_dtypes=True))
                            s_gpc.inc(gp.iota(rowraw[:, :], pattern=[[-1, S]], base=0,
                                              channel_multiplier=0,
                                              allow_small_or_imprecise_dtypes=True))

                        @blk.scalar
                        def _(sc):
                            sc.wait_ge(s_gpc.sem, s_gpc.n)
                            s_s2s.inc(sc.activation(dmrawB[:, :], dmraw[:, :], AF.Abs))
                            s_s2s.inc(sc.activation(rowP[:, :], rowraw[:, :], AF.Exp,
                                                    scale=consts[:, 1:2]))
                            s_s2s.inc(sc.activation(rowM[:, :], rowraw[:, :], AF.Exp,
                                                    scale=consts[:, 4:5]))
                            sc.wait_ge(s_v2s.sem, v2s_base + 1)
                            s_s2s.inc(sc.activation(decayR[:, :], dmrawB[:, :], AF.Exp,
                                                    scale=consts[:, 1:2]))

                        @blk.vector
                        def _(ve):
                            ve.wait_ge(s_gpc.sem, s_gpc.n)
                            ve.wait_ge(s_s2s.sem, s2s_base + 1)
                            s_v2s.inc(ve.tensor_add(dmrawB[:, :], dmrawB[:, :], dmraw[:, :]))

                krecTP = sb2("krecTP", [128, 2 * S], BF16)
                krecTM = sb2("krecTM", [128, 2 * S], BF16)
                vrec = sb2("vrec", [128, NT * 260], BF16)
                expT = sb2("expT", [128, 6 * 1024], BF16)
                qstg = [sb2(f"qstg{i}", [128, 1024], BF16) for i in range(2)]
                pvT = sb2("pvT", [128, 2 * S], BF16)
                pvst = [sb2(f"pvst{i}", [64, 512], BF16) for i in range(2)]
                bcsb = [sb2(f"bcsb{i}", [64, 512], F32) for i in range(2)]
                rr_sb = sb2("rr_sb", [128, 512], F32)

                NHK = KT // 2
                JPG = 4 * NQC
                NJOBS = 4 * JPG
                SQ2 = NHK + 2
                VB = 7 * JPG + 1
                trec = {}
                oddrec = {}
                ojn_s = [0]
                ojn_v = [0]

                def vseg(J):
                    gj, pj = divmod(J, JPG)
                    return gj * VB + 7 * pj

                def evac_done(J):
                    gj, pj = divmod(J, JPG)
                    if pj == JPG - 1:
                        return (gj + 1) * VB
                    return gj * VB + 7 * (pj + 2)

                LAG = min(4, NHK)
                LSL = min(4, NHK)
                SB = SQ2 * JPG + 2

                def exp_done(J, hk):
                    gj, pj = divmod(J, JPG)
                    return gj * SB + SQ2 * pj + hk + 1 + (2 if hk >= LSL else 0)

                def rr_done(J):
                    gj, pj = divmod(J, JPG)
                    if pj == JPG - 1:
                        return (gj + 1) * SB
                    return gj * SB + SQ2 * (pj + 1) + LSL + 2

                stps2 = stp2.enter_context(ExitStack())
                scp = [stps2.enter_context(nc.psum_tensor(f"scp{i}", [128, 1024], F32)) for i in range(3)]
                pvp = [stps2.enter_context(nc.psum_tensor(f"pvp{i}", [128, 512], F32)) for i in range(2)]

                for g in range(4):
                    with nc.Block() as blk:
                        krp = [scp[0][:, i * 512:(i + 1) * 512] for i in range(2)]
                        vrp = [scp[1][:, i * 512:i * 512 + 256] for i in range(2)]
                        T0p, V0p = s_t2p.n, s_v2p.n
                        NKR = 2 * NQC

                        @blk.tensor
                        def _(te):
                            te.wait_ge(s_gw.sem, dmarec["wupv"])
                            if g == 0:
                                te.wait_ge(s_v1b.sem, 10 * NQC)
                            else:
                                te.wait_ge(s_s2m.sem, SB * g)
                                te.wait_ge(s_v2m.sem, VB * g)
                            for mt in range(2):
                                for nn in range(NQC):
                                    gi = mt * NQC + nn
                                    if gi >= 2:
                                        te.wait_ge(s_v2p.sem, V0p + 1 + 2 * (gi - 1))
                                    last = None
                                    for lc in range(2):
                                        last = te.matmul(
                                            krp[gi % 2],
                                            wupk_b[:, lc * D + g * 256 + mt * 128:lc * D + g * 256 + (mt + 1) * 128],
                                            latT[:, lc * S + nn * 512:lc * S + (nn + 1) * 512],
                                            start=(lc == 0), stop=(lc == 1))
                                    s_t2p.inc(last)
                            for tt in range(NT):
                                if tt >= 2:
                                    te.wait_ge(s_v2p.sem, V0p + 1 + 2 * NKR + tt - 1)
                                last = None
                                for lc in range(2):
                                    last = te.matmul(
                                        vrp[tt % 2],
                                        latT[:, lc * S + tt * 128:lc * S + (tt + 1) * 128],
                                        wupv_b[:, lc * D + g * 256:lc * D + (g + 1) * 256],
                                        start=(lc == 0), stop=(lc == 1))
                                s_t2p.inc(last)
                            assert s_t2p.n == T0p + NKR + NT

                        @blk.vector
                        def _(ve):
                            if g == 0:
                                ve.wait_ge(s_s2s.sem, s2s_base + 3)
                            else:
                                ve.wait_ge(s_t2m.sem, trec[(JPG * g - 1, "pv", NHK - 1)])
                            s_v2p.inc(ve.memset(vrec[:, :], 1.0))
                            for mt in range(2):
                                for nn in range(NQC):
                                    gi = mt * NQC + nn
                                    ve.wait_ge(s_t2p.sem, T0p + gi + 1)
                                    s_v2p.inc(ve.tensor_mul(
                                        krecTP[:, mt * S + nn * 512:mt * S + (nn + 1) * 512],
                                        krp[gi % 2],
                                        rowP[:, nn * 512:(nn + 1) * 512]))
                                    s_v2p.inc(ve.tensor_mul(
                                        krecTM[:, mt * S + nn * 512:mt * S + (nn + 1) * 512],
                                        krp[gi % 2],
                                        rowM[:, nn * 512:(nn + 1) * 512]))
                            ve.wait_ge(s_v2p.sem, V0p + 1)
                            for tt in range(NT):
                                ve.wait_ge(s_t2p.sem, T0p + NKR + tt + 1)
                                last = None
                                for lam in range(4):
                                    last = ve.tensor_copy(
                                        vrec[:, tt * 260 + lam * 65:tt * 260 + lam * 65 + 64],
                                        vrp[tt % 2][:, lam * 64:(lam + 1) * 64])
                                s_v2p.inc(last)
                            assert s_v2p.n == V0p + 1 + 2 * NKR + NT

                    with nc.Block() as blk:
                        v2p_done = s_v2p.n
                        s1c_done = s_s1c.n

                        def jinfo(pj):
                            J = JPG * g + pj
                            lam, qc = divmod(pj, NQC)
                            return J, lam, qc

                        @blk.tensor
                        def _(te):
                            te.wait_ge(s_v2p.sem, v2p_done)
                            te.wait_ge(s_s1c.sem, s1c_done)

                            def emit_sc(pj, hk):
                                J, lam, qc = jinfo(pj)
                                pbase = 64 * (lam % 2)
                                ktile = lam // 2
                                lh = NHK * J + hk
                                u = lh % 3
                                for half in range(2):
                                    kt = 2 * hk + half
                                    if kt < 4 * qc + 4:
                                        ksrc, qcol = krecTP, 0
                                    else:
                                        ksrc, qcol = krecTM, 512
                                    trec[(J, "sc", kt)] = s_t2m.inc(te.matmul(
                                        scp[u][:, half * 512:(half + 1) * 512],
                                        ksrc[pbase:pbase + 64, ktile * S + kt * 128:ktile * S + (kt + 1) * 128],
                                        qstg[J % 2][pbase:pbase + 64, qcol:qcol + 512],
                                        start=True, stop=True))

                            def emit_pv(pvJ, hk):
                                pj2 = pvJ - JPG * g
                                lam, qc = divmod(pj2, NQC)
                                lh = NHK * pvJ + hk
                                u6 = lh % 6
                                c = None
                                for half in range(2):
                                    kt = 2 * hk + half
                                    mv = expT[:, u6 * 1024 + half * 512:u6 * 1024 + (half + 1) * 512]
                                    c = s_t2m.inc(te.matmul(
                                        pvp[pvJ % 2][0:65, :],
                                        vrec[:, kt * 260 + lam * 65:kt * 260 + lam * 65 + 65],
                                        mv,
                                        start=(kt == 0), stop=(kt == KT - 1)))
                                trec[(pvJ, "pv", hk)] = c

                            pending = []
                            for pj in range(JPG):
                                J, lam, qc = jinfo(pj)
                                for b in range(NHK // 2):
                                    hk0 = 2 * b
                                    pops = []
                                    pending.append((J, hk0))
                                    pending.append((J, hk0 + 1))
                                    while len(pending) > LAG:
                                        pops.append(pending.pop(0))
                                    scw = 0
                                    lh1 = NHK * J + hk0 + 1
                                    if lh1 >= 3:
                                        Jp, hkp = divmod(lh1 - 3, NHK)
                                        scw = exp_done(Jp, hkp)
                                    for (Jv, hkv) in pops:
                                        scw = max(scw, exp_done(Jv, hkv))
                                    if scw:
                                        te.wait_ge(s_s2m.sem, scw)
                                    if hk0 == 0:
                                        te.wait_ge(s_v2m.sem, vseg(J) + 2)
                                    for (Jv, hkv) in pops:
                                        if hkv == 0 and Jv >= 2:
                                            te.wait_ge(s_v2m.sem, evac_done(Jv - 2))
                                    emit_sc(pj, hk0)
                                    emit_sc(pj, hk0 + 1)
                                    for (Jv, hkv) in pops:
                                        emit_pv(Jv, hkv)
                            if g == 3 or True:
                                scw = 0
                                for (Jv, hkv) in pending:
                                    scw = max(scw, exp_done(Jv, hkv))
                                te.wait_ge(s_s2m.sem, scw)
                                for (Jv, hkv) in pending:
                                    if hkv == 0 and Jv >= 2:
                                        te.wait_ge(s_v2m.sem, evac_done(Jv - 2))
                                    emit_pv(Jv, hkv)
                                pending = []

                        @blk.sync
                        def _(sy):
                            for pj in range(JPG):
                                J, lam, qc = jinfo(pj)
                                sy.wait_ge(s_s2m.sem, rr_done(J))
                                dmarec[("rrst", J)] = dma_ser(sy, s_bc,
                                    rr_dram[J % 2:J % 2 + 1, :], rr_sb[64:65, :])
                                if J >= 2:
                                    sy.wait_ge(s_v2m.sem, evac_done(J - 2))
                                dmarec[("bc", J)] = dma_ser(sy, s_bc,
                                    bcsb[J % 2][0:64, :],
                                    bass.AP(rr_dram.tensor, (J % 2) * 512, [[0, 64], [1, 512]]))
                                if lam % 2 == 1:
                                    if g > 0 and pj == NQC:
                                        sy.wait_ge(s_gsp.sem, dmarec[("pvspill", g - 1)])
                                    sy.wait_ge(s_v2m.sem, evac_done(J))
                                    oj = ojn_s[0]
                                    oddrec[oj] = dma_ser(sy, s_qx,
                                        pvT[64:128, (lam // 2) * S + qc * 512:(lam // 2) * S + (qc + 1) * 512],
                                        pvst[oj % 2][0:64, :])
                                    ojn_s[0] = oj + 1

                        @blk.gpsimd
                        def _(gp):
                            gp.wait_ge(s_v2m.sem, (g + 1) * VB)
                            gp.wait_ge(s_qx.sem, s_qx.n)
                            dmarec[("pvspill", g)] = dma_ser(gp, s_gsp,
                                pv_sp[:, (2 * g) * S:(2 * g + 2) * S], pvT[:, :])

                        @blk.scalar
                        def _(sc):
                            def emit_exp(pj, hk):
                                J, lam, qc = jinfo(pj)
                                lh = NHK * J + hk
                                u = lh % 3
                                u6 = lh % 6
                                if hk in (2 * qc, 2 * qc + 1):
                                    b = hk - 2 * qc
                                    sc.wait_ge(s_v2m.sem, vseg(J) + 4 + 2 * b)
                                    s_s2m.inc(sc.activation(expT[:, u6 * 1024:(u6 + 1) * 1024],
                                                            expT[:, u6 * 1024:(u6 + 1) * 1024],
                                                            AF.Exp, scale=consts[:, 5:6]))
                                else:
                                    sc.wait_ge(s_t2m.sem, trec[(J, "sc", 2 * hk + 1)])
                                    if lh >= 6:
                                        Jp, hkp = divmod(lh - 6, NHK)
                                        sc.wait_ge(s_t2m.sem, trec[(Jp, "pv", hkp)])
                                    s_s2m.inc(sc.activation(expT[:, u6 * 1024:(u6 + 1) * 1024],
                                                            scp[u][:, :], AF.Exp,
                                                            scale=consts[:, 5:6]))

                            def emit_lnrr(Jm):
                                sc.wait_ge(s_t2m.sem, trec[(Jm, "pv", NHK - 1)])
                                sc.wait_ge(s_s2m.sem, s_s2m.n)
                                if Jm >= 1:
                                    sc.wait_ge(s_bc.sem, dmarec[("rrst", Jm - 1)])
                                s_s2m.inc(sc.activation(rr_sb[64:65, :], pvp[Jm % 2][64:65, :],
                                                        AF.Ln))
                                sc.wait_ge(s_s2m.sem, s_s2m.n)
                                s_s2m.inc(sc.activation(rr_sb[64:65, :], rr_sb[64:65, :],
                                                        AF.Exp, scale=consts[64:65, 3:4]))

                            for pj in range(JPG):
                                J, lam, qc = jinfo(pj)
                                for hk in range(min(LSL, NHK)):
                                    emit_exp(pj, hk)
                                if pj >= 1:
                                    emit_lnrr(J - 1)
                                else:
                                    s_s2m.inc(sc.activation(rr_sb[0:1, 1:2], consts[0:1, 0:1],
                                                            AF.Identity))
                                    s_s2m.inc(sc.activation(rr_sb[0:1, 2:3], consts[0:1, 0:1],
                                                            AF.Identity))
                                for hk in range(min(LSL, NHK), NHK):
                                    emit_exp(pj, hk)
                            emit_lnrr(JPG * (g + 1) - 1)
                            assert s_s2m.n == SB * (g + 1)

                        @blk.vector
                        def _(ve):
                            if g == 0:
                                ve.wait_ge(s_s2s.sem, s2s_base + 4)

                            def emit_evac(Jm):
                                lamm, qcm = divmod(Jm % JPG, NQC)
                                ve.wait_ge(s_t2m.sem, trec[(Jm, "pv", NHK - 1)])
                                ve.wait_ge(s_bc.sem, dmarec[("bc", Jm)])
                                if lamm % 2 == 1:
                                    oj = ojn_v[0]
                                    if oj >= 2:
                                        ve.wait_ge(s_qx.sem, oddrec[oj - 2])
                                    dst = pvst[oj % 2][0:64, :]
                                    ojn_v[0] = oj + 1
                                else:
                                    if Jm % JPG == 0 and Jm >= JPG:
                                        ve.wait_ge(s_gsp.sem, dmarec[("pvspill", Jm // JPG - 1)])
                                    dst = pvT[0:64, (lamm // 2) * S + qcm * 512:(lamm // 2) * S + (qcm + 1) * 512]
                                s_v2m.inc(ve.tensor_mul(dst, pvp[Jm % 2][0:64, :],
                                                        bcsb[Jm % 2][:, :]))

                            for pj in range(JPG):
                                J, lam, qc = jinfo(pj)
                                pbase = 64 * (lam % 2)
                                qtile = (g * 4 + lam) // 2
                                if J >= 2:
                                    ve.wait_ge(s_t2m.sem, trec[(J - 2, "sc", KT - 1)])
                                qsl = qT[pbase:pbase + 64, qtile * S + qc * 512:qtile * S + (qc + 1) * 512]
                                s_v2m.inc(ve.tensor_mul(qstg[J % 2][pbase:pbase + 64, 0:512],
                                                        qsl, rowM[pbase:pbase + 64, qc * 512:(qc + 1) * 512]))
                                s_v2m.inc(ve.tensor_mul(qstg[J % 2][pbase:pbase + 64, 512:1024],
                                                        qsl, rowP[pbase:pbase + 64, qc * 512:(qc + 1) * 512]))
                                for i0 in range(4):
                                    kt = 4 * qc + i0
                                    hk = 2 * qc + i0 // 2
                                    lh = NHK * J + hk
                                    u = lh % 3
                                    u6 = lh % 6
                                    ve.wait_ge(s_t2m.sem, trec[(J, "sc", kt)])
                                    if lh >= 6:
                                        Jp, hkp = divmod(lh - 6, NHK)
                                        ve.wait_ge(s_t2m.sem, trec[(Jp, "pv", hkp)])
                                    s_v2m.inc(ve.tensor_mul(
                                        expT[:, u6 * 1024 + (kt % 2) * 512:u6 * 1024 + (kt % 2) * 512 + 512],
                                        scp[u][:, (kt % 2) * 512:(kt % 2 + 1) * 512],
                                        decayR[:, 384 - 128 * i0:384 - 128 * i0 + 512]))
                                if pj >= 1:
                                    emit_evac(J - 1)
                                else:
                                    s_v2m.inc(ve.memset(rr_sb[0:1, 0:1], 0.0))
                            emit_evac(JPG * (g + 1) - 1)
                            assert s_v2m.n == VB * (g + 1)

        with ExitStack() as stp3:
            sb3 = lambda n, sh, dt: stp3.enter_context(nc.sbuf_tensor(n, sh, dt, side="right"))
            pvF = sb3("pvF", [128, NKC * S], BF16)
            wo_b = sb3("wo_b", [128, NKC * D], BF16)
            ast = sb3("ast", [128, 2 * D], F32)
            with ExitStack() as stps, nc.Block() as blk:
                wop = [stps.enter_context(nc.psum_tensor(f"wop{i}", [128, 512], F32)) for i in range(2)]

                @blk.gpsimd
                def _(gp):
                    dmarec["wo"] = dma_ser(gp, s_gw, wo_b.ap().rearrange("p (kc d) -> p kc d", kc=NKC),
                        wo_h.rearrange("(kc p) d -> p kc d", p=128))
                    for tt in range(1, NT, 2):
                        if tt == 1:
                            gp.wait_ge(s_gsp.sem, dmarec[("pvspill", 3)])
                        dmarec[("pvl", tt)] = dma_ser(gp, s_gsp,
                            pvF.ap().rearrange("p (kc t) -> p kc t", kc=NKC)[:, :, tt * 128:(tt + 1) * 128],
                            pv_sp.rearrange("p (kc t) -> p kc t", kc=NKC)[:, :, tt * 128:(tt + 1) * 128])

                @blk.sync
                def _(sy):
                    for tt in range(0, NT, 2):
                        if tt == 0:
                            sy.wait_ge(s_gsp.sem, dmarec[("pvspill", 3)])
                        dmarec[("pvl", tt)] = dma_ser(sy, s_qx,
                            pvF.ap().rearrange("p (kc t) -> p kc t", kc=NKC)[:, :, tt * 128:(tt + 1) * 128],
                            pv_sp.rearrange("p (kc t) -> p kc t", kc=NKC)[:, :, tt * 128:(tt + 1) * 128])
                    for tt in range(NT):
                        sy.wait_ge(s_v3.sem, 2 * tt + 2)
                        dmarec[("attn", tt)] = dma_ser(sy, s_qx, attn_sp[tt * 128:(tt + 1) * 128, :],
                            ast[:, (tt % 2) * D:(tt % 2 + 1) * D])

                @blk.tensor
                def _(te):
                    te.wait_ge(s_gw.sem, dmarec["wo"])
                    for tt in range(NT):
                        te.wait_ge(s_qx.sem if tt % 2 == 0 else s_gsp.sem, dmarec[("pvl", tt)])
                        for nn in range(2):
                            gi = 2 * tt + nn
                            if gi >= 2:
                                te.wait_ge(s_v3.sem, gi - 1)
                            last = None
                            for kc in range(NKC):
                                last = te.matmul(
                                    wop[gi % 2][:, :],
                                    pvF[:, kc * S + tt * 128:kc * S + (tt + 1) * 128],
                                    wo_b[:, kc * D + nn * 512:kc * D + (nn + 1) * 512],
                                    start=(kc == 0), stop=(kc == NKC - 1))
                            s_t3.inc(last)
                    assert s_t3.n == 2 * NT

                @blk.vector
                def _(ve):
                    for tt in range(NT):
                        for nn in range(2):
                            gi = 2 * tt + nn
                            ve.wait_ge(s_t3.sem, gi + 1)
                            if tt >= 2 and nn == 0:
                                ve.wait_ge(s_qx.sem, dmarec[("attn", tt - 2)])
                            s_v3.inc(ve.tensor_copy(
                                ast[:, (tt % 2) * D + nn * 512:(tt % 2) * D + (nn + 1) * 512],
                                wop[gi % 2][:, :]))
                    assert s_v3.n == 2 * NT

        with ExitStack() as stp4:
            sb4 = lambda n, sh, dt: stp4.enter_context(nc.sbuf_tensor(n, sh, dt, side="right"))
            xnTh = sb4("xnTh", [128, NKC * TH], BF16)
            comb = sb4("comb", [128, NTTH * D], F32)
            wu_sb = sb4("wu_sb", [128, NKC * FCB], BF16)
            wd_sb = sb4("wd_sb", [128, NFT * D], BF16)
            a1s = sb4("a1s", [128, NFT * TC], BF16)
            bu_sb = sb4("bu_sb", [128, 2 * (FF // 128)], F32)
            wbf = sb4("wbf", [128, 8], BF16)
            xst5 = sb4("xst5", [128, 2 * D], F32)
            att5 = sb4("att5", [128, 2 * D], F32)

            halfrec = {}
            TQ4 = NTC * (NFT + NTI)
            SQ4 = NTC * NFT
            VQ4 = NTTH
            H1LAST = (NTC - 1) * (NFT + NTI) + NFT

            def t4_h1_val(bl, tc, ft):
                return bl * TQ4 + tc * (NFT + NTI) + ft + 1

            def t4_y_val(bl, tc, tti):
                return bl * TQ4 + tc * (NFT + NTI) + NFT + tti + 1

            for half in range(2):
                with ExitStack() as stps, nc.Block() as blk:
                    trwp = stps.enter_context(nc.psum_tensor(f"trw{half}", [8, 128], BF16))
                    wbdp = [stps.enter_context(nc.psum_tensor(f"wbd{half}{i}", [128, D], F32)) for i in range(2)]
                    T0w, V0w = s_t4w.n, s_v4w.n

                    @blk.sync
                    def _(sy):
                        for tt in range(NTTH):
                            r0 = (half * NTTH + tt) * 128
                            if tt >= 2:
                                sy.wait_ge(s_v4w.sem, V0w + 5 * (tt - 2) + 5)
                            dmarec[("x5", half, tt)] = dma_ser(sy, s_qx, xst5[:, (tt % 2) * D:(tt % 2 + 1) * D],
                                x_h[r0:r0 + 128, :])

                    @blk.scalar
                    def _(sc):
                        for tt in range(NTTH):
                            r0 = (half * NTTH + tt) * 128
                            if tt >= 2:
                                sc.wait_ge(s_v4w.sem, V0w + 5 * (tt - 2) + 5)
                            dmarec[("a5", half, tt)] = dma_ser(sc, s_sx, att5[:, (tt % 2) * D:(tt % 2 + 1) * D],
                                attn_sp[r0:r0 + 128, :])

                    @blk.gpsimd
                    def _(gp):
                        if half == 1:
                            v4d = halfrec["v4done0"]
                            for tt in range(NTTH):
                                r0 = tt * 128
                                gp.wait_ge(s_v4.sem, v4d - VQ4 + tt + 1)
                                dmarec[("out", 0, tt)] = dma_ser(gp, s_gsp, out_h[r0:r0 + 128, :],
                                    comb[:, tt * D:(tt + 1) * D])
                        dmarec[("xnth", half)] = dma_ser(gp, s_gsp, xnTh.ap().rearrange("p (kc t) -> p kc t", kc=NKC),
                            xnt_sp.rearrange("p (kc t) -> p kc t", kc=NKC)[:, :, half * TH:(half + 1) * TH])

                    @blk.tensor
                    def _(te):
                        te.wait_ge(s_gw.sem, dmarec["bd"])
                        te.wait_ge(s_qx.sem, dmarec["sbdrow"])
                        for tt in range(NTTH):
                            te.wait_ge(s_v4w.sem, V0w + 5 * tt + 1)
                            if tt >= 1:
                                te.wait_ge(s_v4w.sem, V0w + 5 * (tt - 1) + 2)
                            s_t4w.inc(te.transpose(trwp[:, :], wbf[:, :], ident[:, :]))
                            te.wait_ge(s_v4w.sem, V0w + 5 * tt + 2)
                            if tt >= 2:
                                te.wait_ge(s_v4w.sem, V0w + 5 * (tt - 2) + 3)
                            last = None
                            for nn in range(2):
                                last = te.matmul(wbdp[tt % 2][:, nn * 512:(nn + 1) * 512],
                                                 wt_sb[0:9, :],
                                                 bd_aug[0:9, nn * 512:(nn + 1) * 512],
                                                 start=True, stop=True)
                            s_t4w.inc(last)
                        assert s_t4w.n == T0w + 2 * NTTH

                    @blk.vector
                    def _(ve):
                        if half == 1:
                            ve.wait_ge(s_qx.sem, dmarec[("out", 0, NTTH - 1)])
                        ve.wait_ge(s_v0.sem, 1)
                        for tt in range(NTTH):
                            gt = half * NTTH + tt
                            if tt >= 1:
                                ve.wait_ge(s_t4w.sem, T0w + 2 * (tt - 1) + 1)
                            s_v4w.inc(ve.tensor_copy(wbf[:, :],
                                                     wgt[:, gt * NE:(gt + 1) * NE]))
                            ve.wait_ge(s_t4w.sem, T0w + 2 * tt + 1)
                            s_v4w.inc(ve.tensor_copy(wt_sb[0:8, :], trwp[:, :]))
                            ve.wait_ge(s_t4w.sem, T0w + 2 * tt + 2)
                            if half == 1:
                                ve.wait_ge(s_gsp.sem, dmarec[("out", 0, tt)])
                            s_v4w.inc(ve.tensor_copy(comb[:, tt * D:(tt + 1) * D],
                                                     wbdp[tt % 2][:, :]))
                            ve.wait_ge(s_qx.sem, dmarec[("x5", half, tt)])
                            ve.wait_ge(s_sx.sem, dmarec[("a5", half, tt)])
                            ve.wait_ge(s_v4w.sem, V0w + 5 * tt + 3)
                            s_v4w.inc(ve.tensor_add(comb[:, tt * D:(tt + 1) * D],
                                                    comb[:, tt * D:(tt + 1) * D],
                                                    xst5[:, (tt % 2) * D:(tt % 2 + 1) * D]))
                            ve.wait_ge(s_v4w.sem, V0w + 5 * tt + 4)
                            s_v4w.inc(ve.tensor_add(comb[:, tt * D:(tt + 1) * D],
                                                    comb[:, tt * D:(tt + 1) * D],
                                                    att5[:, (tt % 2) * D:(tt % 2 + 1) * D]))
                        assert s_v4w.n == V0w + 5 * NTTH

                with ExitStack() as stps, nc.Block() as blk:
                    h1p = [stps.enter_context(nc.psum_tensor(f"h1p{half}{i}", [128, TC], F32)) for i in range(2)]
                    yp = [stps.enter_context(nc.psum_tensor(f"yp{half}{i}", [128, D], F32)) for i in range(2)]
                    T0, S0, V0 = s_t4.n, s_s4.n, s_v4.n
                    B0 = T0 // TQ4
                    v4w_done = s_v4w.n

                    @blk.gpsimd
                    def _(gp):
                        for e in range(9):
                            for fc in range(NFC):
                                b = B0 + e * NFC + fc
                                wsrc = wut_h[e, :, :] if e < NE else swut_h
                                dsrc = wdt_h[e, :, :] if e < NE else swdt_h
                                if b >= 1:
                                    gp.wait_ge(s_t4.sem, TQ4 * (b - 1) + H1LAST)
                                dmarec[("wu", b)] = dma_ser(gp, s_gw, wu_sb.ap().rearrange("p (kc f) -> p kc f", kc=NKC),
                                    wsrc[:, fc * FCB:(fc + 1) * FCB].rearrange(
                                        "(kc p) f -> p kc f", p=128))
                                if b >= 1:
                                    gp.wait_ge(s_t4.sem, TQ4 * b)
                                dmarec[("wd", b)] = dma_ser(gp, s_gw, wd_sb.ap().rearrange("p (fi d) -> p fi d", fi=NFT),
                                    dsrc[fc * FCB:(fc + 1) * FCB, :].rearrange(
                                        "(fi p) d -> p fi d", p=128))

                    @blk.sync
                    def _(sy):
                        for e in range(9):
                            b = B0 + e * NFC
                            if e >= 2:
                                sy.wait_ge(s_s4.sem, SQ4 * (b - NFC))
                            src = bu2_h[e, :, :] if e < NE else sbu2_h
                            dmarec[("bu", half, e)] = dma_ser(sy, s_qx, bu_sb[:, (e % 2) * 32:(e % 2) * 32 + 32], src)

                    @blk.tensor
                    def _(te):
                        te.wait_ge(s_gsp.sem, dmarec[("xnth", half)])
                        for e in range(9):
                            for fc in range(NFC):
                                bl = e * NFC + fc
                                b = B0 + bl
                                te.wait_ge(s_gw.sem, dmarec[("wu", b)])
                                for tc in range(NTC):
                                    for ft in range(NFT):
                                        hh = SQ4 * b + tc * NFT + ft
                                        if hh >= 2:
                                            te.wait_ge(s_s4.sem, hh - 1)
                                        last = None
                                        for kc in range(NKC):
                                            for no, nw in nsplit(TC):
                                                last = te.matmul(
                                                    h1p[(tc * NFT + ft) % 2][:, no:no + nw],
                                                    wu_sb[:, kc * FCB + ft * 128:kc * FCB + (ft + 1) * 128],
                                                    xnTh[:, kc * TH + tc * TC + no:kc * TH + tc * TC + no + nw],
                                                    start=(kc == 0), stop=(kc == NKC - 1))
                                        s_t4.inc(last)
                                    te.wait_ge(s_gw.sem, dmarec[("wd", b)])
                                    for tti in range(NTI):
                                        tt = tc * NTI + tti
                                        yy = VQ4 * b + tt
                                        if yy >= 2:
                                            te.wait_ge(s_v4.sem, yy - 1)
                                        te.wait_ge(s_s4.sem, SQ4 * b + NFT * (tc + 1))
                                        last = None
                                        for kc in range(NFT):
                                            for no, nw in nsplit(D):
                                                last = te.matmul(
                                                    yp[tt % 2][:, no:no + nw],
                                                    a1s[:, kc * TC + tti * 128:kc * TC + (tti + 1) * 128],
                                                    wd_sb[:, kc * D + no:kc * D + no + nw],
                                                    start=(kc == 0), stop=(kc == NFT - 1))
                                        s_t4.inc(last)
                        assert s_t4.n == T0 + TQ4 * 9 * NFC

                    @blk.scalar
                    def _(sc):
                        for e in range(9):
                            for fc in range(NFC):
                                bl = e * NFC + fc
                                b = B0 + bl
                                if fc == 0:
                                    sc.wait_ge(s_qx.sem, dmarec[("bu", half, e)])
                                for tc in range(NTC):
                                    for ft in range(NFT):
                                        sc.wait_ge(s_t4.sem, T0 + t4_h1_val(bl, tc, ft))
                                        s_s4.inc(sc.activation(
                                            a1s[:, ft * TC:(ft + 1) * TC],
                                            h1p[(tc * NFT + ft) % 2][:, :], AF.Gelu,
                                            bias=bu_sb[:, (e % 2) * 32 + fc * NFT + ft:
                                                       (e % 2) * 32 + fc * NFT + ft + 1]))
                        assert s_s4.n == S0 + SQ4 * 9 * NFC

                    @blk.vector
                    def _(ve):
                        ve.wait_ge(s_v4w.sem, v4w_done)
                        for e in range(9):
                            for fc in range(NFC):
                                bl = e * NFC + fc
                                b = B0 + bl
                                for tc in range(NTC):
                                    for tti in range(NTI):
                                        tt = tc * NTI + tti
                                        ve.wait_ge(s_t4.sem, T0 + t4_y_val(bl, tc, tti))
                                        gt = half * NTTH + tt
                                        scalar = (wgt[:, gt * NE + e:gt * NE + e + 1]
                                                  if e < NE else 0.1)
                                        yy = VQ4 * bl + tt
                                        if yy >= NTTH:
                                            ve.wait_ge(s_v4.sem, V0 + yy - NTTH + 1)
                                        s_v4.inc(ve.scalar_tensor_tensor(
                                            comb[:, tt * D:(tt + 1) * D],
                                            yp[tt % 2][:, :], scalar,
                                            comb[:, tt * D:(tt + 1) * D],
                                            op0=ALU.mult, op1=ALU.add))
                        assert s_v4.n == V0 + VQ4 * 9 * NFC

                if half == 0:
                    halfrec["v4done0"] = s_v4.n
                else:
                    with nc.Block() as blk:
                        v4done = s_v4.n

                        @blk.sync
                        def _(sy):
                            for tt in range(NTTH):
                                r0 = (half * NTTH + tt) * 128
                                sy.wait_ge(s_v4.sem, v4done - VQ4 + tt + 1)
                                dmarec[("out", half, tt)] = dma_ser(sy, s_qx, out_h[r0:r0 + 128, :],
                                    comb[:, tt * D:(tt + 1) * D])
                            sy.wait_ge(s_qx.sem, dmarec[("out", half, NTTH - 1)])
                            sy.wait_ge(s_gsp.sem, dmarec[("out", 0, NTTH - 1)])

    return nc


def _prep_inputs(L0_attn_x, L0_attn_Wq, L0_attn_Wk, L0_attn_Wv, L0_attn_Wo,
                 L0_attn_W_down_kv, L0_attn_W_up_k, L0_attn_W_up_v,
                 L0_ffn_gate_w, L0_ffn_gate_bias, L0_ffn_Wu, L0_ffn_bu,
                 L0_ffn_Wd, L0_ffn_bd, L0_ffn_sWu, L0_ffn_sbu, L0_ffn_sWd,
                 L0_ffn_sbd, S):
    c = np.ascontiguousarray
    f = lambda a: np.asarray(a, dtype=np.float32)
    shared = {
        "wq": c(f(L0_attn_Wq)),
        "wk": c(f(L0_attn_Wk)),
        "wv": c(f(L0_attn_Wv)),
        "wo": c(f(L0_attn_Wo)),
        "wdn": c(f(L0_attn_W_down_kv)),
        "wupk": c(f(L0_attn_W_up_k)),
        "wupv": c(f(L0_attn_W_up_v)),
        "gwt": c(f(L0_ffn_gate_w).T),
        "gbias": c(f(L0_ffn_gate_bias).reshape(1, NE)),
        "wut": c(f(L0_ffn_Wu).transpose(0, 2, 1)),
        "bu2": c(f(L0_ffn_bu).reshape(NE, FF // 128, 128).transpose(0, 2, 1)),
        "wdt": c(f(L0_ffn_Wd).transpose(0, 2, 1)),
        "bd": c(f(L0_ffn_bd)),
        "swut": c(f(L0_ffn_sWu).T),
        "sbu2": c(f(L0_ffn_sbu).reshape(FF // 128, 128).T),
        "swdt": c(f(L0_ffn_sWd).T),
        "sbd": c(f(L0_ffn_sbd).reshape(1, D)),
    }
    x = f(L0_attn_x)
    in_maps = []
    for b in range(x.shape[0]):
        m = dict(shared)
        m["x"] = c(x[b, :S, :])
        in_maps.append(m)
    return in_maps


def kernel(**inputs):
    S = 4096
    in_maps = _prep_inputs(S=S, **inputs)
    nc = build(S=S)
    res = run_bass_kernel_spmd(nc, in_maps, core_ids=list(range(len(in_maps))))
    return np.stack([r["out"] for r in res.results], axis=0)


# version 19
# speedup vs baseline: 1.1749x; 1.0501x over previous
import numpy as np
from contextlib import ExitStack

import concourse.bass as bass
import concourse.mybir as mybir
from concourse.bass_utils import run_bass_kernel_spmd

F32 = mybir.dt.float32
BF16 = mybir.dt.bfloat16
AF = mybir.ActivationFunctionType
ALU = mybir.AluOpType

D = 1024
NKC = D // 128
NH = 16
HD = 64
DL = 256
NE = 8
FF = 4096
FCB = 1024
NFC = FF // FCB
NFT = FCB // 128
QB = 512
EXPB = 4

LN_EPS = 1e-5
GATE_EPS = 1e-6
DECAY_SCALE = -(0.01 * 0.01)
ATT_SCALE_LNBIAS = float(np.log(1.0 / (np.sqrt(np.float64(1024.0)) + 1e-8)))
ATT_SCALE = float(1.0 / (np.sqrt(np.float64(1024.0)) + 1e-8))


class Chan:

    def __init__(self, stack, nc, name):
        self.sem = stack.enter_context(nc.semaphore(name))
        self.n = 0

    def inc(self, inst, k=1):
        inst.then_inc(self.sem, k)
        self.n += k
        return self.n


def nsplit(total, maxn=512):
    return [(o, min(maxn, total - o)) for o in range(0, total, maxn)]


def dma_ser(eng, chan, out, in_):
    v = chan.inc(eng.dma_start(out, in_), 16)
    eng.wait_ge(chan.sem, v)
    return v


def build(S=4096, debug=False):
    NT = S // 128
    NQC = S // QB
    KT = NT
    NKG = KT // EXPB
    TH = S // 2
    NTTH = TH // 128
    TC = 1024 if TH % 1024 == 0 else TH
    NTC = TH // TC
    NTI = TC // 128

    nc = bass.Bass("TRN2", target_bir_lowering=False, debug=False)

    x_h = nc.dram_tensor("x", [S, D], F32, kind="ExternalInput").ap()
    wq_h = nc.dram_tensor("wq", [D, D], F32, kind="ExternalInput").ap()
    wk_h = nc.dram_tensor("wk", [D, D], F32, kind="ExternalInput").ap()
    wv_h = nc.dram_tensor("wv", [D, D], F32, kind="ExternalInput").ap()
    wo_h = nc.dram_tensor("wo", [D, D], F32, kind="ExternalInput").ap()
    wdn_h = nc.dram_tensor("wdn", [D, DL], F32, kind="ExternalInput").ap()
    wupk_h = nc.dram_tensor("wupk", [DL, D], F32, kind="ExternalInput").ap()
    wupv_h = nc.dram_tensor("wupv", [DL, D], F32, kind="ExternalInput").ap()
    gwt_h = nc.dram_tensor("gwt", [D, NE], F32, kind="ExternalInput").ap()
    gbias_h = nc.dram_tensor("gbias", [1, NE], F32, kind="ExternalInput").ap()
    wut_h = nc.dram_tensor("wut", [NE, D, FF], F32, kind="ExternalInput").ap()
    bu2_h = nc.dram_tensor("bu2", [NE, 128, FF // 128], F32, kind="ExternalInput").ap()
    wdt_h = nc.dram_tensor("wdt", [NE, FF, D], F32, kind="ExternalInput").ap()
    bd_h = nc.dram_tensor("bd", [NE, D], F32, kind="ExternalInput").ap()
    swut_h = nc.dram_tensor("swut", [D, FF], F32, kind="ExternalInput").ap()
    sbu2_h = nc.dram_tensor("sbu2", [128, FF // 128], F32, kind="ExternalInput").ap()
    swdt_h = nc.dram_tensor("swdt", [FF, D], F32, kind="ExternalInput").ap()
    sbd_h = nc.dram_tensor("sbd", [1, D], F32, kind="ExternalInput").ap()
    out_h = nc.dram_tensor("out", [S, D], F32, kind="ExternalOutput").ap()

    scratch_kind = "ExternalOutput" if debug else "Internal"
    xnt_sp = nc.dram_tensor("xnt_sp", [128, NKC * S], BF16, kind=scratch_kind).ap()
    pv_sp = nc.dram_tensor("pv_sp", [128, NKC * S], BF16, kind=scratch_kind).ap()
    rr_dram = nc.dram_tensor("rr_dram", [2, 512], F32, kind="Internal").ap()
    attn_sp = nc.dram_tensor("attn_sp", [S, D], F32, kind=scratch_kind).ap()
    if debug:
        dbg_lat = nc.dram_tensor("dbg_lat", [128, 2 * S], BF16, kind="ExternalOutput").ap()
        dbg_q = nc.dram_tensor("dbg_q", [128, NKC * S], BF16, kind="ExternalOutput").ap()
        dbg_gsig = nc.dram_tensor("dbg_gsig", [128, NE * NT], F32, kind="ExternalOutput").ap()
        dbg_w = nc.dram_tensor("dbg_w", [128, NE * NT], F32, kind="ExternalOutput").ap()

    st = ExitStack()
    with st:
        def sbuf(name, shape, dtype, **kw):
            return st.enter_context(nc.sbuf_tensor(name, shape, dtype, **kw))

        ident = sbuf("ident", [128, 128], BF16)
        onesf = sbuf("onesf", [128, 128], BF16)
        consts = sbuf("consts", [128, 6], F32)
        mrv = sbuf("mrv", [128, 8], F32)
        gsig = sbuf("gsig", [128, NE * NT], F32)
        wgt = sbuf("wgt", [128, NE * NT], F32)
        wupk_b = sbuf("wupk_b", [128, 2 * D], BF16)
        wupv_b = sbuf("wupv_b", [128, 2 * D], BF16)
        gwt_b = sbuf("gwt_b", [128, NKC * NE], BF16)
        gbias_b = sbuf("gbias_b", [1, NE], BF16)
        bd_aug = sbuf("bd_aug", [32, D], BF16)
        wt_sb = sbuf("wt_sb", [32, 128], BF16)
        sbd_st = sbuf("sbd_st", [1, D], F32)
        sbd_bf = sbuf("sbd_bf", [1, D], BF16)

        s_gw = Chan(st, nc, "s_gw")
        s_qx = Chan(st, nc, "s_qx")
        s_sx = Chan(st, nc, "s_sx")
        s_bc = Chan(st, nc, "s_bc")
        s_gsp = Chan(st, nc, "s_gsp")
        s_gpc = Chan(st, nc, "s_gpc")
        s_v0 = Chan(st, nc, "s_v0")
        s_t1 = Chan(st, nc, "s_t1")
        s_s1 = Chan(st, nc, "s_s1")
        s_v1 = Chan(st, nc, "s_v1")
        s_t1b = Chan(st, nc, "s_t1b")
        s_v1b = Chan(st, nc, "s_v1b")
        s_t1c = Chan(st, nc, "s_t1c")
        s_s1c = Chan(st, nc, "s_s1c")
        s_v2s = Chan(st, nc, "s_v2s")
        s_s2s = Chan(st, nc, "s_s2s")
        s_t2p = Chan(st, nc, "s_t2p")
        s_v2p = Chan(st, nc, "s_v2p")
        s_t2m = Chan(st, nc, "s_t2m")
        s_s2m = Chan(st, nc, "s_s2m")
        s_v2m = Chan(st, nc, "s_v2m")
        s_t2mB = Chan(st, nc, "s_t2mB")
        s_s2mB = Chan(st, nc, "s_s2mB")
        s_v2mB = Chan(st, nc, "s_v2mB")
        s_t3 = Chan(st, nc, "s_t3")
        s_v3 = Chan(st, nc, "s_v3")
        s_t4w = Chan(st, nc, "s_t4w")
        s_v4w = Chan(st, nc, "s_v4w")
        s_t4 = Chan(st, nc, "s_t4")
        s_s4 = Chan(st, nc, "s_s4")
        s_v4 = Chan(st, nc, "s_v4")
        s_v5 = Chan(st, nc, "s_v5")

        dmarec = {}

        with nc.Block() as blk:
            @blk.gpsimd
            def _(gp):
                s_gpc.inc(gp.iota(ident[:, :], pattern=[[-1, 128]], base=0,
                                  channel_multiplier=1,
                                  allow_small_or_imprecise_dtypes=True))
                dmarec["gwt"] = dma_ser(gp, s_gw, gwt_b.ap().rearrange("p (kc e) -> p kc e", kc=NKC),
                    gwt_h.rearrange("(kc p) e -> p kc e", p=128))
                dmarec["gbias"] = dma_ser(gp, s_gw, gbias_b[:, :], gbias_h)
                dmarec["wupk"] = dma_ser(gp, s_gw, wupk_b.ap().rearrange("p (lc d) -> p lc d", lc=2),
                    wupk_h.rearrange("(lc p) d -> p lc d", p=128))
                dmarec["wupv"] = dma_ser(gp, s_gw, wupv_b.ap().rearrange("p (lc d) -> p lc d", lc=2),
                    wupv_h.rearrange("(lc p) d -> p lc d", p=128))
                dmarec["bd"] = dma_ser(gp, s_gw, bd_aug[0:8, :], bd_h)

            @blk.sync
            def _(sy):
                dmarec["sbd"] = dma_ser(sy, s_qx, sbd_st[:, :], sbd_h)
                sy.wait_ge(s_v0.sem, 5)
                dmarec["sbdrow"] = dma_ser(sy, s_qx, bd_aug[8:9, :], sbd_bf[:, :])

            @blk.vector
            def _(ve):
                ve.memset(consts[:, 0:1], LN_EPS)
                ve.memset(consts[:, 1:2], DECAY_SCALE)
                ve.memset(consts[:, 2:3], ATT_SCALE_LNBIAS)
                ve.memset(consts[:, 3:4], -1.0)
                ve.memset(consts[:, 4:5], -DECAY_SCALE)
                ve.memset(consts[:, 5:6], ATT_SCALE)
                ve.memset(onesf[:, :], 1.0)
                ve.memset(mrv[:, 2:8], -7.25e30)
                s_v0.inc(ve.memset(wt_sb[0:32, :], 1.0))
                ve.wait_ge(s_gpc.sem, 1)
                s_v0.inc(ve.tensor_single_scalar(ident[:, :], ident[:, :], 0.0,
                                                 op=ALU.is_equal))
                ve.wait_ge(s_qx.sem, dmarec["sbd"])
                s_v0.inc(ve.tensor_single_scalar(sbd_st[:, :], sbd_st[:, :], 1.0,
                                                 op=ALU.mult))
                ve.wait_ge(s_v0.sem, 3)
                s_v0.inc(ve.tensor_single_scalar(sbd_st[:, :], sbd_st[:, :], 0.1,
                                                 op=ALU.mult))
                ve.wait_ge(s_v0.sem, 4)
                s_v0.inc(ve.tensor_copy(sbd_bf[:, :], sbd_st[:, :]))

        with ExitStack() as stp1:
            latT = stp1.enter_context(nc.sbuf_tensor("latT", [128, 2 * S], BF16, side="left"))
            with ExitStack() as stp1q:
                xnT = stp1q.enter_context(nc.sbuf_tensor("xnT", [128, NKC * S], BF16, side="right"))
                wq_b = stp1q.enter_context(nc.sbuf_tensor("wq_b", [128, NKC * D], BF16, side="right"))
                with ExitStack() as stp1b:
                    wk_b = stp1b.enter_context(nc.sbuf_tensor("wk_b", [128, NKC * D], BF16, side="right"))
                    wv_b = stp1b.enter_context(nc.sbuf_tensor("wv_b", [128, NKC * D], BF16, side="right"))
                    wdn_b = stp1b.enter_context(nc.sbuf_tensor("wdn_b", [128, NKC * DL], BF16, side="right"))
                    kv_b = stp1b.enter_context(nc.sbuf_tensor("kv_b", [128, NKC * 512], BF16, side="right"))
                    with ExitStack() as stp1a:
                        sb1 = lambda n, sh, dt: stp1a.enter_context(nc.sbuf_tensor(n, sh, dt, side="right"))
                        xst = sb1("xst", [128, 2 * D], F32)
                        stats = sb1("stats", [128, 2 * 12], F32)
                        mv = sb1("mv", [128, 2 * 2], F32)
                        rstd = sb1("rstd", [128, 2], F32)
                        nmr = sb1("nmr", [128, 2], F32)
                        xnbf = sb1("xnbf", [128, 2 * D], BF16)
                        m8 = sb1("m8", [128, 8], F32)
                        killed = sb1("killed", [128, 8], F32)
                        gmask = sb1("gmask", [128, 8], F32)
                        gden = sb1("gden", [128, 1], F32)

                        VQ1, SQ1, TQ1 = 13, 11, 9
                        with ExitStack() as stps, nc.Block() as blk:
                            trp = [stps.enter_context(nc.psum_tensor(f"trp{i}", [128, 128], BF16)) for i in range(2)]
                            gps = stps.enter_context(nc.psum_tensor("gps", [128, NE], F32))
                            kvp = [stps.enter_context(nc.psum_tensor(f"kvp{i}", [128, 512], F32)) for i in range(2)]
                            latp = stps.enter_context(nc.psum_tensor("latp", [128, 512], F32))

                            @blk.gpsimd
                            def _(gp):
                                for nm, dst, src in (("wk", wk_b, wk_h), ("wv", wv_b, wv_h),
                                                     ("wq", wq_b, wq_h)):
                                    dmarec[nm] = dma_ser(gp, s_gw, dst.ap().rearrange("p (kc d) -> p kc d", kc=NKC),
                                        src.rearrange("(kc p) d -> p kc d", p=128))
                                dmarec["wdn"] = dma_ser(gp, s_gw, wdn_b.ap().rearrange("p (kc l) -> p kc l", kc=NKC),
                                    wdn_h.rearrange("(kc p) l -> p kc l", p=128))

                            @blk.sync
                            def _(sy):
                                for tt in range(0, NT, 2):
                                    if tt >= 2:
                                        sy.wait_ge(s_v1.sem, VQ1 * (tt - 2) + 1)
                                        sy.wait_ge(s_s1.sem, SQ1 * (tt - 2) + 2)
                                    dmarec[("xt", tt)] = dma_ser(sy, s_qx, xst[:, 0:D],
                                        x_h[tt * 128:(tt + 1) * 128, :])
                                sy.wait_ge(s_s1.sem, SQ1 * NT)
                                dmarec["xnt_spill"] = dma_ser(sy, s_qx, xnt_sp, xnT[:, :])

                            @blk.tensor
                            def _(te):
                                te.wait_ge(s_v0.sem, 2)
                                te.wait_ge(s_gw.sem, dmarec["gbias"])
                                for tt in range(NT):
                                    te.wait_ge(s_s1.sem, SQ1 * tt + 2)
                                    for i in range(8):
                                        gi = 8 * tt + i
                                        if gi >= 2:
                                            pj, pi = divmod(gi - 2, 8)
                                            te.wait_ge(s_s1.sem, SQ1 * pj + 3 + pi)
                                        s_t1.inc(te.transpose(
                                            trp[gi % 2][:, :],
                                            xnbf[:, (tt % 2) * D + i * 128:(tt % 2) * D + (i + 1) * 128],
                                            ident[:, :]))
                                    te.wait_ge(s_s1.sem, SQ1 * tt + 10)
                                    if tt >= 1:
                                        te.wait_ge(s_s1.sem, SQ1 * (tt - 1) + 11)
                                    for kc in range(NKC):
                                        te.matmul(gps[:, :],
                                                  xnT[:, kc * S + tt * 128:kc * S + (tt + 1) * 128],
                                                  gwt_b[:, kc * NE:(kc + 1) * NE],
                                                  start=(kc == 0), stop=False)
                                    s_t1.inc(te.matmul(gps[:, :], onesf[0:1, 0:128],
                                                       gbias_b[0:1, :], start=False, stop=True))
                                    if tt % 4 != 3:
                                        continue
                                    nn = tt // 4
                                    if nn == 0:
                                        te.wait_ge(s_gw.sem, dmarec["wdn"])
                                    for mt in range(8):
                                        gi = 10 * nn + mt
                                        if gi >= 2:
                                            te.wait_ge(s_v1b.sem, gi - 1)
                                        first = True
                                        last = None
                                        for wsrc in (wk_b, wv_b):
                                            for kc in range(NKC):
                                                last = te.matmul(
                                                    kvp[mt % 2][:, :],
                                                    wsrc[:, kc * D + mt * 128:kc * D + (mt + 1) * 128],
                                                    xnT[:, kc * S + nn * 512:kc * S + (nn + 1) * 512],
                                                    start=first,
                                                    stop=(wsrc is wv_b and kc == NKC - 1))
                                                first = False
                                        s_t1b.inc(last)
                                    te.wait_ge(s_v1b.sem, 10 * nn + 8)
                                    for lt in range(2):
                                        if nn + lt >= 1:
                                            te.wait_ge(s_v1b.sem, 10 * nn + 9 if lt == 1
                                                       else 10 * (nn - 1) + 10)
                                        last = None
                                        for kc in range(NKC):
                                            last = te.matmul(
                                                latp[:, :],
                                                wdn_b[:, kc * DL + lt * 128:kc * DL + (lt + 1) * 128],
                                                kv_b[:, kc * 512:(kc + 1) * 512],
                                                start=(kc == 0), stop=(kc == NKC - 1))
                                        s_t1b.inc(last)
                                assert s_t1.n == TQ1 * NT
                                assert s_t1b.n == 10 * NQC

                            @blk.scalar
                            def _(sc):
                                dmarec[("xt", 1)] = s_sx.inc(sc.dma_start(
                                    xst[:, D:2 * D], x_h[128:256, :]), 16)
                                for tt in range(NT):
                                    sc.wait_ge(s_v1.sem, VQ1 * tt + 2)
                                    s_s1.inc(sc.activation(rstd[:, tt % 2:tt % 2 + 1],
                                                           mv[:, (tt % 2) * 2 + 1:(tt % 2) * 2 + 2],
                                                           AF.Sqrt, bias=consts[:, 0:1]))
                                    sc.wait_ge(s_v1.sem, VQ1 * tt + 5)
                                    sc.wait_ge((s_qx if tt % 2 == 0 else s_sx).sem,
                                               dmarec[("xt", tt)])
                                    if tt >= 2:
                                        sc.wait_ge(s_t1.sem, TQ1 * (tt - 2) + 8)
                                    s_s1.inc(sc.activation(
                                        xnbf[:, (tt % 2) * D:(tt % 2 + 1) * D],
                                        xst[:, (tt % 2) * D:(tt % 2 + 1) * D],
                                        AF.Identity,
                                        bias=nmr[:, tt % 2:tt % 2 + 1],
                                        scale=rstd[:, tt % 2:tt % 2 + 1]))
                                    if tt % 2 == 1 and tt + 2 < NT:
                                        sc.wait_ge(s_s1.sem, SQ1 * tt + 2)
                                        dmarec[("xt", tt + 2)] = s_sx.inc(sc.dma_start(
                                            xst[:, D:2 * D],
                                            x_h[(tt + 2) * 128:(tt + 3) * 128, :]), 16)
                                    for i in range(8):
                                        sc.wait_ge(s_t1.sem, TQ1 * tt + i + 1)
                                        s_s1.inc(sc.activation(
                                            xnT[:, i * S + tt * 128:i * S + (tt + 1) * 128],
                                            trp[(8 * tt + i) % 2][:, :], AF.Identity))
                                    sc.wait_ge(s_t1.sem, TQ1 * tt + 9)
                                    s_s1.inc(sc.activation(gsig[:, tt * NE:(tt + 1) * NE],
                                                           gps[:, :], AF.Sigmoid))
                                assert s_s1.n == SQ1 * NT

                            @blk.vector
                            def _(ve):
                                ve.wait_ge(s_v0.sem, 5)
                                for tt in range(NT):
                                    b = tt % 2
                                    ve.wait_ge((s_qx if tt % 2 == 0 else s_sx).sem,
                                               dmarec[("xt", tt)])
                                    if tt >= 2:
                                        ve.wait_ge(s_v1.sem, VQ1 * (tt - 2) + 2)
                                    ve.bn_stats(stats[:, b * 12:b * 12 + 6],
                                                xst[:, b * D:b * D + 512])
                                    s_v1.inc(ve.bn_stats(stats[:, b * 12 + 6:b * 12 + 12],
                                                         xst[:, b * D + 512:b * D + 1024]))
                                    ve.wait_ge(s_v1.sem, VQ1 * tt + 1)
                                    if tt >= 2:
                                        ve.wait_ge(s_s1.sem, SQ1 * (tt - 2) + 1)
                                    s_v1.inc(ve.bn_aggr(mv[:, b * 2:b * 2 + 2],
                                                        stats[:, b * 12:b * 12 + 12]))
                                    ve.wait_ge(s_s1.sem, SQ1 * tt + 1)
                                    s_v1.inc(ve.reciprocal(rstd[:, b:b + 1], rstd[:, b:b + 1]))
                                    ve.wait_ge(s_v1.sem, VQ1 * tt + 3)
                                    s_v1.inc(ve.tensor_mul(nmr[:, b:b + 1], mv[:, b * 2:b * 2 + 1],
                                                           rstd[:, b:b + 1]))
                                    ve.wait_ge(s_v1.sem, VQ1 * tt + 4)
                                    s_v1.inc(ve.tensor_single_scalar(nmr[:, b:b + 1], nmr[:, b:b + 1],
                                                                     -1.0, op=ALU.mult))
                                    gs = gsig[:, tt * NE:(tt + 1) * NE]
                                    ve.wait_ge(s_s1.sem, SQ1 * tt + 11)
                                    s_v1.inc(ve.max(m8[:, :], gs))
                                    ve.wait_ge(s_v1.sem, VQ1 * tt + 6)
                                    if tt >= 1:
                                        ve.wait_ge(s_v1.sem, VQ1 * (tt - 1) + 8)
                                    s_v1.inc(ve.tensor_copy(mrv[:, 0:2], m8[:, 0:2]))
                                    ve.wait_ge(s_v1.sem, VQ1 * tt + 7)
                                    s_v1.inc(ve.match_replace(killed[:, :], mrv[:, :], gs, -1e30))
                                    ve.wait_ge(s_v1.sem, VQ1 * tt + 8)
                                    s_v1.inc(ve.tensor_tensor(gmask[:, :], gs, killed[:, :],
                                                              op=ALU.is_gt))
                                    s_v1.inc(ve.tensor_add(gden[:, :], m8[:, 0:1], m8[:, 1:2]))
                                    ve.wait_ge(s_v1.sem, VQ1 * tt + 10)
                                    s_v1.inc(ve.tensor_single_scalar(gden[:, :], gden[:, :],
                                                                     GATE_EPS, op=ALU.add))
                                    ve.wait_ge(s_v1.sem, VQ1 * tt + 11)
                                    s_v1.inc(ve.reciprocal(gden[:, :], gden[:, :]))
                                    ve.wait_ge(s_v1.sem, VQ1 * tt + 12)
                                    s_v1.inc(ve.scalar_tensor_tensor(
                                        wgt[:, tt * NE:(tt + 1) * NE], gs, gden[:, :],
                                        gmask[:, :], op0=ALU.mult, op1=ALU.mult))
                                    if tt % 4 == 3:
                                        nn = tt // 4
                                        for mt in range(8):
                                            ve.wait_ge(s_t1b.sem, 10 * nn + mt + 1)
                                            if mt == 0 and nn >= 1:
                                                ve.wait_ge(s_t1b.sem, 10 * (nn - 1) + 10)
                                            s_v1b.inc(ve.tensor_copy(kv_b[:, mt * 512:(mt + 1) * 512],
                                                                     kvp[mt % 2][:, :]))
                                        for lt in range(2):
                                            ve.wait_ge(s_t1b.sem, 10 * nn + 9 + lt)
                                            s_v1b.inc(ve.tensor_copy(
                                                latT[:, lt * S + nn * 512:lt * S + (nn + 1) * 512],
                                                latp[:, :]))
                                assert s_v1.n == VQ1 * NT
                                assert s_v1b.n == 10 * NQC

                qT = stp1.enter_context(nc.sbuf_tensor("qT", [128, NKC * S], BF16, side="left"))
                with ExitStack() as stps, nc.Block() as blk:
                    qp = [stps.enter_context(nc.psum_tensor(f"qp{i}", [128, 512], F32)) for i in range(2)]

                    @blk.tensor
                    def _(te):
                        te.wait_ge(s_gw.sem, dmarec["wq"])
                        for mt in range(8):
                            for nn in range(NQC):
                                gi = mt * NQC + nn
                                if gi >= 2:
                                    te.wait_ge(s_s1c.sem, gi - 1)
                                last = None
                                for kc in range(NKC):
                                    last = te.matmul(
                                        qp[gi % 2][:, :],
                                        wq_b[:, kc * D + mt * 128:kc * D + (mt + 1) * 128],
                                        xnT[:, kc * S + nn * 512:kc * S + (nn + 1) * 512],
                                        start=(kc == 0), stop=(kc == NKC - 1))
                                s_t1c.inc(last)
                        assert s_t1c.n == 8 * NQC

                    @blk.scalar
                    def _(sc):
                        for mt in range(8):
                            for nn in range(NQC):
                                gi = mt * NQC + nn
                                sc.wait_ge(s_t1c.sem, gi + 1)
                                s_s1c.inc(sc.activation(
                                    qT[:, mt * S + nn * 512:mt * S + (nn + 1) * 512],
                                    qp[gi % 2][:, :], AF.Identity))
                        assert s_s1c.n == 8 * NQC

                if debug:
                    with nc.Block() as blk:
                        @blk.sync
                        def _(sy):
                            dma_ser(sy, s_qx, dbg_lat, latT[:, :])
                            dma_ser(sy, s_qx, dbg_q, qT[:, :])
                            dma_ser(sy, s_qx, dbg_gsig, gsig[:, :])
                            dma_ser(sy, s_qx, dbg_w, wgt[:, :])
                            sy.wait_ge(s_qx.sem, s_qx.n)

            with ExitStack() as stp2:
                sb2 = lambda n, sh, dt: stp2.enter_context(nc.sbuf_tensor(n, sh, dt, side="right"))
                rowP = sb2("rowP", [128, S], BF16)
                rowM = sb2("rowM", [128, S], BF16)
                decayR = sb2("decayR", [128, 1024], BF16)
                s2s_base = s_s2s.n
                v2s_base = s_v2s.n
                with ExitStack() as stp2s:
                    sb2t = lambda n, sh, dt: stp2s.enter_context(nc.sbuf_tensor(n, sh, dt, side="right"))
                    dmraw = sb2t("dmraw", [128, 1024], F32)
                    dmrawB = sb2t("dmrawB", [128, 1024], F32)
                    rowraw = sb2t("rowraw", [128, S], F32)
                    with nc.Block() as blk:
                        @blk.gpsimd
                        def _(gp):
                            s_gpc.inc(gp.iota(dmraw[:, :], pattern=[[-1, 1024]], base=384,
                                              channel_multiplier=1,
                                              allow_small_or_imprecise_dtypes=True))
                            s_gpc.inc(gp.iota(rowraw[:, :], pattern=[[-1, S]], base=0,
                                              channel_multiplier=0,
                                              allow_small_or_imprecise_dtypes=True))

                        @blk.scalar
                        def _(sc):
                            sc.wait_ge(s_gpc.sem, s_gpc.n)
                            s_s2s.inc(sc.activation(dmrawB[:, :], dmraw[:, :], AF.Abs))
                            s_s2s.inc(sc.activation(rowP[:, :], rowraw[:, :], AF.Exp,
                                                    scale=consts[:, 1:2]))
                            s_s2s.inc(sc.activation(rowM[:, :], rowraw[:, :], AF.Exp,
                                                    scale=consts[:, 4:5]))
                            sc.wait_ge(s_v2s.sem, v2s_base + 1)
                            s_s2s.inc(sc.activation(decayR[:, :], dmrawB[:, :], AF.Exp,
                                                    scale=consts[:, 1:2]))

                        @blk.vector
                        def _(ve):
                            ve.wait_ge(s_gpc.sem, s_gpc.n)
                            ve.wait_ge(s_s2s.sem, s2s_base + 1)
                            s_v2s.inc(ve.tensor_add(dmrawB[:, :], dmrawB[:, :], dmraw[:, :]))

                krecTP = sb2("krecTP", [128, 2 * S], BF16)
                krecTM = sb2("krecTM", [128, 2 * S], BF16)
                vrec = sb2("vrec", [128, NT * 260], BF16)
                expT = sb2("expT", [128, 6 * 1024], BF16)
                qstg = [sb2(f"qstg{i}", [128, 1024], BF16) for i in range(2)]
                pvT = sb2("pvT", [128, 2 * S], BF16)
                pvst = [sb2(f"pvst{i}", [64, 512], BF16) for i in range(2)]
                bcsb = [sb2(f"bcsb{i}", [64, 512], F32) for i in range(2)]
                rr_sb = sb2("rr_sb", [128, 512], F32)

                NHK = KT // 2
                JPG = 4 * NQC
                NJOBS = 4 * JPG
                SQ2 = NHK + 2
                VB = 7 * JPG + 1
                trec = {}
                oddrec = {}
                ojn_s = [0]
                ojn_v = [0]

                def vseg(J):
                    gj, pj = divmod(J, JPG)
                    return gj * VB + 7 * pj

                def evac_done(J):
                    gj, pj = divmod(J, JPG)
                    if pj == JPG - 1:
                        return (gj + 1) * VB
                    return gj * VB + 7 * (pj + 2)

                LAG = min(4, NHK)
                LSL = min(4, NHK)
                SB = SQ2 * JPG + 2

                def exp_done(J, hk):
                    gj, pj = divmod(J, JPG)
                    return gj * SB + SQ2 * pj + hk + 1 + (2 if hk >= LSL else 0)

                def rr_done(J):
                    gj, pj = divmod(J, JPG)
                    if pj == JPG - 1:
                        return (gj + 1) * SB
                    return gj * SB + SQ2 * (pj + 1) + LSL + 2

                stps2 = stp2.enter_context(ExitStack())
                scp = [stps2.enter_context(nc.psum_tensor(f"scp{i}", [128, 1024], F32)) for i in range(3)]
                pvp = [stps2.enter_context(nc.psum_tensor(f"pvp{i}", [128, 512], F32)) for i in range(2)]

                for g in range(4):
                    with nc.Block() as blk:
                        krp = [scp[0][:, i * 512:(i + 1) * 512] for i in range(2)]
                        vrp = [scp[1][:, i * 512:i * 512 + 256] for i in range(2)]
                        T0p, V0p = s_t2p.n, s_v2p.n
                        NKR = 2 * NQC

                        @blk.tensor
                        def _(te):
                            te.wait_ge(s_gw.sem, dmarec["wupv"])
                            if g == 0:
                                te.wait_ge(s_v1b.sem, 10 * NQC)
                            else:
                                te.wait_ge(s_s2m.sem, SB * g)
                                te.wait_ge(s_v2m.sem, VB * g)
                            for mt in range(2):
                                for nn in range(NQC):
                                    gi = mt * NQC + nn
                                    if gi >= 2:
                                        te.wait_ge(s_v2p.sem, V0p + 1 + 2 * (gi - 1))
                                    last = None
                                    for lc in range(2):
                                        last = te.matmul(
                                            krp[gi % 2],
                                            wupk_b[:, lc * D + g * 256 + mt * 128:lc * D + g * 256 + (mt + 1) * 128],
                                            latT[:, lc * S + nn * 512:lc * S + (nn + 1) * 512],
                                            start=(lc == 0), stop=(lc == 1))
                                    s_t2p.inc(last)
                            for tt in range(NT):
                                if tt >= 2:
                                    te.wait_ge(s_v2p.sem, V0p + 1 + 2 * NKR + tt - 1)
                                last = None
                                for lc in range(2):
                                    last = te.matmul(
                                        vrp[tt % 2],
                                        latT[:, lc * S + tt * 128:lc * S + (tt + 1) * 128],
                                        wupv_b[:, lc * D + g * 256:lc * D + (g + 1) * 256],
                                        start=(lc == 0), stop=(lc == 1))
                                s_t2p.inc(last)
                            assert s_t2p.n == T0p + NKR + NT

                        @blk.vector
                        def _(ve):
                            if g == 0:
                                ve.wait_ge(s_s2s.sem, s2s_base + 3)
                            else:
                                ve.wait_ge(s_t2m.sem, trec[(JPG * g - 1, "pv", NHK - 1)])
                            s_v2p.inc(ve.memset(vrec[:, :], 1.0))
                            for mt in range(2):
                                for nn in range(NQC):
                                    gi = mt * NQC + nn
                                    ve.wait_ge(s_t2p.sem, T0p + gi + 1)
                                    s_v2p.inc(ve.tensor_mul(
                                        krecTP[:, mt * S + nn * 512:mt * S + (nn + 1) * 512],
                                        krp[gi % 2],
                                        rowP[:, nn * 512:(nn + 1) * 512]))
                                    s_v2p.inc(ve.tensor_mul(
                                        krecTM[:, mt * S + nn * 512:mt * S + (nn + 1) * 512],
                                        krp[gi % 2],
                                        rowM[:, nn * 512:(nn + 1) * 512]))
                            ve.wait_ge(s_v2p.sem, V0p + 1)
                            for tt in range(NT):
                                ve.wait_ge(s_t2p.sem, T0p + NKR + tt + 1)
                                last = None
                                for lam in range(4):
                                    last = ve.tensor_copy(
                                        vrec[:, tt * 260 + lam * 65:tt * 260 + lam * 65 + 64],
                                        vrp[tt % 2][:, lam * 64:(lam + 1) * 64])
                                s_v2p.inc(last)
                            assert s_v2p.n == V0p + 1 + 2 * NKR + NT

                    with nc.Block() as blk:
                        v2p_done = s_v2p.n
                        s1c_done = s_s1c.n

                        def jinfo(pj):
                            J = JPG * g + pj
                            lam, qc = divmod(pj, NQC)
                            return J, lam, qc

                        @blk.tensor
                        def _(te):
                            te.wait_ge(s_v2p.sem, v2p_done)
                            te.wait_ge(s_s1c.sem, s1c_done)

                            def emit_sc(pj, hk):
                                J, lam, qc = jinfo(pj)
                                pbase = 64 * (lam % 2)
                                ktile = lam // 2
                                lh = NHK * J + hk
                                u = lh % 3
                                for half in range(2):
                                    kt = 2 * hk + half
                                    if kt < 4 * qc + 4:
                                        ksrc, qcol = krecTP, 0
                                    else:
                                        ksrc, qcol = krecTM, 512
                                    trec[(J, "sc", kt)] = s_t2m.inc(te.matmul(
                                        scp[u][:, half * 512:(half + 1) * 512],
                                        ksrc[pbase:pbase + 64, ktile * S + kt * 128:ktile * S + (kt + 1) * 128],
                                        qstg[J % 2][pbase:pbase + 64, qcol:qcol + 512],
                                        start=True, stop=True))

                            def emit_pv(pvJ, hk):
                                pj2 = pvJ - JPG * g
                                lam, qc = divmod(pj2, NQC)
                                lh = NHK * pvJ + hk
                                u6 = lh % 6
                                c = None
                                for half in range(2):
                                    kt = 2 * hk + half
                                    mv = expT[:, u6 * 1024 + half * 512:u6 * 1024 + (half + 1) * 512]
                                    c = s_t2m.inc(te.matmul(
                                        pvp[pvJ % 2][0:65, :],
                                        vrec[:, kt * 260 + lam * 65:kt * 260 + lam * 65 + 65],
                                        mv,
                                        start=(kt == 0), stop=(kt == KT - 1)))
                                trec[(pvJ, "pv", hk)] = c

                            pending = []
                            for pj in range(JPG):
                                J, lam, qc = jinfo(pj)
                                for b in range(NHK // 2):
                                    hk0 = 2 * b
                                    pops = []
                                    pending.append((J, hk0))
                                    pending.append((J, hk0 + 1))
                                    while len(pending) > LAG:
                                        pops.append(pending.pop(0))
                                    scw = 0
                                    lh1 = NHK * J + hk0 + 1
                                    if lh1 >= 3:
                                        Jp, hkp = divmod(lh1 - 3, NHK)
                                        scw = exp_done(Jp, hkp)
                                    for (Jv, hkv) in pops:
                                        scw = max(scw, exp_done(Jv, hkv))
                                    if scw:
                                        te.wait_ge(s_s2m.sem, scw)
                                    if hk0 == 0:
                                        te.wait_ge(s_v2m.sem, vseg(J) + 2)
                                    for (Jv, hkv) in pops:
                                        if hkv == 0 and Jv >= 2:
                                            te.wait_ge(s_v2m.sem, evac_done(Jv - 2))
                                    emit_sc(pj, hk0)
                                    emit_sc(pj, hk0 + 1)
                                    for (Jv, hkv) in pops:
                                        emit_pv(Jv, hkv)
                            if g == 3 or True:
                                scw = 0
                                for (Jv, hkv) in pending:
                                    scw = max(scw, exp_done(Jv, hkv))
                                te.wait_ge(s_s2m.sem, scw)
                                for (Jv, hkv) in pending:
                                    if hkv == 0 and Jv >= 2:
                                        te.wait_ge(s_v2m.sem, evac_done(Jv - 2))
                                    emit_pv(Jv, hkv)
                                pending = []

                        @blk.sync
                        def _(sy):
                            for pj in range(JPG):
                                J, lam, qc = jinfo(pj)
                                sy.wait_ge(s_s2m.sem, rr_done(J))
                                dmarec[("rrst", J)] = dma_ser(sy, s_bc,
                                    rr_dram[J % 2:J % 2 + 1, :], rr_sb[64:65, :])
                                if J >= 2:
                                    sy.wait_ge(s_v2m.sem, evac_done(J - 2))
                                dmarec[("bc", J)] = dma_ser(sy, s_bc,
                                    bcsb[J % 2][0:64, :],
                                    bass.AP(rr_dram.tensor, (J % 2) * 512, [[0, 64], [1, 512]]))
                                if lam % 2 == 1:
                                    if g > 0 and pj == NQC:
                                        sy.wait_ge(s_gsp.sem, dmarec[("pvspill", g - 1)])
                                    sy.wait_ge(s_v2m.sem, evac_done(J))
                                    oj = ojn_s[0]
                                    oddrec[oj] = dma_ser(sy, s_qx,
                                        pvT[64:128, (lam // 2) * S + qc * 512:(lam // 2) * S + (qc + 1) * 512],
                                        pvst[oj % 2][0:64, :])
                                    ojn_s[0] = oj + 1

                        @blk.gpsimd
                        def _(gp):
                            gp.wait_ge(s_v2m.sem, (g + 1) * VB)
                            gp.wait_ge(s_qx.sem, s_qx.n)
                            dmarec[("pvspill", g)] = dma_ser(gp, s_gsp,
                                pv_sp[:, (2 * g) * S:(2 * g + 2) * S], pvT[:, :])

                        @blk.scalar
                        def _(sc):
                            def emit_exp(pj, hk):
                                J, lam, qc = jinfo(pj)
                                lh = NHK * J + hk
                                u = lh % 3
                                u6 = lh % 6
                                if hk in (2 * qc, 2 * qc + 1):
                                    b = hk - 2 * qc
                                    sc.wait_ge(s_v2m.sem, vseg(J) + 4 + 2 * b)
                                    s_s2m.inc(sc.activation(expT[:, u6 * 1024:(u6 + 1) * 1024],
                                                            expT[:, u6 * 1024:(u6 + 1) * 1024],
                                                            AF.Exp, scale=consts[:, 5:6]))
                                else:
                                    sc.wait_ge(s_t2m.sem, trec[(J, "sc", 2 * hk + 1)])
                                    if lh >= 6:
                                        Jp, hkp = divmod(lh - 6, NHK)
                                        sc.wait_ge(s_t2m.sem, trec[(Jp, "pv", hkp)])
                                    s_s2m.inc(sc.activation(expT[:, u6 * 1024:(u6 + 1) * 1024],
                                                            scp[u][:, :], AF.Exp,
                                                            scale=consts[:, 5:6]))

                            def emit_lnrr(Jm):
                                sc.wait_ge(s_t2m.sem, trec[(Jm, "pv", NHK - 1)])
                                sc.wait_ge(s_s2m.sem, s_s2m.n)
                                if Jm >= 1:
                                    sc.wait_ge(s_bc.sem, dmarec[("rrst", Jm - 1)])
                                s_s2m.inc(sc.activation(rr_sb[64:65, :], pvp[Jm % 2][64:65, :],
                                                        AF.Ln))
                                sc.wait_ge(s_s2m.sem, s_s2m.n)
                                s_s2m.inc(sc.activation(rr_sb[64:65, :], rr_sb[64:65, :],
                                                        AF.Exp, scale=consts[64:65, 3:4]))

                            for pj in range(JPG):
                                J, lam, qc = jinfo(pj)
                                for hk in range(min(LSL, NHK)):
                                    emit_exp(pj, hk)
                                if pj >= 1:
                                    emit_lnrr(J - 1)
                                else:
                                    s_s2m.inc(sc.activation(rr_sb[0:1, 1:2], consts[0:1, 0:1],
                                                            AF.Identity))
                                    s_s2m.inc(sc.activation(rr_sb[0:1, 2:3], consts[0:1, 0:1],
                                                            AF.Identity))
                                for hk in range(min(LSL, NHK), NHK):
                                    emit_exp(pj, hk)
                            emit_lnrr(JPG * (g + 1) - 1)
                            assert s_s2m.n == SB * (g + 1)

                        @blk.vector
                        def _(ve):
                            if g == 0:
                                ve.wait_ge(s_s2s.sem, s2s_base + 4)

                            def emit_evac(Jm):
                                lamm, qcm = divmod(Jm % JPG, NQC)
                                ve.wait_ge(s_t2m.sem, trec[(Jm, "pv", NHK - 1)])
                                ve.wait_ge(s_bc.sem, dmarec[("bc", Jm)])
                                if lamm % 2 == 1:
                                    oj = ojn_v[0]
                                    if oj >= 2:
                                        ve.wait_ge(s_qx.sem, oddrec[oj - 2])
                                    dst = pvst[oj % 2][0:64, :]
                                    ojn_v[0] = oj + 1
                                else:
                                    if Jm % JPG == 0 and Jm >= JPG:
                                        ve.wait_ge(s_gsp.sem, dmarec[("pvspill", Jm // JPG - 1)])
                                    dst = pvT[0:64, (lamm // 2) * S + qcm * 512:(lamm // 2) * S + (qcm + 1) * 512]
                                s_v2m.inc(ve.tensor_mul(dst, pvp[Jm % 2][0:64, :],
                                                        bcsb[Jm % 2][:, :]))

                            for pj in range(JPG):
                                J, lam, qc = jinfo(pj)
                                pbase = 64 * (lam % 2)
                                qtile = (g * 4 + lam) // 2
                                if J >= 2:
                                    ve.wait_ge(s_t2m.sem, trec[(J - 2, "sc", KT - 1)])
                                qsl = qT[pbase:pbase + 64, qtile * S + qc * 512:qtile * S + (qc + 1) * 512]
                                s_v2m.inc(ve.tensor_mul(qstg[J % 2][pbase:pbase + 64, 0:512],
                                                        qsl, rowM[pbase:pbase + 64, qc * 512:(qc + 1) * 512]))
                                s_v2m.inc(ve.tensor_mul(qstg[J % 2][pbase:pbase + 64, 512:1024],
                                                        qsl, rowP[pbase:pbase + 64, qc * 512:(qc + 1) * 512]))
                                for i0 in range(4):
                                    kt = 4 * qc + i0
                                    hk = 2 * qc + i0 // 2
                                    lh = NHK * J + hk
                                    u = lh % 3
                                    u6 = lh % 6
                                    ve.wait_ge(s_t2m.sem, trec[(J, "sc", kt)])
                                    if lh >= 6:
                                        Jp, hkp = divmod(lh - 6, NHK)
                                        ve.wait_ge(s_t2m.sem, trec[(Jp, "pv", hkp)])
                                    s_v2m.inc(ve.tensor_mul(
                                        expT[:, u6 * 1024 + (kt % 2) * 512:u6 * 1024 + (kt % 2) * 512 + 512],
                                        scp[u][:, (kt % 2) * 512:(kt % 2 + 1) * 512],
                                        decayR[:, 384 - 128 * i0:384 - 128 * i0 + 512]))
                                if pj >= 1:
                                    emit_evac(J - 1)
                                else:
                                    s_v2m.inc(ve.memset(rr_sb[0:1, 0:1], 0.0))
                            emit_evac(JPG * (g + 1) - 1)
                            assert s_v2m.n == VB * (g + 1)

        with ExitStack() as stp3:
            sb3 = lambda n, sh, dt: stp3.enter_context(nc.sbuf_tensor(n, sh, dt, side="right"))
            pvF = sb3("pvF", [128, NKC * S], BF16)
            wo_b = sb3("wo_b", [128, NKC * D], BF16)
            ast = sb3("ast", [128, 2 * D], F32)
            with ExitStack() as stps, nc.Block() as blk:
                wop = [stps.enter_context(nc.psum_tensor(f"wop{i}", [128, 512], F32)) for i in range(2)]

                @blk.gpsimd
                def _(gp):
                    dmarec["wo"] = dma_ser(gp, s_gw, wo_b.ap().rearrange("p (kc d) -> p kc d", kc=NKC),
                        wo_h.rearrange("(kc p) d -> p kc d", p=128))
                    for tt in range(1, NT, 2):
                        if tt == 1:
                            gp.wait_ge(s_gsp.sem, dmarec[("pvspill", 3)])
                        dmarec[("pvl", tt)] = dma_ser(gp, s_gsp,
                            pvF.ap().rearrange("p (kc t) -> p kc t", kc=NKC)[:, :, tt * 128:(tt + 1) * 128],
                            pv_sp.rearrange("p (kc t) -> p kc t", kc=NKC)[:, :, tt * 128:(tt + 1) * 128])

                @blk.sync
                def _(sy):
                    for tt in range(0, NT, 2):
                        if tt == 0:
                            sy.wait_ge(s_gsp.sem, dmarec[("pvspill", 3)])
                        dmarec[("pvl", tt)] = dma_ser(sy, s_qx,
                            pvF.ap().rearrange("p (kc t) -> p kc t", kc=NKC)[:, :, tt * 128:(tt + 1) * 128],
                            pv_sp.rearrange("p (kc t) -> p kc t", kc=NKC)[:, :, tt * 128:(tt + 1) * 128])
                    for tt in range(NT):
                        sy.wait_ge(s_v3.sem, 2 * tt + 2)
                        dmarec[("attn", tt)] = dma_ser(sy, s_qx, attn_sp[tt * 128:(tt + 1) * 128, :],
                            ast[:, (tt % 2) * D:(tt % 2 + 1) * D])

                @blk.tensor
                def _(te):
                    te.wait_ge(s_gw.sem, dmarec["wo"])
                    for tt in range(NT):
                        te.wait_ge(s_qx.sem if tt % 2 == 0 else s_gsp.sem, dmarec[("pvl", tt)])
                        for nn in range(2):
                            gi = 2 * tt + nn
                            if gi >= 2:
                                te.wait_ge(s_v3.sem, gi - 1)
                            last = None
                            for kc in range(NKC):
                                last = te.matmul(
                                    wop[gi % 2][:, :],
                                    pvF[:, kc * S + tt * 128:kc * S + (tt + 1) * 128],
                                    wo_b[:, kc * D + nn * 512:kc * D + (nn + 1) * 512],
                                    start=(kc == 0), stop=(kc == NKC - 1))
                            s_t3.inc(last)
                    assert s_t3.n == 2 * NT

                @blk.vector
                def _(ve):
                    for tt in range(NT):
                        for nn in range(2):
                            gi = 2 * tt + nn
                            ve.wait_ge(s_t3.sem, gi + 1)
                            if tt >= 2 and nn == 0:
                                ve.wait_ge(s_qx.sem, dmarec[("attn", tt - 2)])
                            s_v3.inc(ve.tensor_copy(
                                ast[:, (tt % 2) * D + nn * 512:(tt % 2) * D + (nn + 1) * 512],
                                wop[gi % 2][:, :]))
                    assert s_v3.n == 2 * NT

        with ExitStack() as stp4:
            sb4 = lambda n, sh, dt: stp4.enter_context(nc.sbuf_tensor(n, sh, dt, side="right"))
            xnTh = sb4("xnTh", [128, NKC * TH], BF16)
            comb = sb4("comb", [128, NTTH * D], F32)
            wu_sb = sb4("wu_sb", [128, NKC * FCB], BF16)
            wd_sb = sb4("wd_sb", [128, NFT * D], BF16)
            a1s = sb4("a1s", [128, NFT * TC], BF16)
            bu_sb = sb4("bu_sb", [128, 2 * (FF // 128)], F32)
            wbf = sb4("wbf", [128, 8], BF16)
            xst5 = sb4("xst5", [128, 2 * D], F32)
            att5 = sb4("att5", [128, 2 * D], F32)

            halfrec = {}
            TQ4 = NTC * (NFT + NTI)
            SQ4 = NTC * NFT
            VQ4 = NTTH
            H1LAST = (NTC - 1) * (NFT + NTI) + NFT

            def t4_h1_val(bl, tc, ft):
                return bl * TQ4 + tc * (NFT + NTI) + ft + 1

            def t4_y_val(bl, tc, tti):
                return bl * TQ4 + tc * (NFT + NTI) + NFT + tti + 1

            for half in range(2):
                with ExitStack() as stps, nc.Block() as blk:
                    trwp = stps.enter_context(nc.psum_tensor(f"trw{half}", [8, 128], BF16))
                    wbdp = [stps.enter_context(nc.psum_tensor(f"wbd{half}{i}", [128, D], F32)) for i in range(2)]
                    T0w, V0w = s_t4w.n, s_v4w.n

                    @blk.sync
                    def _(sy):
                        for tt in range(NTTH):
                            r0 = (half * NTTH + tt) * 128
                            if tt >= 2:
                                sy.wait_ge(s_v4w.sem, V0w + 5 * (tt - 2) + 5)
                            dmarec[("x5", half, tt)] = dma_ser(sy, s_qx, xst5[:, (tt % 2) * D:(tt % 2 + 1) * D],
                                x_h[r0:r0 + 128, :])

                    @blk.scalar
                    def _(sc):
                        for tt in range(NTTH):
                            r0 = (half * NTTH + tt) * 128
                            if tt >= 2:
                                sc.wait_ge(s_v4w.sem, V0w + 5 * (tt - 2) + 5)
                            dmarec[("a5", half, tt)] = dma_ser(sc, s_sx, att5[:, (tt % 2) * D:(tt % 2 + 1) * D],
                                attn_sp[r0:r0 + 128, :])

                    @blk.gpsimd
                    def _(gp):
                        if half == 1:
                            v4d = halfrec["v4done0"]
                            for tt in range(NTTH):
                                r0 = tt * 128
                                gp.wait_ge(s_v4.sem, v4d - VQ4 + tt + 1)
                                dmarec[("out", 0, tt)] = dma_ser(gp, s_gsp, out_h[r0:r0 + 128, :],
                                    comb[:, tt * D:(tt + 1) * D])
                        gp.wait_ge(s_qx.sem, dmarec["xnt_spill"])
                        dmarec[("xnth", half)] = dma_ser(gp, s_gsp, xnTh.ap().rearrange("p (kc t) -> p kc t", kc=NKC),
                            xnt_sp.rearrange("p (kc t) -> p kc t", kc=NKC)[:, :, half * TH:(half + 1) * TH])

                    @blk.tensor
                    def _(te):
                        te.wait_ge(s_gw.sem, dmarec["bd"])
                        te.wait_ge(s_qx.sem, dmarec["sbdrow"])
                        for tt in range(NTTH):
                            te.wait_ge(s_v4w.sem, V0w + 5 * tt + 1)
                            if tt >= 1:
                                te.wait_ge(s_v4w.sem, V0w + 5 * (tt - 1) + 2)
                            s_t4w.inc(te.transpose(trwp[:, :], wbf[:, :], ident[:, :]))
                            te.wait_ge(s_v4w.sem, V0w + 5 * tt + 2)
                            if tt >= 2:
                                te.wait_ge(s_v4w.sem, V0w + 5 * (tt - 2) + 3)
                            last = None
                            for nn in range(2):
                                last = te.matmul(wbdp[tt % 2][:, nn * 512:(nn + 1) * 512],
                                                 wt_sb[0:9, :],
                                                 bd_aug[0:9, nn * 512:(nn + 1) * 512],
                                                 start=True, stop=True)
                            s_t4w.inc(last)
                        assert s_t4w.n == T0w + 2 * NTTH

                    @blk.vector
                    def _(ve):
                        if half == 1:
                            ve.wait_ge(s_qx.sem, dmarec[("out", 0, NTTH - 1)])
                        ve.wait_ge(s_v0.sem, 1)
                        for tt in range(NTTH):
                            gt = half * NTTH + tt
                            if tt >= 1:
                                ve.wait_ge(s_t4w.sem, T0w + 2 * (tt - 1) + 1)
                            s_v4w.inc(ve.tensor_copy(wbf[:, :],
                                                     wgt[:, gt * NE:(gt + 1) * NE]))
                            ve.wait_ge(s_t4w.sem, T0w + 2 * tt + 1)
                            s_v4w.inc(ve.tensor_copy(wt_sb[0:8, :], trwp[:, :]))
                            ve.wait_ge(s_t4w.sem, T0w + 2 * tt + 2)
                            if half == 1:
                                ve.wait_ge(s_gsp.sem, dmarec[("out", 0, tt)])
                            s_v4w.inc(ve.tensor_copy(comb[:, tt * D:(tt + 1) * D],
                                                     wbdp[tt % 2][:, :]))
                            ve.wait_ge(s_qx.sem, dmarec[("x5", half, tt)])
                            ve.wait_ge(s_sx.sem, dmarec[("a5", half, tt)])
                            ve.wait_ge(s_v4w.sem, V0w + 5 * tt + 3)
                            s_v4w.inc(ve.tensor_add(comb[:, tt * D:(tt + 1) * D],
                                                    comb[:, tt * D:(tt + 1) * D],
                                                    xst5[:, (tt % 2) * D:(tt % 2 + 1) * D]))
                            ve.wait_ge(s_v4w.sem, V0w + 5 * tt + 4)
                            s_v4w.inc(ve.tensor_add(comb[:, tt * D:(tt + 1) * D],
                                                    comb[:, tt * D:(tt + 1) * D],
                                                    att5[:, (tt % 2) * D:(tt % 2 + 1) * D]))
                        assert s_v4w.n == V0w + 5 * NTTH

                with ExitStack() as stps, nc.Block() as blk:
                    h1p = [stps.enter_context(nc.psum_tensor(f"h1p{half}{i}", [128, TC], F32)) for i in range(2)]
                    yp = [stps.enter_context(nc.psum_tensor(f"yp{half}{i}", [128, D], F32)) for i in range(2)]
                    T0, S0, V0 = s_t4.n, s_s4.n, s_v4.n
                    B0 = T0 // TQ4
                    v4w_done = s_v4w.n

                    @blk.gpsimd
                    def _(gp):
                        for e in range(9):
                            for fc in range(NFC):
                                b = B0 + e * NFC + fc
                                wsrc = wut_h[e, :, :] if e < NE else swut_h
                                dsrc = wdt_h[e, :, :] if e < NE else swdt_h
                                if b >= 1:
                                    gp.wait_ge(s_t4.sem, TQ4 * (b - 1) + H1LAST)
                                dmarec[("wu", b)] = dma_ser(gp, s_gw, wu_sb.ap().rearrange("p (kc f) -> p kc f", kc=NKC),
                                    wsrc[:, fc * FCB:(fc + 1) * FCB].rearrange(
                                        "(kc p) f -> p kc f", p=128))
                                if b >= 1:
                                    gp.wait_ge(s_t4.sem, TQ4 * b)
                                dmarec[("wd", b)] = dma_ser(gp, s_gw, wd_sb.ap().rearrange("p (fi d) -> p fi d", fi=NFT),
                                    dsrc[fc * FCB:(fc + 1) * FCB, :].rearrange(
                                        "(fi p) d -> p fi d", p=128))

                    @blk.sync
                    def _(sy):
                        for e in range(9):
                            b = B0 + e * NFC
                            if e >= 2:
                                sy.wait_ge(s_s4.sem, SQ4 * (b - NFC))
                            src = bu2_h[e, :, :] if e < NE else sbu2_h
                            dmarec[("bu", half, e)] = dma_ser(sy, s_qx, bu_sb[:, (e % 2) * 32:(e % 2) * 32 + 32], src)

                    @blk.tensor
                    def _(te):
                        te.wait_ge(s_gsp.sem, dmarec[("xnth", half)])
                        for e in range(9):
                            for fc in range(NFC):
                                bl = e * NFC + fc
                                b = B0 + bl
                                te.wait_ge(s_gw.sem, dmarec[("wu", b)])
                                for tc in range(NTC):
                                    for ft in range(NFT):
                                        hh = SQ4 * b + tc * NFT + ft
                                        if hh >= 2:
                                            te.wait_ge(s_s4.sem, hh - 1)
                                        last = None
                                        for kc in range(NKC):
                                            for no, nw in nsplit(TC):
                                                last = te.matmul(
                                                    h1p[(tc * NFT + ft) % 2][:, no:no + nw],
                                                    wu_sb[:, kc * FCB + ft * 128:kc * FCB + (ft + 1) * 128],
                                                    xnTh[:, kc * TH + tc * TC + no:kc * TH + tc * TC + no + nw],
                                                    start=(kc == 0), stop=(kc == NKC - 1))
                                        s_t4.inc(last)
                                    te.wait_ge(s_gw.sem, dmarec[("wd", b)])
                                    for tti in range(NTI):
                                        tt = tc * NTI + tti
                                        yy = VQ4 * b + tt
                                        if yy >= 2:
                                            te.wait_ge(s_v4.sem, yy - 1)
                                        te.wait_ge(s_s4.sem, SQ4 * b + NFT * (tc + 1))
                                        last = None
                                        for kc in range(NFT):
                                            for no, nw in nsplit(D):
                                                last = te.matmul(
                                                    yp[tt % 2][:, no:no + nw],
                                                    a1s[:, kc * TC + tti * 128:kc * TC + (tti + 1) * 128],
                                                    wd_sb[:, kc * D + no:kc * D + no + nw],
                                                    start=(kc == 0), stop=(kc == NFT - 1))
                                        s_t4.inc(last)
                        assert s_t4.n == T0 + TQ4 * 9 * NFC

                    @blk.scalar
                    def _(sc):
                        for e in range(9):
                            for fc in range(NFC):
                                bl = e * NFC + fc
                                b = B0 + bl
                                if fc == 0:
                                    sc.wait_ge(s_qx.sem, dmarec[("bu", half, e)])
                                for tc in range(NTC):
                                    for ft in range(NFT):
                                        sc.wait_ge(s_t4.sem, T0 + t4_h1_val(bl, tc, ft))
                                        s_s4.inc(sc.activation(
                                            a1s[:, ft * TC:(ft + 1) * TC],
                                            h1p[(tc * NFT + ft) % 2][:, :], AF.Gelu,
                                            bias=bu_sb[:, (e % 2) * 32 + fc * NFT + ft:
                                                       (e % 2) * 32 + fc * NFT + ft + 1]))
                        assert s_s4.n == S0 + SQ4 * 9 * NFC

                    @blk.vector
                    def _(ve):
                        ve.wait_ge(s_v4w.sem, v4w_done)
                        for e in range(9):
                            for fc in range(NFC):
                                bl = e * NFC + fc
                                b = B0 + bl
                                for tc in range(NTC):
                                    for tti in range(NTI):
                                        tt = tc * NTI + tti
                                        ve.wait_ge(s_t4.sem, T0 + t4_y_val(bl, tc, tti))
                                        gt = half * NTTH + tt
                                        scalar = (wgt[:, gt * NE + e:gt * NE + e + 1]
                                                  if e < NE else 0.1)
                                        yy = VQ4 * bl + tt
                                        if yy >= NTTH:
                                            ve.wait_ge(s_v4.sem, V0 + yy - NTTH + 1)
                                        s_v4.inc(ve.scalar_tensor_tensor(
                                            comb[:, tt * D:(tt + 1) * D],
                                            yp[tt % 2][:, :], scalar,
                                            comb[:, tt * D:(tt + 1) * D],
                                            op0=ALU.mult, op1=ALU.add))
                        assert s_v4.n == V0 + VQ4 * 9 * NFC

                if half == 0:
                    halfrec["v4done0"] = s_v4.n
                else:
                    with nc.Block() as blk:
                        v4done = s_v4.n

                        @blk.sync
                        def _(sy):
                            for tt in range(NTTH):
                                r0 = (half * NTTH + tt) * 128
                                sy.wait_ge(s_v4.sem, v4done - VQ4 + tt + 1)
                                dmarec[("out", half, tt)] = dma_ser(sy, s_qx, out_h[r0:r0 + 128, :],
                                    comb[:, tt * D:(tt + 1) * D])
                            sy.wait_ge(s_qx.sem, dmarec[("out", half, NTTH - 1)])
                            sy.wait_ge(s_gsp.sem, dmarec[("out", 0, NTTH - 1)])

    return nc


def _prep_inputs(L0_attn_x, L0_attn_Wq, L0_attn_Wk, L0_attn_Wv, L0_attn_Wo,
                 L0_attn_W_down_kv, L0_attn_W_up_k, L0_attn_W_up_v,
                 L0_ffn_gate_w, L0_ffn_gate_bias, L0_ffn_Wu, L0_ffn_bu,
                 L0_ffn_Wd, L0_ffn_bd, L0_ffn_sWu, L0_ffn_sbu, L0_ffn_sWd,
                 L0_ffn_sbd, S):
    c = np.ascontiguousarray
    f = lambda a: np.asarray(a, dtype=np.float32)
    shared = {
        "wq": c(f(L0_attn_Wq)),
        "wk": c(f(L0_attn_Wk)),
        "wv": c(f(L0_attn_Wv)),
        "wo": c(f(L0_attn_Wo)),
        "wdn": c(f(L0_attn_W_down_kv)),
        "wupk": c(f(L0_attn_W_up_k)),
        "wupv": c(f(L0_attn_W_up_v)),
        "gwt": c(f(L0_ffn_gate_w).T),
        "gbias": c(f(L0_ffn_gate_bias).reshape(1, NE)),
        "wut": c(f(L0_ffn_Wu).transpose(0, 2, 1)),
        "bu2": c(f(L0_ffn_bu).reshape(NE, FF // 128, 128).transpose(0, 2, 1)),
        "wdt": c(f(L0_ffn_Wd).transpose(0, 2, 1)),
        "bd": c(f(L0_ffn_bd)),
        "swut": c(f(L0_ffn_sWu).T),
        "sbu2": c(f(L0_ffn_sbu).reshape(FF // 128, 128).T),
        "swdt": c(f(L0_ffn_sWd).T),
        "sbd": c(f(L0_ffn_sbd).reshape(1, D)),
    }
    x = f(L0_attn_x)
    in_maps = []
    for b in range(x.shape[0]):
        m = dict(shared)
        m["x"] = c(x[b, :S, :])
        in_maps.append(m)
    return in_maps


def kernel(**inputs):
    S = 4096
    in_maps = _prep_inputs(S=S, **inputs)
    nc = build(S=S)
    res = run_bass_kernel_spmd(nc, in_maps, core_ids=list(range(len(in_maps))))
    return np.stack([r["out"] for r in res.results], axis=0)


# version 20
# speedup vs baseline: 1.1881x; 1.0112x over previous
import numpy as np
from contextlib import ExitStack

import concourse.bass as bass
import concourse.mybir as mybir
from concourse.bass_utils import run_bass_kernel_spmd

F32 = mybir.dt.float32
BF16 = mybir.dt.bfloat16
AF = mybir.ActivationFunctionType
ALU = mybir.AluOpType

D = 1024
NKC = D // 128
NH = 16
HD = 64
DL = 256
NE = 8
FF = 4096
FCB = 1024
NFC = FF // FCB
NFT = FCB // 128
QB = 512
EXPB = 4

LN_EPS = 1e-5
GATE_EPS = 1e-6
DECAY_SCALE = -(0.01 * 0.01)
ATT_SCALE_LNBIAS = float(np.log(1.0 / (np.sqrt(np.float64(1024.0)) + 1e-8)))
ATT_SCALE = float(1.0 / (np.sqrt(np.float64(1024.0)) + 1e-8))


class Chan:

    def __init__(self, stack, nc, name):
        self.sem = stack.enter_context(nc.semaphore(name))
        self.n = 0

    def inc(self, inst, k=1):
        inst.then_inc(self.sem, k)
        self.n += k
        return self.n


def nsplit(total, maxn=512):
    return [(o, min(maxn, total - o)) for o in range(0, total, maxn)]


def dma_ser(eng, chan, out, in_):
    v = chan.inc(eng.dma_start(out, in_), 16)
    eng.wait_ge(chan.sem, v)
    return v


def build(S=4096, debug=False):
    NT = S // 128
    NQC = S // QB
    KT = NT
    NKG = KT // EXPB
    TH = S // 2
    NTTH = TH // 128
    TC = 1024 if TH % 1024 == 0 else TH
    NTC = TH // TC
    NTI = TC // 128

    nc = bass.Bass("TRN2", target_bir_lowering=False, debug=False)

    x_h = nc.dram_tensor("x", [S, D], F32, kind="ExternalInput").ap()
    wq_h = nc.dram_tensor("wq", [D, D], F32, kind="ExternalInput").ap()
    wk_h = nc.dram_tensor("wk", [D, D], F32, kind="ExternalInput").ap()
    wv_h = nc.dram_tensor("wv", [D, D], F32, kind="ExternalInput").ap()
    wo_h = nc.dram_tensor("wo", [D, D], F32, kind="ExternalInput").ap()
    wdn_h = nc.dram_tensor("wdn", [D, DL], F32, kind="ExternalInput").ap()
    wupk_h = nc.dram_tensor("wupk", [DL, D], F32, kind="ExternalInput").ap()
    wupv_h = nc.dram_tensor("wupv", [DL, D], F32, kind="ExternalInput").ap()
    gwt_h = nc.dram_tensor("gwt", [D, NE], F32, kind="ExternalInput").ap()
    gbias_h = nc.dram_tensor("gbias", [1, NE], F32, kind="ExternalInput").ap()
    wut_h = nc.dram_tensor("wut", [NE, D, FF], F32, kind="ExternalInput").ap()
    bu2_h = nc.dram_tensor("bu2", [NE, 128, FF // 128], F32, kind="ExternalInput").ap()
    wdt_h = nc.dram_tensor("wdt", [NE, FF, D], F32, kind="ExternalInput").ap()
    bd_h = nc.dram_tensor("bd", [NE, D], F32, kind="ExternalInput").ap()
    swut_h = nc.dram_tensor("swut", [D, FF], F32, kind="ExternalInput").ap()
    sbu2_h = nc.dram_tensor("sbu2", [128, FF // 128], F32, kind="ExternalInput").ap()
    swdt_h = nc.dram_tensor("swdt", [FF, D], F32, kind="ExternalInput").ap()
    sbd_h = nc.dram_tensor("sbd", [1, D], F32, kind="ExternalInput").ap()
    out_h = nc.dram_tensor("out", [S, D], F32, kind="ExternalOutput").ap()

    scratch_kind = "ExternalOutput" if debug else "Internal"
    xnt_sp = nc.dram_tensor("xnt_sp", [128, NKC * S], BF16, kind=scratch_kind).ap()
    pv_sp = nc.dram_tensor("pv_sp", [128, NKC * S], BF16, kind=scratch_kind).ap()
    rr_dram = nc.dram_tensor("rr_dram", [2, 512], F32, kind="Internal").ap()
    attn_sp = nc.dram_tensor("attn_sp", [S, D], F32, kind=scratch_kind).ap()
    if debug:
        dbg_lat = nc.dram_tensor("dbg_lat", [128, 2 * S], BF16, kind="ExternalOutput").ap()
        dbg_q = nc.dram_tensor("dbg_q", [128, NKC * S], BF16, kind="ExternalOutput").ap()
        dbg_gsig = nc.dram_tensor("dbg_gsig", [128, NE * NT], F32, kind="ExternalOutput").ap()
        dbg_w = nc.dram_tensor("dbg_w", [128, NE * NT], F32, kind="ExternalOutput").ap()

    st = ExitStack()
    with st:
        def sbuf(name, shape, dtype, **kw):
            return st.enter_context(nc.sbuf_tensor(name, shape, dtype, **kw))

        ident = sbuf("ident", [128, 128], BF16)
        onesf = sbuf("onesf", [128, 128], BF16)
        consts = sbuf("consts", [128, 6], F32)
        mrv = sbuf("mrv", [128, 8], F32)
        gsig = sbuf("gsig", [128, NE * NT], F32)
        wgt = sbuf("wgt", [128, NE * NT], F32)
        wupk_b = sbuf("wupk_b", [128, 2 * D], BF16)
        wupv_b = sbuf("wupv_b", [128, 2 * D], BF16)
        gwt_b = sbuf("gwt_b", [128, NKC * NE], BF16)
        gbias_b = sbuf("gbias_b", [1, NE], BF16)
        bd_aug = sbuf("bd_aug", [32, D], BF16)
        wt_sb = sbuf("wt_sb", [32, 128], BF16)
        sbd_st = sbuf("sbd_st", [1, D], F32)
        sbd_bf = sbuf("sbd_bf", [1, D], BF16)

        s_gw = Chan(st, nc, "s_gw")
        s_qx = Chan(st, nc, "s_qx")
        s_sx = Chan(st, nc, "s_sx")
        s_bc = Chan(st, nc, "s_bc")
        s_gsp = Chan(st, nc, "s_gsp")
        s_gpc = Chan(st, nc, "s_gpc")
        s_v0 = Chan(st, nc, "s_v0")
        s_t1 = Chan(st, nc, "s_t1")
        s_s1 = Chan(st, nc, "s_s1")
        s_v1 = Chan(st, nc, "s_v1")
        s_t1b = Chan(st, nc, "s_t1b")
        s_v1b = Chan(st, nc, "s_v1b")
        s_t1c = Chan(st, nc, "s_t1c")
        s_s1c = Chan(st, nc, "s_s1c")
        s_v2s = Chan(st, nc, "s_v2s")
        s_s2s = Chan(st, nc, "s_s2s")
        s_t2p = Chan(st, nc, "s_t2p")
        s_v2p = Chan(st, nc, "s_v2p")
        s_t2m = Chan(st, nc, "s_t2m")
        s_s2m = Chan(st, nc, "s_s2m")
        s_v2m = Chan(st, nc, "s_v2m")
        s_t2mB = Chan(st, nc, "s_t2mB")
        s_s2mB = Chan(st, nc, "s_s2mB")
        s_v2mB = Chan(st, nc, "s_v2mB")
        s_t3 = Chan(st, nc, "s_t3")
        s_v3 = Chan(st, nc, "s_v3")
        s_t4w = Chan(st, nc, "s_t4w")
        s_v4w = Chan(st, nc, "s_v4w")
        s_t4 = Chan(st, nc, "s_t4")
        s_s4 = Chan(st, nc, "s_s4")
        s_v4 = Chan(st, nc, "s_v4")
        s_v5 = Chan(st, nc, "s_v5")

        dmarec = {}

        with nc.Block() as blk:
            @blk.gpsimd
            def _(gp):
                s_gpc.inc(gp.iota(ident[:, :], pattern=[[-1, 128]], base=0,
                                  channel_multiplier=1,
                                  allow_small_or_imprecise_dtypes=True))
                dmarec["gwt"] = dma_ser(gp, s_gw, gwt_b.ap().rearrange("p (kc e) -> p kc e", kc=NKC),
                    gwt_h.rearrange("(kc p) e -> p kc e", p=128))
                dmarec["gbias"] = dma_ser(gp, s_gw, gbias_b[:, :], gbias_h)
                dmarec["wupk"] = dma_ser(gp, s_gw, wupk_b.ap().rearrange("p (lc d) -> p lc d", lc=2),
                    wupk_h.rearrange("(lc p) d -> p lc d", p=128))
                dmarec["wupv"] = dma_ser(gp, s_gw, wupv_b.ap().rearrange("p (lc d) -> p lc d", lc=2),
                    wupv_h.rearrange("(lc p) d -> p lc d", p=128))
                dmarec["bd"] = dma_ser(gp, s_gw, bd_aug[0:8, :], bd_h)

            @blk.sync
            def _(sy):
                dmarec["sbd"] = dma_ser(sy, s_qx, sbd_st[:, :], sbd_h)
                sy.wait_ge(s_v0.sem, 5)
                dmarec["sbdrow"] = dma_ser(sy, s_qx, bd_aug[8:9, :], sbd_bf[:, :])

            @blk.vector
            def _(ve):
                ve.memset(consts[:, 0:1], LN_EPS)
                ve.memset(consts[:, 1:2], DECAY_SCALE)
                ve.memset(consts[:, 2:3], ATT_SCALE_LNBIAS)
                ve.memset(consts[:, 3:4], -1.0)
                ve.memset(consts[:, 4:5], -DECAY_SCALE)
                ve.memset(consts[:, 5:6], ATT_SCALE)
                ve.memset(onesf[:, :], 1.0)
                ve.memset(mrv[:, 2:8], -7.25e30)
                s_v0.inc(ve.memset(wt_sb[0:32, :], 1.0))
                ve.wait_ge(s_gpc.sem, 1)
                s_v0.inc(ve.tensor_single_scalar(ident[:, :], ident[:, :], 0.0,
                                                 op=ALU.is_equal))
                ve.wait_ge(s_qx.sem, dmarec["sbd"])
                s_v0.inc(ve.tensor_single_scalar(sbd_st[:, :], sbd_st[:, :], 1.0,
                                                 op=ALU.mult))
                ve.wait_ge(s_v0.sem, 3)
                s_v0.inc(ve.tensor_single_scalar(sbd_st[:, :], sbd_st[:, :], 0.1,
                                                 op=ALU.mult))
                ve.wait_ge(s_v0.sem, 4)
                s_v0.inc(ve.tensor_copy(sbd_bf[:, :], sbd_st[:, :]))

        with ExitStack() as stp1:
            latT = stp1.enter_context(nc.sbuf_tensor("latT", [128, 2 * S], BF16, side="left"))
            with ExitStack() as stp1q:
                xnT = stp1q.enter_context(nc.sbuf_tensor("xnT", [128, NKC * S], BF16, side="right"))
                wq_b = stp1q.enter_context(nc.sbuf_tensor("wq_b", [128, NKC * D], BF16, side="right"))
                with ExitStack() as stp1b:
                    wk_b = stp1b.enter_context(nc.sbuf_tensor("wk_b", [128, NKC * D], BF16, side="right"))
                    wv_b = stp1b.enter_context(nc.sbuf_tensor("wv_b", [128, NKC * D], BF16, side="right"))
                    wdn_b = stp1b.enter_context(nc.sbuf_tensor("wdn_b", [128, NKC * DL], BF16, side="right"))
                    kv_b = stp1b.enter_context(nc.sbuf_tensor("kv_b", [128, NKC * 512], BF16, side="right"))
                    with ExitStack() as stp1a:
                        sb1 = lambda n, sh, dt: stp1a.enter_context(nc.sbuf_tensor(n, sh, dt, side="right"))
                        xst = sb1("xst", [128, 2 * D], F32)
                        stats = sb1("stats", [128, 2 * 12], F32)
                        mv = sb1("mv", [128, 2 * 2], F32)
                        rstd = sb1("rstd", [128, 2], F32)
                        nmr = sb1("nmr", [128, 2], F32)
                        xnbf = sb1("xnbf", [128, 2 * D], BF16)
                        m8 = sb1("m8", [128, 8], F32)
                        killed = sb1("killed", [128, 8], F32)
                        gmask = sb1("gmask", [128, 8], F32)
                        gden = sb1("gden", [128, 1], F32)

                        VQ1, SQ1, TQ1 = 13, 11, 9
                        with ExitStack() as stps, nc.Block() as blk:
                            trp = [stps.enter_context(nc.psum_tensor(f"trp{i}", [128, 128], BF16)) for i in range(2)]
                            gps = stps.enter_context(nc.psum_tensor("gps", [128, NE], F32))
                            kvp = [stps.enter_context(nc.psum_tensor(f"kvp{i}", [128, 512], F32)) for i in range(2)]
                            latp = stps.enter_context(nc.psum_tensor("latp", [128, 512], F32))

                            @blk.gpsimd
                            def _(gp):
                                for nm, dst, src in (("wk", wk_b, wk_h), ("wv", wv_b, wv_h),
                                                     ("wq", wq_b, wq_h)):
                                    dmarec[nm] = dma_ser(gp, s_gw, dst.ap().rearrange("p (kc d) -> p kc d", kc=NKC),
                                        src.rearrange("(kc p) d -> p kc d", p=128))
                                dmarec["wdn"] = dma_ser(gp, s_gw, wdn_b.ap().rearrange("p (kc l) -> p kc l", kc=NKC),
                                    wdn_h.rearrange("(kc p) l -> p kc l", p=128))

                            @blk.sync
                            def _(sy):
                                for tt in range(0, NT, 2):
                                    if tt >= 2:
                                        sy.wait_ge(s_v1.sem, VQ1 * (tt - 2) + 1)
                                        sy.wait_ge(s_s1.sem, SQ1 * (tt - 2) + 2)
                                    dmarec[("xt", tt)] = dma_ser(sy, s_qx, xst[:, 0:D],
                                        x_h[tt * 128:(tt + 1) * 128, :])
                                sy.wait_ge(s_s1.sem, SQ1 * NT)
                                dmarec["xnt_spill"] = dma_ser(sy, s_qx, xnt_sp, xnT[:, :])

                            @blk.tensor
                            def _(te):
                                te.wait_ge(s_v0.sem, 2)
                                te.wait_ge(s_gw.sem, dmarec["gbias"])
                                for tt in range(NT):
                                    te.wait_ge(s_s1.sem, SQ1 * tt + 2)
                                    for i in range(8):
                                        gi = 8 * tt + i
                                        if gi >= 2:
                                            pj, pi = divmod(gi - 2, 8)
                                            te.wait_ge(s_s1.sem, SQ1 * pj + 3 + pi)
                                        s_t1.inc(te.transpose(
                                            trp[gi % 2][:, :],
                                            xnbf[:, (tt % 2) * D + i * 128:(tt % 2) * D + (i + 1) * 128],
                                            ident[:, :]))
                                    te.wait_ge(s_s1.sem, SQ1 * tt + 10)
                                    if tt >= 1:
                                        te.wait_ge(s_s1.sem, SQ1 * (tt - 1) + 11)
                                    for kc in range(NKC):
                                        te.matmul(gps[:, :],
                                                  xnT[:, kc * S + tt * 128:kc * S + (tt + 1) * 128],
                                                  gwt_b[:, kc * NE:(kc + 1) * NE],
                                                  start=(kc == 0), stop=False)
                                    s_t1.inc(te.matmul(gps[:, :], onesf[0:1, 0:128],
                                                       gbias_b[0:1, :], start=False, stop=True))
                                    if tt % 4 != 3:
                                        continue
                                    nn = tt // 4
                                    if nn == 0:
                                        te.wait_ge(s_gw.sem, dmarec["wdn"])
                                    for mt in range(8):
                                        gi = 10 * nn + mt
                                        if gi >= 2:
                                            te.wait_ge(s_v1b.sem, gi - 1)
                                        first = True
                                        last = None
                                        for wsrc in (wk_b, wv_b):
                                            for kc in range(NKC):
                                                last = te.matmul(
                                                    kvp[mt % 2][:, :],
                                                    wsrc[:, kc * D + mt * 128:kc * D + (mt + 1) * 128],
                                                    xnT[:, kc * S + nn * 512:kc * S + (nn + 1) * 512],
                                                    start=first,
                                                    stop=(wsrc is wv_b and kc == NKC - 1))
                                                first = False
                                        s_t1b.inc(last)
                                    te.wait_ge(s_v1b.sem, 10 * nn + 8)
                                    for lt in range(2):
                                        if nn + lt >= 1:
                                            te.wait_ge(s_v1b.sem, 10 * nn + 9 if lt == 1
                                                       else 10 * (nn - 1) + 10)
                                        last = None
                                        for kc in range(NKC):
                                            last = te.matmul(
                                                latp[:, :],
                                                wdn_b[:, kc * DL + lt * 128:kc * DL + (lt + 1) * 128],
                                                kv_b[:, kc * 512:(kc + 1) * 512],
                                                start=(kc == 0), stop=(kc == NKC - 1))
                                        s_t1b.inc(last)
                                assert s_t1.n == TQ1 * NT
                                assert s_t1b.n == 10 * NQC

                            @blk.scalar
                            def _(sc):
                                dmarec[("xt", 1)] = s_sx.inc(sc.dma_start(
                                    xst[:, D:2 * D], x_h[128:256, :]), 16)
                                for tt in range(NT):
                                    sc.wait_ge(s_v1.sem, VQ1 * tt + 2)
                                    s_s1.inc(sc.activation(rstd[:, tt % 2:tt % 2 + 1],
                                                           mv[:, (tt % 2) * 2 + 1:(tt % 2) * 2 + 2],
                                                           AF.Sqrt, bias=consts[:, 0:1]))
                                    sc.wait_ge(s_v1.sem, VQ1 * tt + 5)
                                    sc.wait_ge((s_qx if tt % 2 == 0 else s_sx).sem,
                                               dmarec[("xt", tt)])
                                    if tt >= 2:
                                        sc.wait_ge(s_t1.sem, TQ1 * (tt - 2) + 8)
                                    s_s1.inc(sc.activation(
                                        xnbf[:, (tt % 2) * D:(tt % 2 + 1) * D],
                                        xst[:, (tt % 2) * D:(tt % 2 + 1) * D],
                                        AF.Identity,
                                        bias=nmr[:, tt % 2:tt % 2 + 1],
                                        scale=rstd[:, tt % 2:tt % 2 + 1]))
                                    if tt % 2 == 1 and tt + 2 < NT:
                                        sc.wait_ge(s_s1.sem, SQ1 * tt + 2)
                                        dmarec[("xt", tt + 2)] = s_sx.inc(sc.dma_start(
                                            xst[:, D:2 * D],
                                            x_h[(tt + 2) * 128:(tt + 3) * 128, :]), 16)
                                    for i in range(8):
                                        sc.wait_ge(s_t1.sem, TQ1 * tt + i + 1)
                                        s_s1.inc(sc.activation(
                                            xnT[:, i * S + tt * 128:i * S + (tt + 1) * 128],
                                            trp[(8 * tt + i) % 2][:, :], AF.Identity))
                                    sc.wait_ge(s_t1.sem, TQ1 * tt + 9)
                                    s_s1.inc(sc.activation(gsig[:, tt * NE:(tt + 1) * NE],
                                                           gps[:, :], AF.Sigmoid))
                                assert s_s1.n == SQ1 * NT

                            @blk.vector
                            def _(ve):
                                ve.wait_ge(s_v0.sem, 5)
                                for tt in range(NT):
                                    b = tt % 2
                                    ve.wait_ge((s_qx if tt % 2 == 0 else s_sx).sem,
                                               dmarec[("xt", tt)])
                                    if tt >= 2:
                                        ve.wait_ge(s_v1.sem, VQ1 * (tt - 2) + 2)
                                    ve.bn_stats(stats[:, b * 12:b * 12 + 6],
                                                xst[:, b * D:b * D + 512])
                                    s_v1.inc(ve.bn_stats(stats[:, b * 12 + 6:b * 12 + 12],
                                                         xst[:, b * D + 512:b * D + 1024]))
                                    ve.wait_ge(s_v1.sem, VQ1 * tt + 1)
                                    if tt >= 2:
                                        ve.wait_ge(s_s1.sem, SQ1 * (tt - 2) + 1)
                                    s_v1.inc(ve.bn_aggr(mv[:, b * 2:b * 2 + 2],
                                                        stats[:, b * 12:b * 12 + 12]))
                                    ve.wait_ge(s_s1.sem, SQ1 * tt + 1)
                                    s_v1.inc(ve.reciprocal(rstd[:, b:b + 1], rstd[:, b:b + 1]))
                                    ve.wait_ge(s_v1.sem, VQ1 * tt + 3)
                                    s_v1.inc(ve.tensor_mul(nmr[:, b:b + 1], mv[:, b * 2:b * 2 + 1],
                                                           rstd[:, b:b + 1]))
                                    ve.wait_ge(s_v1.sem, VQ1 * tt + 4)
                                    s_v1.inc(ve.tensor_single_scalar(nmr[:, b:b + 1], nmr[:, b:b + 1],
                                                                     -1.0, op=ALU.mult))
                                    gs = gsig[:, tt * NE:(tt + 1) * NE]
                                    ve.wait_ge(s_s1.sem, SQ1 * tt + 11)
                                    s_v1.inc(ve.max(m8[:, :], gs))
                                    ve.wait_ge(s_v1.sem, VQ1 * tt + 6)
                                    if tt >= 1:
                                        ve.wait_ge(s_v1.sem, VQ1 * (tt - 1) + 8)
                                    s_v1.inc(ve.tensor_copy(mrv[:, 0:2], m8[:, 0:2]))
                                    ve.wait_ge(s_v1.sem, VQ1 * tt + 7)
                                    s_v1.inc(ve.match_replace(killed[:, :], mrv[:, :], gs, -1e30))
                                    ve.wait_ge(s_v1.sem, VQ1 * tt + 8)
                                    s_v1.inc(ve.tensor_tensor(gmask[:, :], gs, killed[:, :],
                                                              op=ALU.is_gt))
                                    s_v1.inc(ve.tensor_add(gden[:, :], m8[:, 0:1], m8[:, 1:2]))
                                    ve.wait_ge(s_v1.sem, VQ1 * tt + 10)
                                    s_v1.inc(ve.tensor_single_scalar(gden[:, :], gden[:, :],
                                                                     GATE_EPS, op=ALU.add))
                                    ve.wait_ge(s_v1.sem, VQ1 * tt + 11)
                                    s_v1.inc(ve.reciprocal(gden[:, :], gden[:, :]))
                                    ve.wait_ge(s_v1.sem, VQ1 * tt + 12)
                                    s_v1.inc(ve.scalar_tensor_tensor(
                                        wgt[:, tt * NE:(tt + 1) * NE], gs, gden[:, :],
                                        gmask[:, :], op0=ALU.mult, op1=ALU.mult))
                                    if tt % 4 == 3:
                                        nn = tt // 4
                                        for mt in range(8):
                                            ve.wait_ge(s_t1b.sem, 10 * nn + mt + 1)
                                            if mt == 0 and nn >= 1:
                                                ve.wait_ge(s_t1b.sem, 10 * (nn - 1) + 10)
                                            s_v1b.inc(ve.tensor_copy(kv_b[:, mt * 512:(mt + 1) * 512],
                                                                     kvp[mt % 2][:, :]))
                                        for lt in range(2):
                                            ve.wait_ge(s_t1b.sem, 10 * nn + 9 + lt)
                                            s_v1b.inc(ve.tensor_copy(
                                                latT[:, lt * S + nn * 512:lt * S + (nn + 1) * 512],
                                                latp[:, :]))
                                assert s_v1.n == VQ1 * NT
                                assert s_v1b.n == 10 * NQC

                qT = stp1.enter_context(nc.sbuf_tensor("qT", [128, NKC * S], BF16, side="left"))
                with ExitStack() as stps, nc.Block() as blk:
                    qp = [stps.enter_context(nc.psum_tensor(f"qp{i}", [128, 512], F32)) for i in range(2)]

                    @blk.tensor
                    def _(te):
                        te.wait_ge(s_gw.sem, dmarec["wq"])
                        for mt in range(8):
                            for nn in range(NQC):
                                gi = mt * NQC + nn
                                if gi >= 2:
                                    te.wait_ge(s_s1c.sem, gi - 1)
                                last = None
                                for kc in range(NKC):
                                    last = te.matmul(
                                        qp[gi % 2][:, :],
                                        wq_b[:, kc * D + mt * 128:kc * D + (mt + 1) * 128],
                                        xnT[:, kc * S + nn * 512:kc * S + (nn + 1) * 512],
                                        start=(kc == 0), stop=(kc == NKC - 1))
                                s_t1c.inc(last)
                        assert s_t1c.n == 8 * NQC

                    @blk.scalar
                    def _(sc):
                        for mt in range(8):
                            for nn in range(NQC):
                                gi = mt * NQC + nn
                                sc.wait_ge(s_t1c.sem, gi + 1)
                                s_s1c.inc(sc.activation(
                                    qT[:, mt * S + nn * 512:mt * S + (nn + 1) * 512],
                                    qp[gi % 2][:, :], AF.Identity))
                        assert s_s1c.n == 8 * NQC

                if debug:
                    with nc.Block() as blk:
                        @blk.sync
                        def _(sy):
                            dma_ser(sy, s_qx, dbg_lat, latT[:, :])
                            dma_ser(sy, s_qx, dbg_q, qT[:, :])
                            dma_ser(sy, s_qx, dbg_gsig, gsig[:, :])
                            dma_ser(sy, s_qx, dbg_w, wgt[:, :])
                            sy.wait_ge(s_qx.sem, s_qx.n)

            with ExitStack() as stp2:
                sb2 = lambda n, sh, dt: stp2.enter_context(nc.sbuf_tensor(n, sh, dt, side="right"))
                rowP = sb2("rowP", [128, S], BF16)
                rowM = sb2("rowM", [128, S], BF16)
                decayR = sb2("decayR", [128, 1024], BF16)
                s2s_base = s_s2s.n
                v2s_base = s_v2s.n
                with ExitStack() as stp2s:
                    sb2t = lambda n, sh, dt: stp2s.enter_context(nc.sbuf_tensor(n, sh, dt, side="right"))
                    dmraw = sb2t("dmraw", [128, 1024], F32)
                    dmrawB = sb2t("dmrawB", [128, 1024], F32)
                    rowraw = sb2t("rowraw", [128, S], F32)
                    with nc.Block() as blk:
                        @blk.gpsimd
                        def _(gp):
                            s_gpc.inc(gp.iota(dmraw[:, :], pattern=[[-1, 1024]], base=384,
                                              channel_multiplier=1,
                                              allow_small_or_imprecise_dtypes=True))
                            s_gpc.inc(gp.iota(rowraw[:, :], pattern=[[-1, S]], base=0,
                                              channel_multiplier=0,
                                              allow_small_or_imprecise_dtypes=True))

                        @blk.scalar
                        def _(sc):
                            sc.wait_ge(s_gpc.sem, s_gpc.n)
                            s_s2s.inc(sc.activation(dmrawB[:, :], dmraw[:, :], AF.Abs))
                            s_s2s.inc(sc.activation(rowP[:, :], rowraw[:, :], AF.Exp,
                                                    scale=consts[:, 1:2]))
                            s_s2s.inc(sc.activation(rowM[:, :], rowraw[:, :], AF.Exp,
                                                    scale=consts[:, 4:5]))
                            sc.wait_ge(s_v2s.sem, v2s_base + 1)
                            s_s2s.inc(sc.activation(decayR[:, :], dmrawB[:, :], AF.Exp,
                                                    scale=consts[:, 1:2]))

                        @blk.vector
                        def _(ve):
                            ve.wait_ge(s_gpc.sem, s_gpc.n)
                            ve.wait_ge(s_s2s.sem, s2s_base + 1)
                            s_v2s.inc(ve.tensor_add(dmrawB[:, :], dmrawB[:, :], dmraw[:, :]))

                krecTP = sb2("krecTP", [128, 2 * S], BF16)
                krecTM = sb2("krecTM", [128, 2 * S], BF16)
                vrec = sb2("vrec", [128, NT * 260], BF16)
                expT = sb2("expT", [128, 6 * 1024], BF16)
                qstg = [sb2(f"qstg{i}", [128, 1024], BF16) for i in range(2)]
                pvT = sb2("pvT", [128, 2 * S], BF16)
                pvst = [sb2(f"pvst{i}", [64, 512], BF16) for i in range(2)]
                bcsb = [sb2(f"bcsb{i}", [64, 512], F32) for i in range(2)]
                rr_sb = sb2("rr_sb", [128, 512], F32)

                NHK = KT // 2
                JPG = 4 * NQC
                NJOBS = 4 * JPG
                SQ2 = NHK + 2
                VB = 7 * JPG + 1
                trec = {}
                oddrec = {}
                ojn_s = [0]
                ojn_v = [0]

                def vseg(J):
                    gj, pj = divmod(J, JPG)
                    return gj * VB + 7 * pj

                def evac_done(J):
                    gj, pj = divmod(J, JPG)
                    if pj == JPG - 1:
                        return (gj + 1) * VB
                    return gj * VB + 7 * (pj + 2)

                LAG = min(4, NHK)
                LSL = min(4, NHK)
                SB = SQ2 * JPG + 2

                def exp_done(J, hk):
                    gj, pj = divmod(J, JPG)
                    return gj * SB + SQ2 * pj + hk + 1 + (2 if hk >= LSL else 0)

                def rr_done(J):
                    gj, pj = divmod(J, JPG)
                    if pj == JPG - 1:
                        return (gj + 1) * SB
                    return gj * SB + SQ2 * (pj + 1) + LSL + 2

                stps2 = stp2.enter_context(ExitStack())
                scp = [stps2.enter_context(nc.psum_tensor(f"scp{i}", [128, 1024], F32)) for i in range(3)]
                pvp = [stps2.enter_context(nc.psum_tensor(f"pvp{i}", [128, 512], F32)) for i in range(2)]

                for g in range(4):
                    with nc.Block() as blk:
                        krp = [scp[0][:, i * 512:(i + 1) * 512] for i in range(2)]
                        vrp = [scp[1][:, i * 512:i * 512 + 256] for i in range(2)]
                        T0p, V0p = s_t2p.n, s_v2p.n
                        NKR = 2 * NQC

                        @blk.tensor
                        def _(te):
                            te.wait_ge(s_gw.sem, dmarec["wupv"])
                            if g == 0:
                                te.wait_ge(s_v1b.sem, 10 * NQC)
                            else:
                                te.wait_ge(s_s2m.sem, SB * g)
                                te.wait_ge(s_v2m.sem, VB * g)
                            for mt in range(2):
                                for nn in range(NQC):
                                    gi = mt * NQC + nn
                                    if gi >= 2:
                                        te.wait_ge(s_v2p.sem, V0p + 1 + 2 * (gi - 1))
                                    last = None
                                    for lc in range(2):
                                        last = te.matmul(
                                            krp[gi % 2],
                                            wupk_b[:, lc * D + g * 256 + mt * 128:lc * D + g * 256 + (mt + 1) * 128],
                                            latT[:, lc * S + nn * 512:lc * S + (nn + 1) * 512],
                                            start=(lc == 0), stop=(lc == 1))
                                    s_t2p.inc(last)
                            for tt in range(NT):
                                if tt >= 2:
                                    te.wait_ge(s_v2p.sem, V0p + 1 + 2 * NKR + tt - 1)
                                last = None
                                for lc in range(2):
                                    last = te.matmul(
                                        vrp[tt % 2],
                                        latT[:, lc * S + tt * 128:lc * S + (tt + 1) * 128],
                                        wupv_b[:, lc * D + g * 256:lc * D + (g + 1) * 256],
                                        start=(lc == 0), stop=(lc == 1))
                                s_t2p.inc(last)
                            assert s_t2p.n == T0p + NKR + NT

                        @blk.vector
                        def _(ve):
                            if g == 0:
                                ve.wait_ge(s_s2s.sem, s2s_base + 3)
                            else:
                                ve.wait_ge(s_t2m.sem, trec[(JPG * g - 1, "pv", NHK - 1)])
                            s_v2p.inc(ve.memset(vrec[:, :], 1.0))
                            for mt in range(2):
                                for nn in range(NQC):
                                    gi = mt * NQC + nn
                                    ve.wait_ge(s_t2p.sem, T0p + gi + 1)
                                    s_v2p.inc(ve.tensor_mul(
                                        krecTP[:, mt * S + nn * 512:mt * S + (nn + 1) * 512],
                                        krp[gi % 2],
                                        rowP[:, nn * 512:(nn + 1) * 512]))
                                    s_v2p.inc(ve.tensor_mul(
                                        krecTM[:, mt * S + nn * 512:mt * S + (nn + 1) * 512],
                                        krp[gi % 2],
                                        rowM[:, nn * 512:(nn + 1) * 512]))
                            ve.wait_ge(s_v2p.sem, V0p + 1)
                            for tt in range(NT):
                                ve.wait_ge(s_t2p.sem, T0p + NKR + tt + 1)
                                last = None
                                for lam in range(4):
                                    last = ve.tensor_copy(
                                        vrec[:, tt * 260 + lam * 65:tt * 260 + lam * 65 + 64],
                                        vrp[tt % 2][:, lam * 64:(lam + 1) * 64])
                                s_v2p.inc(last)
                            assert s_v2p.n == V0p + 1 + 2 * NKR + NT

                    with nc.Block() as blk:
                        v2p_done = s_v2p.n
                        s1c_done = s_s1c.n

                        def jinfo(pj):
                            J = JPG * g + pj
                            lam, qc = divmod(pj, NQC)
                            return J, lam, qc

                        @blk.tensor
                        def _(te):
                            te.wait_ge(s_v2p.sem, v2p_done)
                            te.wait_ge(s_s1c.sem, s1c_done)

                            def emit_sc(pj, hk):
                                J, lam, qc = jinfo(pj)
                                pbase = 64 * (lam % 2)
                                ktile = lam // 2
                                lh = NHK * J + hk
                                u = lh % 3
                                for half in range(2):
                                    kt = 2 * hk + half
                                    if kt < 4 * qc + 4:
                                        ksrc, qcol = krecTP, 0
                                    else:
                                        ksrc, qcol = krecTM, 512
                                    trec[(J, "sc", kt)] = s_t2m.inc(te.matmul(
                                        scp[u][:, half * 512:(half + 1) * 512],
                                        ksrc[pbase:pbase + 64, ktile * S + kt * 128:ktile * S + (kt + 1) * 128],
                                        qstg[J % 2][pbase:pbase + 64, qcol:qcol + 512],
                                        start=True, stop=True))

                            def emit_pv(pvJ, hk):
                                pj2 = pvJ - JPG * g
                                lam, qc = divmod(pj2, NQC)
                                lh = NHK * pvJ + hk
                                u6 = lh % 6
                                c = None
                                for half in range(2):
                                    kt = 2 * hk + half
                                    mv = expT[:, u6 * 1024 + half * 512:u6 * 1024 + (half + 1) * 512]
                                    c = s_t2m.inc(te.matmul(
                                        pvp[pvJ % 2][0:65, :],
                                        vrec[:, kt * 260 + lam * 65:kt * 260 + lam * 65 + 65],
                                        mv,
                                        start=(kt == 0), stop=(kt == KT - 1)))
                                trec[(pvJ, "pv", hk)] = c

                            pending = []
                            for pj in range(JPG):
                                J, lam, qc = jinfo(pj)
                                for b in range(NHK // 2):
                                    hk0 = 2 * b
                                    pops = []
                                    pending.append((J, hk0))
                                    pending.append((J, hk0 + 1))
                                    while len(pending) > LAG:
                                        pops.append(pending.pop(0))
                                    scw = 0
                                    lh1 = NHK * J + hk0 + 1
                                    if lh1 >= 3:
                                        Jp, hkp = divmod(lh1 - 3, NHK)
                                        scw = exp_done(Jp, hkp)
                                    for (Jv, hkv) in pops:
                                        scw = max(scw, exp_done(Jv, hkv))
                                    if scw:
                                        te.wait_ge(s_s2m.sem, scw)
                                    if hk0 == 0:
                                        te.wait_ge(s_v2m.sem, vseg(J) + 2)
                                    for (Jv, hkv) in pops:
                                        if hkv == 0 and Jv >= 2:
                                            te.wait_ge(s_v2m.sem, evac_done(Jv - 2))
                                    emit_sc(pj, hk0)
                                    emit_sc(pj, hk0 + 1)
                                    for (Jv, hkv) in pops:
                                        emit_pv(Jv, hkv)
                            if g == 3 or True:
                                scw = 0
                                for (Jv, hkv) in pending:
                                    scw = max(scw, exp_done(Jv, hkv))
                                te.wait_ge(s_s2m.sem, scw)
                                for (Jv, hkv) in pending:
                                    if hkv == 0 and Jv >= 2:
                                        te.wait_ge(s_v2m.sem, evac_done(Jv - 2))
                                    emit_pv(Jv, hkv)
                                pending = []

                        @blk.sync
                        def _(sy):
                            for pj in range(JPG):
                                J, lam, qc = jinfo(pj)
                                sy.wait_ge(s_s2m.sem, rr_done(J))
                                dmarec[("rrst", J)] = dma_ser(sy, s_bc,
                                    rr_dram[J % 2:J % 2 + 1, :], rr_sb[64:65, :])
                                if J >= 2:
                                    sy.wait_ge(s_v2m.sem, evac_done(J - 2))
                                dmarec[("bc", J)] = dma_ser(sy, s_bc,
                                    bcsb[J % 2][0:64, :],
                                    bass.AP(rr_dram.tensor, (J % 2) * 512, [[0, 64], [1, 512]]))
                                if lam % 2 == 1:
                                    if g > 0 and pj == NQC:
                                        sy.wait_ge(s_gsp.sem, dmarec[("pvspill", g - 1)])
                                    sy.wait_ge(s_v2m.sem, evac_done(J))
                                    oj = ojn_s[0]
                                    oddrec[oj] = dma_ser(sy, s_qx,
                                        pvT[64:128, (lam // 2) * S + qc * 512:(lam // 2) * S + (qc + 1) * 512],
                                        pvst[oj % 2][0:64, :])
                                    ojn_s[0] = oj + 1

                        @blk.gpsimd
                        def _(gp):
                            gp.wait_ge(s_v2m.sem, (g + 1) * VB)
                            gp.wait_ge(s_qx.sem, s_qx.n)
                            dmarec[("pvspill", g)] = dma_ser(gp, s_gsp,
                                pv_sp[:, (2 * g) * S:(2 * g + 2) * S], pvT[:, :])

                        @blk.scalar
                        def _(sc):
                            def emit_exp(pj, hk):
                                J, lam, qc = jinfo(pj)
                                lh = NHK * J + hk
                                u = lh % 3
                                u6 = lh % 6
                                if hk in (2 * qc, 2 * qc + 1):
                                    b = hk - 2 * qc
                                    sc.wait_ge(s_v2m.sem, vseg(J) + 4 + 2 * b)
                                    s_s2m.inc(sc.activation(expT[:, u6 * 1024:(u6 + 1) * 1024],
                                                            expT[:, u6 * 1024:(u6 + 1) * 1024],
                                                            AF.Exp, scale=consts[:, 5:6]))
                                else:
                                    sc.wait_ge(s_t2m.sem, trec[(J, "sc", 2 * hk + 1)])
                                    if lh >= 6:
                                        Jp, hkp = divmod(lh - 6, NHK)
                                        sc.wait_ge(s_t2m.sem, trec[(Jp, "pv", hkp)])
                                    s_s2m.inc(sc.activation(expT[:, u6 * 1024:(u6 + 1) * 1024],
                                                            scp[u][:, :], AF.Exp,
                                                            scale=consts[:, 5:6]))

                            def emit_lnrr(Jm):
                                sc.wait_ge(s_t2m.sem, trec[(Jm, "pv", NHK - 1)])
                                sc.wait_ge(s_s2m.sem, s_s2m.n)
                                if Jm >= 1:
                                    sc.wait_ge(s_bc.sem, dmarec[("rrst", Jm - 1)])
                                s_s2m.inc(sc.activation(rr_sb[64:65, :], pvp[Jm % 2][64:65, :],
                                                        AF.Ln))
                                sc.wait_ge(s_s2m.sem, s_s2m.n)
                                s_s2m.inc(sc.activation(rr_sb[64:65, :], rr_sb[64:65, :],
                                                        AF.Exp, scale=consts[64:65, 3:4]))

                            for pj in range(JPG):
                                J, lam, qc = jinfo(pj)
                                for hk in range(min(LSL, NHK)):
                                    emit_exp(pj, hk)
                                if pj >= 1:
                                    emit_lnrr(J - 1)
                                else:
                                    s_s2m.inc(sc.activation(rr_sb[0:1, 1:2], consts[0:1, 0:1],
                                                            AF.Identity))
                                    s_s2m.inc(sc.activation(rr_sb[0:1, 2:3], consts[0:1, 0:1],
                                                            AF.Identity))
                                for hk in range(min(LSL, NHK), NHK):
                                    emit_exp(pj, hk)
                            emit_lnrr(JPG * (g + 1) - 1)
                            assert s_s2m.n == SB * (g + 1)

                        @blk.vector
                        def _(ve):
                            if g == 0:
                                ve.wait_ge(s_s2s.sem, s2s_base + 4)

                            def emit_evac(Jm):
                                lamm, qcm = divmod(Jm % JPG, NQC)
                                ve.wait_ge(s_t2m.sem, trec[(Jm, "pv", NHK - 1)])
                                ve.wait_ge(s_bc.sem, dmarec[("bc", Jm)])
                                if lamm % 2 == 1:
                                    oj = ojn_v[0]
                                    if oj >= 2:
                                        ve.wait_ge(s_qx.sem, oddrec[oj - 2])
                                    dst = pvst[oj % 2][0:64, :]
                                    ojn_v[0] = oj + 1
                                else:
                                    if Jm % JPG == 0 and Jm >= JPG:
                                        ve.wait_ge(s_gsp.sem, dmarec[("pvspill", Jm // JPG - 1)])
                                    dst = pvT[0:64, (lamm // 2) * S + qcm * 512:(lamm // 2) * S + (qcm + 1) * 512]
                                s_v2m.inc(ve.tensor_mul(dst, pvp[Jm % 2][0:64, :],
                                                        bcsb[Jm % 2][:, :]))

                            for pj in range(JPG):
                                J, lam, qc = jinfo(pj)
                                pbase = 64 * (lam % 2)
                                qtile = (g * 4 + lam) // 2
                                if J >= 2:
                                    ve.wait_ge(s_t2m.sem, trec[(J - 2, "sc", KT - 1)])
                                qsl = qT[pbase:pbase + 64, qtile * S + qc * 512:qtile * S + (qc + 1) * 512]
                                s_v2m.inc(ve.tensor_mul(qstg[J % 2][pbase:pbase + 64, 0:512],
                                                        qsl, rowM[pbase:pbase + 64, qc * 512:(qc + 1) * 512]))
                                s_v2m.inc(ve.tensor_mul(qstg[J % 2][pbase:pbase + 64, 512:1024],
                                                        qsl, rowP[pbase:pbase + 64, qc * 512:(qc + 1) * 512]))
                                for i0 in range(4):
                                    kt = 4 * qc + i0
                                    hk = 2 * qc + i0 // 2
                                    lh = NHK * J + hk
                                    u = lh % 3
                                    u6 = lh % 6
                                    ve.wait_ge(s_t2m.sem, trec[(J, "sc", kt)])
                                    if lh >= 6:
                                        Jp, hkp = divmod(lh - 6, NHK)
                                        ve.wait_ge(s_t2m.sem, trec[(Jp, "pv", hkp)])
                                    s_v2m.inc(ve.tensor_mul(
                                        expT[:, u6 * 1024 + (kt % 2) * 512:u6 * 1024 + (kt % 2) * 512 + 512],
                                        scp[u][:, (kt % 2) * 512:(kt % 2 + 1) * 512],
                                        decayR[:, 384 - 128 * i0:384 - 128 * i0 + 512]))
                                if pj >= 1:
                                    emit_evac(J - 1)
                                else:
                                    s_v2m.inc(ve.memset(rr_sb[0:1, 0:1], 0.0))
                            emit_evac(JPG * (g + 1) - 1)
                            assert s_v2m.n == VB * (g + 1)

        with ExitStack() as stp3:
            sb3 = lambda n, sh, dt: stp3.enter_context(nc.sbuf_tensor(n, sh, dt, side="right"))
            pvF = sb3("pvF", [128, NKC * S], BF16)
            wo_b = sb3("wo_b", [128, NKC * D], BF16)
            ast = sb3("ast", [128, 2 * D], F32)
            with ExitStack() as stps, nc.Block() as blk:
                wop = [stps.enter_context(nc.psum_tensor(f"wop{i}", [128, 512], F32)) for i in range(2)]

                @blk.gpsimd
                def _(gp):
                    dmarec["wo"] = dma_ser(gp, s_gw, wo_b.ap().rearrange("p (kc d) -> p kc d", kc=NKC),
                        wo_h.rearrange("(kc p) d -> p kc d", p=128))
                    for tt in range(1, NT, 2):
                        if tt == 1:
                            gp.wait_ge(s_gsp.sem, dmarec[("pvspill", 3)])
                        dmarec[("pvl", tt)] = dma_ser(gp, s_gsp,
                            pvF.ap().rearrange("p (kc t) -> p kc t", kc=NKC)[:, :, tt * 128:(tt + 1) * 128],
                            pv_sp.rearrange("p (kc t) -> p kc t", kc=NKC)[:, :, tt * 128:(tt + 1) * 128])

                @blk.sync
                def _(sy):
                    for tt in range(0, NT, 2):
                        if tt == 0:
                            sy.wait_ge(s_gsp.sem, dmarec[("pvspill", 3)])
                        dmarec[("pvl", tt)] = dma_ser(sy, s_qx,
                            pvF.ap().rearrange("p (kc t) -> p kc t", kc=NKC)[:, :, tt * 128:(tt + 1) * 128],
                            pv_sp.rearrange("p (kc t) -> p kc t", kc=NKC)[:, :, tt * 128:(tt + 1) * 128])
                    for tt in range(NT):
                        sy.wait_ge(s_v3.sem, 2 * tt + 2)
                        dmarec[("attn", tt)] = dma_ser(sy, s_qx, attn_sp[tt * 128:(tt + 1) * 128, :],
                            ast[:, (tt % 2) * D:(tt % 2 + 1) * D])

                @blk.tensor
                def _(te):
                    te.wait_ge(s_gw.sem, dmarec["wo"])
                    for tt in range(NT):
                        te.wait_ge(s_qx.sem if tt % 2 == 0 else s_gsp.sem, dmarec[("pvl", tt)])
                        for nn in range(2):
                            gi = 2 * tt + nn
                            if gi >= 2:
                                te.wait_ge(s_v3.sem, gi - 1)
                            last = None
                            for kc in range(NKC):
                                last = te.matmul(
                                    wop[gi % 2][:, :],
                                    pvF[:, kc * S + tt * 128:kc * S + (tt + 1) * 128],
                                    wo_b[:, kc * D + nn * 512:kc * D + (nn + 1) * 512],
                                    start=(kc == 0), stop=(kc == NKC - 1))
                            s_t3.inc(last)
                    assert s_t3.n == 2 * NT

                @blk.vector
                def _(ve):
                    for tt in range(NT):
                        for nn in range(2):
                            gi = 2 * tt + nn
                            ve.wait_ge(s_t3.sem, gi + 1)
                            if tt >= 2 and nn == 0:
                                ve.wait_ge(s_qx.sem, dmarec[("attn", tt - 2)])
                            s_v3.inc(ve.tensor_copy(
                                ast[:, (tt % 2) * D + nn * 512:(tt % 2) * D + (nn + 1) * 512],
                                wop[gi % 2][:, :]))
                    assert s_v3.n == 2 * NT

        with ExitStack() as stp4:
            sb4 = lambda n, sh, dt: stp4.enter_context(nc.sbuf_tensor(n, sh, dt, side="right"))
            xnTh = sb4("xnTh", [128, NKC * TH], BF16)
            comb = sb4("comb", [128, NTTH * D], F32)
            wu_sb = sb4("wu_sb", [128, NKC * FCB], BF16)
            wd_sb = sb4("wd_sb", [128, NFT * D], BF16)
            a1s = sb4("a1s", [128, NFT * TC], BF16)
            bu_sb = sb4("bu_sb", [128, 2 * (FF // 128)], F32)
            wbf = sb4("wbf", [128, 8], BF16)
            xst5 = sb4("xst5", [128, 2 * D], F32)
            att5 = sb4("att5", [128, 2 * D], F32)

            halfrec = {}
            TQ4 = NTC * (NFT + NTI)
            SQ4 = NTC * NFT
            VQ4 = NTTH
            H1LAST = (NTC - 1) * (NFT + NTI) + NFT

            def t4_h1_val(bl, tc, ft):
                return bl * TQ4 + tc * (NFT + NTI) + ft + 1

            def t4_y_val(bl, tc, tti):
                return bl * TQ4 + tc * (NFT + NTI) + NFT + tti + 1

            for half in range(2):
                with ExitStack() as stps, nc.Block() as blk:
                    trwp = stps.enter_context(nc.psum_tensor(f"trw{half}", [8, 128], BF16))
                    wbdp = [stps.enter_context(nc.psum_tensor(f"wbd{half}{i}", [128, D], F32)) for i in range(2)]
                    T0w, V0w = s_t4w.n, s_v4w.n

                    @blk.sync
                    def _(sy):
                        for tt in range(NTTH):
                            r0 = (half * NTTH + tt) * 128
                            if tt >= 2:
                                sy.wait_ge(s_v4w.sem, V0w + 5 * (tt - 2) + 5)
                            dmarec[("x5", half, tt)] = dma_ser(sy, s_qx, xst5[:, (tt % 2) * D:(tt % 2 + 1) * D],
                                x_h[r0:r0 + 128, :])

                    @blk.scalar
                    def _(sc):
                        for tt in range(NTTH):
                            r0 = (half * NTTH + tt) * 128
                            if tt >= 2:
                                sc.wait_ge(s_v4w.sem, V0w + 5 * (tt - 2) + 5)
                            dmarec[("a5", half, tt)] = dma_ser(sc, s_sx, att5[:, (tt % 2) * D:(tt % 2 + 1) * D],
                                attn_sp[r0:r0 + 128, :])

                    @blk.gpsimd
                    def _(gp):
                        if half == 1:
                            v4d = halfrec["v4done0"]
                            for tt in range(NTTH):
                                r0 = tt * 128
                                gp.wait_ge(s_v4.sem, v4d - VQ4 + tt + 1)
                                dmarec[("out", 0, tt)] = dma_ser(gp, s_gsp, out_h[r0:r0 + 128, :],
                                    comb[:, tt * D:(tt + 1) * D])
                        gp.wait_ge(s_qx.sem, dmarec["xnt_spill"])
                        dmarec[("xnth", half)] = dma_ser(gp, s_gsp, xnTh.ap().rearrange("p (kc t) -> p kc t", kc=NKC),
                            xnt_sp.rearrange("p (kc t) -> p kc t", kc=NKC)[:, :, half * TH:(half + 1) * TH])

                    @blk.tensor
                    def _(te):
                        te.wait_ge(s_gw.sem, dmarec["bd"])
                        te.wait_ge(s_qx.sem, dmarec["sbdrow"])
                        for tt in range(NTTH):
                            te.wait_ge(s_v4w.sem, V0w + 5 * tt + 1)
                            if tt >= 1:
                                te.wait_ge(s_v4w.sem, V0w + 5 * (tt - 1) + 2)
                            s_t4w.inc(te.transpose(trwp[:, :], wbf[:, :], ident[:, :]))
                            te.wait_ge(s_v4w.sem, V0w + 5 * tt + 2)
                            if tt >= 2:
                                te.wait_ge(s_v4w.sem, V0w + 5 * (tt - 2) + 3)
                            last = None
                            for nn in range(2):
                                last = te.matmul(wbdp[tt % 2][:, nn * 512:(nn + 1) * 512],
                                                 wt_sb[0:9, :],
                                                 bd_aug[0:9, nn * 512:(nn + 1) * 512],
                                                 start=True, stop=True)
                            s_t4w.inc(last)
                        assert s_t4w.n == T0w + 2 * NTTH

                    @blk.vector
                    def _(ve):
                        if half == 1:
                            ve.wait_ge(s_qx.sem, dmarec[("out", 0, NTTH - 1)])
                        ve.wait_ge(s_v0.sem, 1)
                        for tt in range(NTTH):
                            gt = half * NTTH + tt
                            if tt >= 1:
                                ve.wait_ge(s_t4w.sem, T0w + 2 * (tt - 1) + 1)
                            s_v4w.inc(ve.tensor_copy(wbf[:, :],
                                                     wgt[:, gt * NE:(gt + 1) * NE]))
                            ve.wait_ge(s_t4w.sem, T0w + 2 * tt + 1)
                            s_v4w.inc(ve.tensor_copy(wt_sb[0:8, :], trwp[:, :]))
                            ve.wait_ge(s_t4w.sem, T0w + 2 * tt + 2)
                            if half == 1:
                                ve.wait_ge(s_gsp.sem, dmarec[("out", 0, tt)])
                            s_v4w.inc(ve.tensor_copy(comb[:, tt * D:(tt + 1) * D],
                                                     wbdp[tt % 2][:, :]))
                            ve.wait_ge(s_qx.sem, dmarec[("x5", half, tt)])
                            ve.wait_ge(s_sx.sem, dmarec[("a5", half, tt)])
                            ve.wait_ge(s_v4w.sem, V0w + 5 * tt + 3)
                            s_v4w.inc(ve.tensor_add(comb[:, tt * D:(tt + 1) * D],
                                                    comb[:, tt * D:(tt + 1) * D],
                                                    xst5[:, (tt % 2) * D:(tt % 2 + 1) * D]))
                            ve.wait_ge(s_v4w.sem, V0w + 5 * tt + 4)
                            s_v4w.inc(ve.tensor_add(comb[:, tt * D:(tt + 1) * D],
                                                    comb[:, tt * D:(tt + 1) * D],
                                                    att5[:, (tt % 2) * D:(tt % 2 + 1) * D]))
                        assert s_v4w.n == V0w + 5 * NTTH

                with ExitStack() as stps, nc.Block() as blk:
                    h1p = [stps.enter_context(nc.psum_tensor(f"h1p{half}{i}", [128, TC], F32)) for i in range(2)]
                    yp = [stps.enter_context(nc.psum_tensor(f"yp{half}{i}", [128, D], F32)) for i in range(2)]
                    T0, S0, V0 = s_t4.n, s_s4.n, s_v4.n
                    B0 = T0 // TQ4
                    v4w_done = s_v4w.n

                    @blk.gpsimd
                    def _(gp):
                        for e in range(9):
                            for fc in range(NFC):
                                b = B0 + e * NFC + fc
                                wsrc = wut_h[e, :, :] if e < NE else swut_h
                                dsrc = wdt_h[e, :, :] if e < NE else swdt_h
                                if b >= 1:
                                    gp.wait_ge(s_t4.sem, TQ4 * (b - 1) + H1LAST)
                                dmarec[("wu", b)] = dma_ser(gp, s_gw, wu_sb.ap().rearrange("p (kc f) -> p kc f", kc=NKC),
                                    wsrc[:, fc * FCB:(fc + 1) * FCB].rearrange(
                                        "(kc p) f -> p kc f", p=128))
                                if b >= 1:
                                    gp.wait_ge(s_t4.sem, TQ4 * b)
                                dmarec[("wd", b)] = dma_ser(gp, s_gw, wd_sb.ap().rearrange("p (fi d) -> p fi d", fi=NFT),
                                    dsrc[fc * FCB:(fc + 1) * FCB, :].rearrange(
                                        "(fi p) d -> p fi d", p=128))

                    @blk.sync
                    def _(sy):
                        for e in range(9):
                            b = B0 + e * NFC
                            if e >= 2:
                                sy.wait_ge(s_s4.sem, SQ4 * (b - NFC))
                            src = bu2_h[e, :, :] if e < NE else sbu2_h
                            dmarec[("bu", half, e)] = dma_ser(sy, s_qx, bu_sb[:, (e % 2) * 32:(e % 2) * 32 + 32], src)

                    @blk.tensor
                    def _(te):
                        te.wait_ge(s_gsp.sem, dmarec[("xnth", half)])
                        for e in range(9):
                            for fc in range(NFC):
                                bl = e * NFC + fc
                                b = B0 + bl
                                te.wait_ge(s_gw.sem, dmarec[("wu", b)])
                                for tc in range(NTC):
                                    for ft in range(NFT):
                                        hh = SQ4 * b + tc * NFT + ft
                                        if hh >= 2:
                                            te.wait_ge(s_s4.sem, hh - 1)
                                        last = None
                                        for kc in range(NKC):
                                            for no, nw in nsplit(TC):
                                                last = te.matmul(
                                                    h1p[(tc * NFT + ft) % 2][:, no:no + nw],
                                                    wu_sb[:, kc * FCB + ft * 128:kc * FCB + (ft + 1) * 128],
                                                    xnTh[:, kc * TH + tc * TC + no:kc * TH + tc * TC + no + nw],
                                                    start=(kc == 0), stop=(kc == NKC - 1))
                                        s_t4.inc(last)
                                    te.wait_ge(s_gw.sem, dmarec[("wd", b)])
                                    for tti in range(NTI):
                                        tt = tc * NTI + tti
                                        yy = VQ4 * b + tt
                                        if yy >= 2:
                                            te.wait_ge(s_v4.sem, yy - 1)
                                        te.wait_ge(s_s4.sem, SQ4 * b + NFT * (tc + 1))
                                        last = None
                                        for kc in range(NFT):
                                            for no, nw in nsplit(D):
                                                last = te.matmul(
                                                    yp[tt % 2][:, no:no + nw],
                                                    a1s[:, kc * TC + tti * 128:kc * TC + (tti + 1) * 128],
                                                    wd_sb[:, kc * D + no:kc * D + no + nw],
                                                    start=(kc == 0), stop=(kc == NFT - 1))
                                        s_t4.inc(last)
                        assert s_t4.n == T0 + TQ4 * 9 * NFC

                    @blk.scalar
                    def _(sc):
                        for e in range(9):
                            for fc in range(NFC):
                                bl = e * NFC + fc
                                b = B0 + bl
                                if fc == 0:
                                    sc.wait_ge(s_qx.sem, dmarec[("bu", half, e)])
                                for tc in range(NTC):
                                    for ft in range(NFT):
                                        sc.wait_ge(s_t4.sem, T0 + t4_h1_val(bl, tc, ft))
                                        s_s4.inc(sc.activation(
                                            a1s[:, ft * TC:(ft + 1) * TC],
                                            h1p[(tc * NFT + ft) % 2][:, :], AF.Gelu,
                                            bias=bu_sb[:, (e % 2) * 32 + fc * NFT + ft:
                                                       (e % 2) * 32 + fc * NFT + ft + 1]))
                        assert s_s4.n == S0 + SQ4 * 9 * NFC

                    @blk.vector
                    def _(ve):
                        ve.wait_ge(s_v4w.sem, v4w_done)
                        for e in range(9):
                            for fc in range(NFC):
                                bl = e * NFC + fc
                                b = B0 + bl
                                for tc in range(NTC):
                                    for tti in range(NTI):
                                        tt = tc * NTI + tti
                                        ve.wait_ge(s_t4.sem, T0 + t4_y_val(bl, tc, tti))
                                        gt = half * NTTH + tt
                                        scalar = (wgt[:, gt * NE + e:gt * NE + e + 1]
                                                  if e < NE else 0.1)
                                        yy = VQ4 * bl + tt
                                        if yy >= NTTH:
                                            ve.wait_ge(s_v4.sem, V0 + yy - NTTH + 1)
                                        s_v4.inc(ve.scalar_tensor_tensor(
                                            comb[:, tt * D:(tt + 1) * D],
                                            yp[tt % 2][:, :], scalar,
                                            comb[:, tt * D:(tt + 1) * D],
                                            op0=ALU.mult, op1=ALU.add))
                        assert s_v4.n == V0 + VQ4 * 9 * NFC

                if half == 0:
                    halfrec["v4done0"] = s_v4.n
                else:
                    with nc.Block() as blk:
                        v4done = s_v4.n

                        @blk.sync
                        def _(sy):
                            for tt in range(0, NTTH, 2):
                                r0 = (half * NTTH + tt) * 128
                                sy.wait_ge(s_v4.sem, v4done - VQ4 + tt + 1)
                                dmarec[("out", half, tt)] = dma_ser(sy, s_qx, out_h[r0:r0 + 128, :],
                                    comb[:, tt * D:(tt + 1) * D])
                            sy.wait_ge(s_qx.sem, dmarec[("out", half, NTTH - 2)])
                            sy.wait_ge(s_gsp.sem, dmarec[("out", 0, NTTH - 1)])

                        @blk.gpsimd
                        def _(gp):
                            for tt in range(1, NTTH, 2):
                                r0 = (half * NTTH + tt) * 128
                                gp.wait_ge(s_v4.sem, v4done - VQ4 + tt + 1)
                                dmarec[("out", half, tt)] = dma_ser(gp, s_gsp, out_h[r0:r0 + 128, :],
                                    comb[:, tt * D:(tt + 1) * D])
                            gp.wait_ge(s_gsp.sem, dmarec[("out", half, NTTH - 1)])

    return nc


def _prep_inputs(L0_attn_x, L0_attn_Wq, L0_attn_Wk, L0_attn_Wv, L0_attn_Wo,
                 L0_attn_W_down_kv, L0_attn_W_up_k, L0_attn_W_up_v,
                 L0_ffn_gate_w, L0_ffn_gate_bias, L0_ffn_Wu, L0_ffn_bu,
                 L0_ffn_Wd, L0_ffn_bd, L0_ffn_sWu, L0_ffn_sbu, L0_ffn_sWd,
                 L0_ffn_sbd, S):
    c = np.ascontiguousarray
    f = lambda a: np.asarray(a, dtype=np.float32)
    shared = {
        "wq": c(f(L0_attn_Wq)),
        "wk": c(f(L0_attn_Wk)),
        "wv": c(f(L0_attn_Wv)),
        "wo": c(f(L0_attn_Wo)),
        "wdn": c(f(L0_attn_W_down_kv)),
        "wupk": c(f(L0_attn_W_up_k)),
        "wupv": c(f(L0_attn_W_up_v)),
        "gwt": c(f(L0_ffn_gate_w).T),
        "gbias": c(f(L0_ffn_gate_bias).reshape(1, NE)),
        "wut": c(f(L0_ffn_Wu).transpose(0, 2, 1)),
        "bu2": c(f(L0_ffn_bu).reshape(NE, FF // 128, 128).transpose(0, 2, 1)),
        "wdt": c(f(L0_ffn_Wd).transpose(0, 2, 1)),
        "bd": c(f(L0_ffn_bd)),
        "swut": c(f(L0_ffn_sWu).T),
        "sbu2": c(f(L0_ffn_sbu).reshape(FF // 128, 128).T),
        "swdt": c(f(L0_ffn_sWd).T),
        "sbd": c(f(L0_ffn_sbd).reshape(1, D)),
    }
    x = f(L0_attn_x)
    in_maps = []
    for b in range(x.shape[0]):
        m = dict(shared)
        m["x"] = c(x[b, :S, :])
        in_maps.append(m)
    return in_maps


def kernel(**inputs):
    S = 4096
    in_maps = _prep_inputs(S=S, **inputs)
    nc = build(S=S)
    res = run_bass_kernel_spmd(nc, in_maps, core_ids=list(range(len(in_maps))))
    return np.stack([r["out"] for r in res.results], axis=0)
